# version 03
# baseline (speedup 1.0000x reference)
import math
import numpy as np
import ml_dtypes
from contextlib import ExitStack

import concourse.bass as bass
import concourse.mybir as mybir
from concourse.bass_utils import run_bass_kernel_spmd

F32 = mybir.dt.float32
BF16 = mybir.dt.bfloat16
AF = mybir.ActivationFunctionType
ALU = mybir.AluOpType
AX = mybir.AxisListType

T = 2048
D = 1024
NTT = 16
NCH = 8
NH = 16
MAGIC = 12582912.0
EPS = 1e-5
CSLACK = 8.0


def host_consts():
    bf = ml_dtypes.bfloat16
    c = {}
    c["C_IDB"] = np.eye(128, dtype=bf)
    c["C_IDF"] = np.eye(128, dtype=np.float32)
    g = np.zeros((128, NTT * 64), dtype=np.float32)
    for tt in range(NTT):
        for n in range(128):
            g[n, tt * 64 + 4 * tt + n // 32] = 1.0 / 32.0
    c["C_G"] = g.astype(bf)
    freqs = 2.0 ** np.linspace(0.0, math.log2(5.0), 16).astype(np.float32)
    pos = 1.0 + np.arange(T, dtype=np.float32)
    rad = pos[None, :] * freqs[:, None] * math.pi
    sinr, cosr = np.sin(rad).astype(np.float32), np.cos(rad).astype(np.float32)
    t2 = np.ones((128, T), dtype=np.float32)
    for d in range(32):
        alt = -1.0 if d % 2 == 0 else 1.0
        t2[d] = sinr[d % 16] + alt * cosr[d % 16]
        t2[64 + d] = t2[d]
    c["C_T2"] = t2
    e = np.zeros((128, 1024), dtype=np.float32)
    for pp in range(128):
        e[pp, pp % 64::64] = 1.0
    c["C_EYE"] = e
    c["C_ONEF"] = np.ones((128, 128), dtype=np.float32)
    c["C_ONEB"] = np.ones((128, 128), dtype=bf)
    return c


class Prog:
    def __init__(self):
        self.nc = bass.Bass("TRN2", target_bir_lowering=False, debug=False)
        self.es = ExitStack()
        self.sems = {}
        self.cnt = {}
        self.steps = []
        self._names = {}

    def mksem(self, *names):
        for n in names:
            self.sems[n] = self.es.enter_context(self.nc.semaphore(n))
            self.cnt[n] = 0

    def _uniq(self, name):
        n = self._names.get(name, 0)
        self._names[name] = n + 1
        return name if n == 0 else "%s_%d" % (name, n)

    def sb(self, name, shape, dt, stack=None):
        return (stack or self.es).enter_context(
            self.nc.sbuf_tensor(self._uniq(name), shape, dt))

    def ps(self, name, shape, dt, stack=None):
        return (stack or self.es).enter_context(
            self.nc.psum_tensor(self._uniq(name), shape, dt))

    def bump(self, name, by=1):
        self.cnt[name] += by
        return self.cnt[name]

    def emit(self, eng_name, waits, fn, inc=None, incby=1):
        post = self.bump(inc, incby) if inc else None
        semtab = self.sems

        def run(eng, waits=tuple(waits), fn=fn, inc=inc, incby=incby):
            for n, v in waits:
                eng.wait_ge(semtab[n], v)
            last = fn(eng)
            if inc is not None:
                assert last is not None, "step with inc returned no instruction"
                last.then_inc(semtab[inc], incby)
        self.steps.append((eng_name, run))
        return post

    def dma(self, eng_name, waits, out, in_, sem="dw"):
        pre = self.cnt[sem]
        post = self.bump(sem, 16)
        semtab = self.sems
        waits = list(waits)
        if pre > 0:
            waits.append((sem, pre))

        def run(eng, waits=tuple(waits)):
            for n, v in waits:
                eng.wait_ge(semtab[n], v)
            eng.dma_start(out=out, in_=in_).then_inc(semtab[sem], 16)
        self.steps.append((eng_name, run))
        return post

    def flush(self, name=None):
        by_eng = {}
        for e, fn in self.steps:
            by_eng.setdefault(e, []).append(fn)
        self.steps = []
        if not by_eng:
            return
        import os as _os
        with self.nc.Block(None if _os.environ.get("NOBLOCKNAME") else name) as block:
            for e, fns in by_eng.items():
                def body(eng, fns=fns):
                    for fn in fns:
                        fn(eng)
                getattr(block, e)(body)


class _Cut(Exception):
    pass


def build(nphases=99, debug=(), pccut=None, pacut=None):
    p = Prog()
    nc = p.nc
    dbg_list = []

    def pchk(tag):
        if pacut == tag:
            raise _Cut()

    def din(name, shape, dt=F32):
        return nc.dram_tensor(name, shape, dt, kind="ExternalInput").ap()

    x_in = din("x", [T, D])
    wqkv = din("wqkv", [D, 3 * D])
    wproj = din("wproj", [D, D])
    fw = din("fw", [D, D])
    fb = din("fb", [1, D])
    bln1g = din("bln1g", [2, D]); bln1b = din("bln1b", [2, D])
    bln2g = din("bln2g", [2, D]); bln2b = din("bln2b", [2, D])
    bwq = din("bwq", [2, D, D]); bwk = din("bwk", [2, D, D])
    bwv = din("bwv", [2, D, D]); bwo = din("bwo", [2, D, D])
    bbq = din("bbq", [2, D]); bbk = din("bbk", [2, D]); bbv = din("bbv", [2, D])
    bw1 = din("bw1", [2, D, 4 * D]); bb1 = din("bb1", [2, 4 * D])
    bw2 = din("bw2", [2, 4 * D, D])
    cln1g = din("cln1g", [2, 128, 8]); cln1b = din("cln1b", [2, 128, 8])
    cln2g = din("cln2g", [2, 128, 8]); cln2b = din("cln2b", [2, 128, 8])
    cbq = din("cbq", [2, 128, 8]); cbk = din("cbk", [2, 128, 8])
    cb1 = din("cb1", [2, 128, 32])
    c_idb = din("C_IDB", [128, 128], BF16)
    c_idf = din("C_IDF", [128, 128], F32)
    c_g = din("C_G", [128, NTT * 64], BF16)
    c_t2 = din("C_T2", [128, T], F32)
    c_eye = din("C_EYE", [128, 1024], F32)
    c_onef = din("C_ONEF", [128, 128], F32)
    c_oneb = din("C_ONEB", [128, 128], BF16)

    y_out = nc.dram_tensor("y", [T, D], F32, kind="ExternalOutput").ap()
    import os as _os
    if _os.environ.get("NOSPILL"):
        spill = nc.dram_tensor("spill", [T, D], F32, kind="ExternalInput").ap()
    else:
        spill = nc.dram_tensor("spill", [T, D], F32).ap()

    X = p.sb("X", [128, NTT * D], F32)
    XNFM = p.sb("XNFM", [128, NCH * T], BF16)
    BIG = p.sb("BIG", [128, 16384], BF16)
    WBUF = p.sb("WBUF", [128, 8192], BF16)
    IDB = p.sb("IDB", [128, 128], BF16)
    IDF = p.sb("IDF", [128, 128], F32)
    CG = p.sb("CG", [128, NTT * 64], BF16)
    CEYE = p.sb("CEYE", [128, 1024], F32)
    ONEF = p.sb("ONEF", [128, 128], F32)
    ONEB = p.sb("ONEB", [128, 128], BF16)
    SM = p.sb("SMALL", [128, 256], F32)
    CZCOL = p.sb("CZCOL", [128, 2], F32)
    XTMPF = p.sb("XTMPF", [128, 2048], F32)
    SOFT2 = p.sb("SOFT2", [128, 1024], F32)

    SM_SUM, SM_SQ, SM_MM, SM_VAR, SM_INV, SM_NMI = 0, 16, 32, 48, 64, 80
    SM_A, SM_AM, SM_RA, SM_R127, SM_XS, SM_MISC = 96, 128, 144, 160, 176, 192

    p.mksem("dx", "dw", "dwa", "dwb", "dwc", "dh0", "dh1", "dc", "dq", "ts", "vs", "ss", "gs")

    v_x = p.dma("sync", [], X[:, :].rearrange("p (t d) -> p t d", t=NTT),
                x_in.rearrange("(t p) d -> p t d", p=128), "dx")
    import os as _os2
    for dst, src in [(IDB, c_idb), (IDF, c_idf), (CG, c_g),
                     (CEYE, c_eye), (ONEF, c_onef), (ONEB, c_oneb)]:
        if not _os2.environ.get("NOCONST"):
            p.dma("sync", [], dst[:, :], src, "dc")
    v_consts = p.cnt["dc"]
    if not _os2.environ.get("NOMEM"):
        p.emit("vector", [], lambda v: (
            v.memset(CZCOL[:, 0:1], 0.0),
            v.memset(CZCOL[:, 1:2], EPS * 1024.0 * 1024.0))[-1], inc="vs")
    v_czcol = p.cnt["vs"]
    p.flush("p0")
    try:
        pchk("p0")

        def ln_stats(have_sums):
            for tt in range(NTT):
                if not have_sums:
                    w = [("dx", v_x)] if tt == 0 else []
                    if p.cnt["ss"] >= 1:
                        w.append(("ss", p.cnt["ss"]))
                    p.emit("scalar", w, lambda s, tt=tt: s.activation(
                        out=XTMPF[:, 0:1024], in_=X[:, tt * D:(tt + 1) * D], func=AF.Copy,
                        accum_out=SM[:, SM_SUM + tt:SM_SUM + tt + 1]), inc="ss")
                w = [("dx", v_x)] if (tt == 0 and have_sums) else []
                if p.cnt["ss"] >= 1:
                    w.append(("ss", p.cnt["ss"]))
                p.emit("scalar", w, lambda s, tt=tt: s.activation(
                    out=XTMPF[:, 1024:2048], in_=X[:, tt * D:(tt + 1) * D], func=AF.Square,
                    accum_out=SM[:, SM_SQ + tt:SM_SQ + tt + 1]), inc="ss")
            vss = p.cnt["ss"]

            def m1(v):
                v.tensor_tensor(out=SM[:, SM_MM:SM_MM + 16], in0=SM[:, SM_SUM:SM_SUM + 16],
                                in1=SM[:, SM_SUM:SM_SUM + 16], op=ALU.mult)
                v.drain()
                return v.scalar_tensor_tensor(out=SM[:, SM_VAR:SM_VAR + 16],
                                              in0=SM[:, SM_SQ:SM_SQ + 16], scalar=1024.0,
                                              in1=SM[:, SM_MM:SM_MM + 16],
                                              op0=ALU.mult, op1=ALU.subtract)
            vv = p.emit("vector", [("ss", vss)], m1, inc="vs")
            sv = p.emit("scalar", [("vs", vv)], lambda s: s.activation(
                out=SM[:, SM_MM:SM_MM + 16], in_=SM[:, SM_VAR:SM_VAR + 16], func=AF.Sqrt,
                bias=CZCOL[:, 1:2]), inc="ss")

            def m2(v):
                v.reciprocal(out=SM[:, SM_VAR:SM_VAR + 16], in_=SM[:, SM_MM:SM_MM + 16])
                v.drain()
                v.tensor_scalar(out=SM[:, SM_INV:SM_INV + 16], in0=SM[:, SM_VAR:SM_VAR + 16],
                                scalar1=1024.0, scalar2=None, op0=ALU.mult, op1=ALU.bypass)
                v.drain()
                return v.scalar_tensor_tensor(out=SM[:, SM_NMI:SM_NMI + 16],
                                              in0=SM[:, SM_SUM:SM_SUM + 16], scalar=-1.0 / 1024.0,
                                              in1=SM[:, SM_INV:SM_INV + 16],
                                              op0=ALU.mult, op1=ALU.mult)
            return p.emit("vector", [("ss", sv)], m2, inc="vs")

        stA = ExitStack()
        stA2 = ExitStack()
        QLB = p.sb("QLB", [128, 512], BF16, stA)
        KLB = p.sb("KLB", [128, 512], BF16, stA)
        XNLF = p.sb("XNLF", [128, 512], BF16, stA)
        K3VB = p.sb("K3VB", [128, 1040], BF16, stA)
        Z2A = p.sb("Z2A", [128, 520], BF16, stA)
        QBUF = p.sb("QBUF", [128, T], BF16, stA2)
        KBUF = p.sb("KBUF", [128, T], BF16, stA2)
        VBUF = p.sb("VBUF", [128, 2 * 1040], BF16, stA2)
        XTMPB = p.sb("XTMPB", [128, 2048], BF16, stA2)
        AE3 = p.sb("AE3", [128, 2048], BF16, stA2)
        XNL = p.sb("XNL", [128, 1024], BF16, stA2)
        NEGC = p.sb("NEGC", [128, 8], F32, stA2)

        pa1 = ExitStack()
        psT = p.ps("psT", [128, 1024], BF16, pa1)
        psT2 = p.ps("psT2", [128, 1024], BF16, pa1)
        psLM = p.ps("psLM", [64, 1024], F32, pa1)
        v_inv = ln_stats(have_sums=False)
        t_tp, v_ev = {}, {}
        for tt in range(NTT):
            slot = tt % 2
            w = [("vs", v_inv)]
            if tt >= 2:
                w.append(("ts", t_tp[tt - 2]))
            va = p.emit("vector", w, lambda v, tt=tt, slot=slot: v.tensor_scalar(
                out=XTMPB[:, slot * 1024:(slot + 1) * 1024], in0=X[:, tt * D:(tt + 1) * D],
                scalar1=SM[:, SM_INV + tt:SM_INV + tt + 1],
                scalar2=SM[:, SM_NMI + tt:SM_NMI + tt + 1],
                op0=ALU.mult, op1=ALU.add), inc="vs")

            def tfn(t, tt=tt, slot=slot):
                xn = XTMPB[:, slot * 1024:(slot + 1) * 1024]
                pst = psT if slot == 0 else psT2
                for c in range(NCH):
                    t.transpose(pst[:, c * 128:(c + 1) * 128], xn[:, c * 128:(c + 1) * 128], IDB[:, :])
                t.matmul(psLM[:, 0:512], CG[:, tt * 64:(tt + 1) * 64], xn[:, 0:512],
                         start=(tt == 0), stop=(tt == NTT - 1))
                return t.matmul(psLM[:, 512:1024], CG[:, tt * 64:(tt + 1) * 64], xn[:, 512:1024],
                                start=(tt == 0), stop=(tt == NTT - 1))
            w = [("vs", va), ("dc", v_consts)]
            if tt >= 2:
                w.append(("vs", v_ev[tt - 2]))
            t_tp[tt] = p.emit("tensor", w, tfn, inc="ts")
            v_ev[tt] = p.emit("vector", [("ts", t_tp[tt])], lambda v, slot=slot, tt=tt: v.tensor_copy(
                out=XNFM[:, :].rearrange("p (c t) -> p c t", c=NCH)[:, :, tt * 128:(tt + 1) * 128],
                in_=(psT if slot == 0 else psT2)[:, :].rearrange("p (c n) -> p c n", c=NCH)),
                inc="vs")
        s_xnl = p.emit("scalar", [("ts", t_tp[NTT - 1])], lambda s: s.activation(
            out=XNL[0:64, :], in_=psLM[:, :], func=AF.Copy), inc="ss")

        def t_xnl(t):
            last = None
            for c in range(NCH):
                last = t.transpose(psT[:, c * 64:(c + 1) * 64], XNL[0:64, c * 128:(c + 1) * 128],
                                   IDB[0:64, 0:64])
            return last
        tx = p.emit("tensor", [("ss", s_xnl), ("vs", v_ev[NTT - 1]), ("vs", v_ev[NTT - 2])],
                    t_xnl, inc="ts")
        v_xnlf = p.emit("vector", [("ts", tx)], lambda v: v.tensor_copy(
            out=XNLF[:, :], in_=psT[:, 0:512]), inc="vs")
        pa1.close()
        p.flush("pa1")
        pchk("pa1")
        if "pa1" in debug:
            dbg_list += [("pa1_xnfm", XNFM), ("pa1_xnlf", XNLF)]

        pa2 = ExitStack()
        ps0 = p.ps("ps0", [128, 512], F32, pa2)
        ps1 = p.ps("ps1", [128, 512], F32, pa2)
        ps2 = p.ps("ps2", [128, 512], F32, pa2)
        ps3 = p.ps("ps3", [128, 512], F32, pa2)
        psx = p.ps("psx", [128, 512], F32, pa2)
        psk = p.ps("psk", [128, 512], F32, pa2)
        psb0 = p.ps("psb0", [128, 1024], BF16, pa2)
        psb1 = p.ps("psb1", [128, 1024], BF16, pa2)

        def dma_qkv(hg, slot, waits):
            res = []
            for vi in range(3):
                sem = ("dwa", "dwb", "dwc")[vi]
                res.append((sem, p.dma("gpsimd", waits if vi == 0 else [],
                            WBUF[:, slot * 4096 + vi * 1024: slot * 4096 + (vi + 1) * 1024]
                            .rearrange("p (c n) -> p c n", c=NCH),
                            wqkv.rearrange("(c p) n -> p c n", p=128)
                            [:, :, vi * D + hg * 128: vi * D + (hg + 1) * 128], sem)))
            return res

        v_w = {0: dma_qkv(0, 0, [])}
        prev_tb = None
        v_vbuf_ones = None
        v_k3n_all = []
        hg_fence = []
        for hg in range(NCH):
            slot = hg % 2
            WQ = WBUF[:, slot * 4096: slot * 4096 + 1024]
            WK = WBUF[:, slot * 4096 + 1024: slot * 4096 + 2048]
            WV = WBUF[:, slot * 4096 + 2048: slot * 4096 + 3072]

            qk_ev = {}
            for tg in range(4):
                pq = ps0 if tg % 2 == 0 else ps1
                pk = ps2 if tg % 2 == 0 else ps3
                w = []
                if tg == 0:
                    w += list(v_w[hg]) + [("vs", v_xnlf)]
                    if hg_fence:
                        w += [("vs", hg_fence[-1][0]), ("ss", hg_fence[-1][1])]
                if tg >= 2:
                    w.append(("ss", qk_ev[tg - 2]))

                def mmqk(t, tg=tg, pq=pq, pk=pk, WQ=WQ, WK=WK):
                    for c in range(NCH):
                        t.matmul(pq[:, :], WQ[:, c * 128:(c + 1) * 128],
                                 XNFM[:, c * T + tg * 512: c * T + (tg + 1) * 512],
                                 start=(c == 0), stop=(c == NCH - 1))
                    last = None
                    for c in range(NCH):
                        last = t.matmul(pk[:, :], WK[:, c * 128:(c + 1) * 128],
                                        XNFM[:, c * T + tg * 512: c * T + (tg + 1) * 512],
                                        start=(c == 0), stop=(c == NCH - 1))
                    return last
                tq = p.emit("tensor", w, mmqk, inc="ts")
                qk_ev[tg] = p.emit("scalar", [("ts", tq)], lambda s, tg=tg, pq=pq, pk=pk: (
                    s.activation(out=QBUF[:, tg * 512:(tg + 1) * 512], in_=pq[:, :],
                                 func=AF.Copy, scale=0.125),
                    s.activation(out=KBUF[:, tg * 512:(tg + 1) * 512], in_=pk[:, :],
                                 func=AF.Copy))[-1], inc="ss")

            def mmql(t, WQ=WQ, WK=WK):
                for c in range(NCH):
                    t.matmul(psx[:, 0:64], WQ[:, c * 128:(c + 1) * 128], XNLF[:, c * 64:(c + 1) * 64],
                             start=(c == 0), stop=(c == NCH - 1))
                last = None
                for c in range(NCH):
                    last = t.matmul(psx[:, 64:128], WK[:, c * 128:(c + 1) * 128],
                                    XNLF[:, c * 64:(c + 1) * 64],
                                    start=(c == 0), stop=(c == NCH - 1))
                return last
            wql = list(v_w[hg]) + [("vs", v_xnlf)]
            if hg_fence:
                wql.append(("vs", hg_fence[-1][0]))
            tql = p.emit("tensor", wql, mmql, inc="ts")
            v_ql = p.emit("vector", [("ts", tql)], lambda v, hg=hg: (
                v.tensor_scalar(out=QLB[:, hg * 64:(hg + 1) * 64], in0=psx[:, 0:64],
                                scalar1=0.125, scalar2=None, op0=ALU.mult, op1=ALU.bypass),
                v.tensor_copy(out=KLB[:, hg * 64:(hg + 1) * 64], in_=psx[:, 64:128]))[-1], inc="vs")

            v_vev = {}
            for g in range(4):
                pv = psk if g % 2 == 0 else psx
                w = []
                if g == 0:
                    w += list(v_w[hg])
                    if hg_fence:
                        w.append(("vs", hg_fence[-1][0]))
                if g == 1:
                    w.append(("vs", v_ql))
                if g >= 2:
                    w.append(("vs", v_vev[g - 2]))

                def mmv(t, g=g, pv=pv, WV=WV):
                    last = None
                    for i in range(4):
                        tt = g * 4 + i
                        for c in range(NCH):
                            last = t.matmul(pv[:, i * 128:(i + 1) * 128],
                                            XNFM[:, c * T + tt * 128: c * T + (tt + 1) * 128],
                                            WV[:, c * 128:(c + 1) * 128],
                                            start=(c == 0), stop=(c == NCH - 1))
                    return last
                tv = p.emit("tensor", w, mmv, inc="ts")

                def vev(v, g=g, pv=pv):
                    v.tensor_copy(
                        out=VBUF[:, 0:1040].rearrange("p (t y) -> p t y", t=NTT)
                        [:, g * 4:(g + 1) * 4, 0:64],
                        in_=pv[:, :].rearrange("p (t n) -> p t n", t=4)[:, :, 0:64])
                    return v.tensor_copy(
                        out=VBUF[:, 1040:2080].rearrange("p (t y) -> p t y", t=NTT)
                        [:, g * 4:(g + 1) * 4, 0:64],
                        in_=pv[:, :].rearrange("p (t n) -> p t n", t=4)[:, :, 64:128])
                v_vev[g] = p.emit("vector", [("ts", tv)], vev, inc="vs")
            if hg == 0:
                v_vbuf_ones = p.emit("vector", [], lambda v: (
                    v.memset(VBUF[:, 64:1040:65], 1.0),
                    v.memset(VBUF[:, 1104:2080:65], 1.0))[-1], inc="vs")

            if hg + 1 < NCH:
                v_w[hg + 1] = dma_qkv(hg + 1, (hg + 1) % 2,
                                      [("ss", qk_ev[3]), ("vs", v_vev[3]), ("vs", v_ql)])
            if hg == 0:
                pchk("pa2a")

            base_w = [("ss", qk_ev[3]), ("vs", v_ql), ("vs", v_vev[2]), ("vs", v_vev[3])]
            tc = p.emit("tensor", base_w, lambda t, hg=hg: (
                t.matmul(ps0[:, 0:64], QBUF[0:64, 0:T:16],
                         KLB[0:64, hg * 64:(hg + 1) * 64], start=True, stop=True),
                t.matmul(ps1[:, 0:64], QBUF[64:128, 0:T:16],
                         KLB[64:128, hg * 64:(hg + 1) * 64], start=True, stop=True))[-1], inc="ts")
            vc = p.emit("vector", [("ts", tc)], lambda v: (
                v.reduce_max(out=SM[:, SM_MISC:SM_MISC + 1], in_=ps0[:, 0:64], axis=AX.X),
                v.reduce_max(out=SM[:, SM_MISC + 1:SM_MISC + 2], in_=ps1[:, 0:64], axis=AX.X),
                v.drain(),
                v.tensor_tensor(out=SM[:, SM_MISC:SM_MISC + 1], in0=SM[:, SM_MISC:SM_MISC + 1],
                                in1=SM[:, SM_MISC + 1:SM_MISC + 2], op=ALU.max))[-1], inc="vs")
            tc2 = p.emit("tensor", [("vs", vc)], lambda t: t.transpose(
                ps1[0:1, 0:128], SM[:, SM_MISC:SM_MISC + 1], IDF[:, :]), inc="ts")
            vc2 = p.emit("vector", [("ts", tc2)], lambda v: v.reduce_max(
                out=SM[0:1, SM_MISC + 1:SM_MISC + 2], in_=ps1[0:1, 0:128], axis=AX.X), inc="vs")
            tc3 = p.emit("tensor", [("vs", vc2)], lambda t: t.matmul(
                ps1[:, 128:129], ONEF[0:1, 0:128], SM[0:1, SM_MISC + 1:SM_MISC + 2],
                start=True, stop=True), inc="ts")
            s_c1 = p.emit("scalar", [("ts", tc3)], lambda s: s.activation(
                out=NEGC[:, 0:1], in_=ps1[:, 128:129], func=AF.Copy, scale=-1.0, bias=-CSLACK),
                inc="ss")

            if hg == 0:
                pchk("pa2b")
            e1_ev = {}
            for qg in range(4):
                pse = ps2 if qg % 2 == 0 else ps3
                pso = psx if qg % 2 == 0 else psk
                w = [("ss", s_c1), ("vs", v_vev[2]), ("vs", v_vev[3])] if qg == 0 else []
                if qg >= 2:
                    w.append(("ss", e1_ev[qg - 2]))

                def s1mm(t, qg=qg, pse=pse, pso=pso, hg=hg):
                    last = None
                    for i in range(4):
                        qt = qg * 4 + i
                        t.matmul(pse[:, i * 64:(i + 1) * 64],
                                 QBUF[0:64, qt * 128:(qt + 1) * 128],
                                 KLB[0:64, hg * 64:(hg + 1) * 64], start=True, stop=True)
                        last = t.matmul(pso[:, i * 64:(i + 1) * 64],
                                        QBUF[64:128, qt * 128:(qt + 1) * 128],
                                        KLB[64:128, hg * 64:(hg + 1) * 64], start=True, stop=True)
                    return last
                ts1 = p.emit("tensor", w, s1mm, inc="ts")
                e1_ev[qg] = p.emit("scalar", [("ts", ts1)], lambda s, qg=qg, pse=pse, pso=pso: (
                    s.activation(out=XTMPB[:, qg * 512:(qg + 1) * 512]
                                 .rearrange("p (i n) -> p i n", i=4)[:, :, 0:64],
                                 in_=pse[:, 0:256].rearrange("p (i n) -> p i n", i=4),
                                 func=AF.Exp, bias=NEGC[:, 0:1]),
                    s.activation(out=XTMPB[:, qg * 512:(qg + 1) * 512]
                                 .rearrange("p (i n) -> p i n", i=4)[:, :, 64:128],
                                 in_=pso[:, 0:256].rearrange("p (i n) -> p i n", i=4),
                                 func=AF.Exp, bias=NEGC[:, 0:1]))[-1], inc="ss")
            tb_ev = {}
            for half in range(2):
                psb = psb0 if half == 0 else psb1
                w = [("ss", e1_ev[half * 2 + 1])]
                if prev_tb is not None:
                    w.append(("vs", prev_tb[half]))

                def e1tr(t, half=half, psb=psb):
                    last = None
                    for i in range(8):
                        qt = half * 8 + i
                        last = t.transpose(psb[:, i * 128:(i + 1) * 128],
                                           XTMPB[:, qt * 128:(qt + 1) * 128], IDB[:, :])
                    return last
                tt1 = p.emit("tensor", w, e1tr, inc="ts")
                tb_ev[half] = p.emit("vector", [("ts", tt1)],
                                     lambda v, half=half, psb=psb, hg=hg: v.tensor_copy(
                    out=BIG[:, hg * 2048 + half * 1024: hg * 2048 + (half + 1) * 1024],
                    in_=psb[:, :]), inc="vs")
            prev_tb = tb_ev

            if hg == 0:
                pchk("pa2c")
            tc = p.emit("tensor", [("ss", s_c1)], lambda t, hg=hg: (
                t.matmul(ps0[0:64, 128:256], QLB[0:64, hg * 64:(hg + 1) * 64],
                         KBUF[0:64, 0:T:16], start=True, stop=True),
                t.matmul(ps0[64:128, 128:256], QLB[64:128, hg * 64:(hg + 1) * 64],
                         KBUF[64:128, 0:T:16], start=True, stop=True))[-1], inc="ts")
            vc = p.emit("vector", [("ts", tc)], lambda v: v.reduce_max(
                out=SM[:, SM_MISC:SM_MISC + 1], in_=ps0[:, 128:256], axis=AX.X), inc="vs")
            tc2 = p.emit("tensor", [("vs", vc)], lambda t: t.transpose(
                ps1[0:1, 0:128], SM[:, SM_MISC:SM_MISC + 1], IDF[:, :]), inc="ts")
            vc2 = p.emit("vector", [("ts", tc2)], lambda v: v.reduce_max(
                out=SM[0:1, SM_MISC + 1:SM_MISC + 2], in_=ps1[0:1, 0:128], axis=AX.X), inc="vs")
            tc3 = p.emit("tensor", [("vs", vc2)], lambda t: t.matmul(
                ps1[:, 128:129], ONEF[0:1, 0:128], SM[0:1, SM_MISC + 1:SM_MISC + 2],
                start=True, stop=True), inc="ts")
            s_c3 = p.emit("scalar", [("ts", tc3)], lambda s: s.activation(
                out=NEGC[:, 1:2], in_=ps1[:, 128:129], func=AF.Copy, scale=-1.0, bias=-CSLACK),
                inc="ss")

            e3_ev = {}
            for ng in range(4):
                pse = ps2 if ng % 2 == 0 else ps3
                pso = psx if ng % 2 == 0 else psk
                w = [("ss", s_c3), ("ss", e1_ev[3])] if ng == 0 else []
                if ng >= 2:
                    w.append(("ss", e3_ev[ng - 2]))

                def s3mm(t, ng=ng, pse=pse, pso=pso, hg=hg):
                    last = None
                    for i in range(4):
                        nb = ng * 4 + i
                        t.matmul(pse[:, i * 64:(i + 1) * 64],
                                 KBUF[0:64, nb * 128:(nb + 1) * 128],
                                 QLB[0:64, hg * 64:(hg + 1) * 64], start=True, stop=True)
                        last = t.matmul(pso[:, i * 64:(i + 1) * 64],
                                        KBUF[64:128, nb * 128:(nb + 1) * 128],
                                        QLB[64:128, hg * 64:(hg + 1) * 64], start=True, stop=True)
                    return last
                ts3 = p.emit("tensor", w, s3mm, inc="ts")
                e3_ev[ng] = p.emit("scalar", [("ts", ts3)], lambda s, ng=ng, pse=pse, pso=pso: (
                    s.activation(out=AE3[:, ng * 512:(ng + 1) * 512]
                                 .rearrange("p (i n) -> p i n", i=4)[:, :, 0:64],
                                 in_=pse[:, 0:256].rearrange("p (i n) -> p i n", i=4),
                                 func=AF.Exp, bias=NEGC[:, 1:2]),
                    s.activation(out=AE3[:, ng * 512:(ng + 1) * 512]
                                 .rearrange("p (i n) -> p i n", i=4)[:, :, 64:128],
                                 in_=pso[:, 0:256].rearrange("p (i n) -> p i n", i=4),
                                 func=AF.Exp, bias=NEGC[:, 1:2]))[-1], inc="ss")

            if hg == 0:
                pchk("pa2d")
            def k3vmm(t, hg=hg):
                last = None
                for hh in range(2):
                    for nb in range(NTT):
                        last = t.matmul(psk[0:64, hh * 65:(hh + 1) * 65],
                                        AE3[:, nb * 128 + hh * 64: nb * 128 + hh * 64 + 64],
                                        VBUF[:, hh * 1040 + nb * 65: hh * 1040 + (nb + 1) * 65],
                                        start=(nb == 0), stop=(nb == NTT - 1))
                return last
            tk3 = p.emit("tensor", [("ss", e3_ev[3]), ("vs", v_vev[2]), ("vs", v_vev[3]),
                                    ("vs", v_vbuf_ones)], k3vmm, inc="ts")

            def k3norm(v, hg=hg):
                v.reciprocal(out=SM[0:64, SM_MISC + 4:SM_MISC + 5], in_=psk[0:64, 64:65])
                v.reciprocal(out=SM[0:64, SM_MISC + 5:SM_MISC + 6], in_=psk[0:64, 129:130])
                v.drain()
                v.tensor_scalar(out=K3VB[0:64, (2 * hg) * 65:(2 * hg + 1) * 65],
                                in0=psk[0:64, 0:65],
                                scalar1=SM[0:64, SM_MISC + 4:SM_MISC + 5], scalar2=None,
                                op0=ALU.mult, op1=ALU.bypass)
                return v.tensor_scalar(out=K3VB[0:64, (2 * hg + 1) * 65:(2 * hg + 2) * 65],
                                       in0=psk[0:64, 65:130],
                                       scalar1=SM[0:64, SM_MISC + 5:SM_MISC + 6], scalar2=None,
                                       op0=ALU.mult, op1=ALU.bypass)
            v_k3n_all.append(p.emit("vector", [("ts", tk3)], k3norm, inc="vs"))
            hg_fence.append((p.cnt["vs"], p.cnt["ss"]))
        v_k3vb = p.cnt["vs"]
        pa2.close()
        p.flush("pa2")
        pchk("pa2")
        if "pa2" in debug:
            dbg_list += [("pa2_qbuf", QBUF), ("pa2_kbuf", KBUF), ("pa2_vbuf", VBUF),
                         ("pa2_qlb", QLB), ("pa2_klb", KLB), ("pa2_k3vb", K3VB),
                         ("pa2_big", BIG), ("pa2_negc", NEGC)]

        stA2.close()
        pa3 = ExitStack()
        SLAB = p.sb("SLAB", [64, 8192], F32, pa3)
        sK2, sKT, sA, sAT, sT1, sT2, sVN, sVNT = [SLAB[0:64, i * 1024:(i + 1) * 1024]
                                                  for i in range(8)]
        VNTB = p.sb("VNTB", [128, 1024], BF16, pa3)
        ZTMPB = p.sb("ZTMPB", [128, 520], BF16, pa3)
        psI1 = p.ps("psI1", [64, 1024], F32, pa3)
        psI2 = p.ps("psI2", [64, 1024], F32, pa3)
        psI3 = p.ps("psI3", [64, 1024], F32, pa3)
        psI4 = p.ps("psI4", [64, 1024], F32, pa3)
        E2B = XTMPF[0:64, 0:1024]

        def mcol(h):
            return SM_MISC + (h % 2) * 8 + h // 2

        def k2mm(t):
            last = None
            for h in range(NH):
                r0 = (h % 2) * 64
                hg = h // 2
                pb = psI1 if h % 2 == 0 else psI2
                last = t.matmul(pb[:, hg * 64:(hg + 1) * 64],
                                QLB[r0:r0 + 64, hg * 64:(hg + 1) * 64],
                                KLB[r0:r0 + 64, hg * 64:(hg + 1) * 64], start=True, stop=True)
            return last
        tk2 = p.emit("tensor", [("vs", v_k3vb)], k2mm, inc="ts")
        v_mx = p.emit("vector", [("ts", tk2)], lambda v: (
            v.reduce_max(out=SM[0:64, SM_MISC:SM_MISC + 8],
                         in_=psI1[:, 0:512].rearrange("p (g n) -> p g n", g=8),
                         axis=AX.X, negate=True),
            v.reduce_max(out=SM[0:64, SM_MISC + 8:SM_MISC + 16],
                         in_=psI2[:, 0:512].rearrange("p (g n) -> p g n", g=8),
                         axis=AX.X, negate=True))[-1], inc="vs")
        s_e2 = None
        for h in range(NH):
            pb2 = psI1 if h % 2 == 0 else psI2
            s_e2 = p.emit("scalar", [("vs", v_mx)], lambda s, h=h, pb2=pb2: s.activation(
                out=E2B[:, h * 64:(h + 1) * 64],
                in_=pb2[:, (h // 2) * 64:(h // 2 + 1) * 64], func=AF.Exp,
                bias=SM[0:64, mcol(h):mcol(h) + 1]), inc="ss")
        pchk("pa3a2")
        v_k2n = p.emit("vector", [("ss", s_e2)], lambda v: (
            v.reduce_sum(out=SM[0:64, SM_MISC + 16:SM_MISC + 32],
                         in_=E2B[:, :].rearrange("p (g n) -> p g n", g=16), axis=AX.X),
            v.drain(),
            v.reciprocal(out=SM[0:64, SM_MISC + 32:SM_MISC + 48],
                         in_=SM[0:64, SM_MISC + 16:SM_MISC + 32]))[-1], inc="vs")
        v_k2f = None
        for h in range(NH):
            v_k2f = p.emit("vector", [("vs", v_k2n)], lambda v, h=h: v.tensor_scalar(
                out=sK2[:, h * 64:(h + 1) * 64], in0=E2B[:, h * 64:(h + 1) * 64],
                scalar1=SM[0:64, SM_MISC + 32 + h:SM_MISC + 33 + h], scalar2=None,
                op0=ALU.mult, op1=ALU.bypass), inc="vs")

        pchk("pa3a3")
        def k2tr(t):
            last = None
            for h in range(NH):
                last = t.transpose(psI2[:, h * 64:(h + 1) * 64], sK2[:, h * 64:(h + 1) * 64],
                                   IDF[0:64, 0:64])
            return last
        tkt = p.emit("tensor", [("vs", v_k2f)], k2tr, inc="ts")
        v_kt = p.emit("vector", [("ts", tkt)], lambda v: v.tensor_copy(
            out=sKT[:, :], in_=psI2[:, :]), inc="vs")

        pchk("pa3a")
        def csmm(t):
            last = None
            for h in range(NH):
                last = t.matmul(psI3[0:1, h * 64:(h + 1) * 64], ONEF[0:64, 0:1],
                                sK2[:, h * 64:(h + 1) * 64], start=True, stop=True)
            return last
        tcs = p.emit("tensor", [("vs", v_kt)], csmm, inc="ts")
        v_cs = p.emit("vector", [("ts", tcs)], lambda v: (
            v.reduce_max(out=SM[0:1, SM_MISC + 48:SM_MISC + 64],
                         in_=psI3[0:1, :].rearrange("p (g n) -> p g n", g=16), axis=AX.X),
            v.drain(),
            v.reciprocal(out=SM[0:1, SM_MISC:SM_MISC + 16],
                         in_=SM[0:1, SM_MISC + 48:SM_MISC + 64]))[-1], inc="vs")

        def rdmm(t):
            last = None
            for h in range(NH):
                last = t.matmul(psI4[:, h:h + 1], ONEF[0:1, 0:64],
                                SM[0:1, SM_MISC + h:SM_MISC + h + 1], start=True, stop=True)
            return last
        trd = p.emit("tensor", [("vs", v_cs)], rdmm, inc="ts")
        v_rd = p.emit("vector", [("ts", trd)], lambda v: v.tensor_copy(
            out=SM[0:64, SM_MISC + 48:SM_MISC + 64], in_=psI4[:, 0:16]), inc="vs")
        v_v0 = None
        for h in range(NH):
            v_v0 = p.emit("vector", [("vs", v_rd)], lambda v, h=h: (
                v.tensor_scalar(out=sVN[:, h * 64:(h + 1) * 64], in0=sKT[:, h * 64:(h + 1) * 64],
                                scalar1=SM[0:64, SM_MISC + 48 + h:SM_MISC + 49 + h], scalar2=None,
                                op0=ALU.mult, op1=ALU.bypass),
                v.tensor_scalar(out=sVNT[:, h * 64:(h + 1) * 64], in0=sK2[:, h * 64:(h + 1) * 64],
                                scalar1=SM[0:64, SM_MISC + 48 + h:SM_MISC + 49 + h], scalar2=None,
                                op0=ALU.mult, op1=ALU.bypass))[-1], inc="vs")

        pchk("pa3b")
        def allmm(t, pso, lhs, rhs):
            last = None
            for h in range(NH):
                last = t.matmul(pso[:, h * 64:(h + 1) * 64], lhs[:, h * 64:(h + 1) * 64],
                                rhs[:, h * 64:(h + 1) * 64], start=True, stop=True)
            return last
        v_prev = v_v0
        for it in range(6):
            t1 = p.emit("tensor", [("vs", v_prev)], lambda t: (
                allmm(t, psI1, sKT, sVN), allmm(t, psI2, sVN, sKT))[-1], inc="ts")
            v1 = p.emit("vector", [("ts", t1)], lambda v: (
                v.tensor_copy(out=sA[:, :], in_=psI1[:, :]),
                v.tensor_copy(out=sAT[:, :], in_=psI2[:, :]),
                v.scalar_tensor_tensor(out=sT1[:, :], in0=CEYE[0:64, :], scalar=7.0,
                                       in1=psI1[:, :], op0=ALU.mult, op1=ALU.subtract),
                v.scalar_tensor_tensor(out=sT2[:, :], in0=CEYE[0:64, :], scalar=7.0,
                                       in1=psI2[:, :], op0=ALU.mult, op1=ALU.subtract))[-1], inc="vs")
            t2 = p.emit("tensor", [("vs", v1)], lambda t: (
                allmm(t, psI3, sAT, sT1), allmm(t, psI4, sT1, sAT))[-1], inc="ts")
            v2 = p.emit("vector", [("ts", t2)], lambda v: (
                v.scalar_tensor_tensor(out=sT1[:, :], in0=CEYE[0:64, :], scalar=15.0,
                                       in1=psI3[:, :], op0=ALU.mult, op1=ALU.subtract),
                v.scalar_tensor_tensor(out=sT2[:, :], in0=CEYE[0:64, :], scalar=15.0,
                                       in1=psI4[:, :], op0=ALU.mult, op1=ALU.subtract))[-1], inc="vs")
            t3 = p.emit("tensor", [("vs", v2)], lambda t: (
                allmm(t, psI1, sAT, sT1), allmm(t, psI2, sT1, sAT))[-1], inc="ts")
            v3 = p.emit("vector", [("ts", t3)], lambda v: (
                v.scalar_tensor_tensor(out=sT1[:, :], in0=CEYE[0:64, :], scalar=13.0,
                                       in1=psI1[:, :], op0=ALU.mult, op1=ALU.subtract),
                v.scalar_tensor_tensor(out=sT2[:, :], in0=CEYE[0:64, :], scalar=13.0,
                                       in1=psI2[:, :], op0=ALU.mult, op1=ALU.subtract))[-1], inc="vs")
            t4 = p.emit("tensor", [("vs", v3)], lambda t: (
                allmm(t, psI3, sVNT, sT1), allmm(t, psI4, sT1, sVNT))[-1], inc="ts")
            v_prev = p.emit("vector", [("ts", t4)], lambda v: (
                v.tensor_scalar(out=sVN[:, :], in0=psI3[:, :], scalar1=0.25, scalar2=None,
                                op0=ALU.mult, op1=ALU.bypass),
                v.tensor_scalar(out=sVNT[:, :], in0=psI4[:, :], scalar1=0.25, scalar2=None,
                                op0=ALU.mult, op1=ALU.bypass))[-1], inc="vs")

        pchk("pa3c")
        v_vntb = p.emit("vector", [("vs", v_prev)], lambda v: v.tensor_copy(
            out=VNTB[0:64, :], in_=sVNT[:, :]), inc="vs")

        def z2mm(t):
            last = None
            for h in range(NH):
                pso = psI1 if h % 2 == 0 else psI2
                last = t.matmul(pso[:, (h // 2) * 64:(h // 2 + 1) * 64],
                                VNTB[0:64, h * 64:(h + 1) * 64],
                                K3VB[0:64, h * 65:h * 65 + 64], start=True, stop=True)
            return last
        tz2 = p.emit("tensor", [("vs", v_vntb)], z2mm, inc="ts")
        v_z2e = p.emit("vector", [("ts", tz2)], lambda v: (
            v.memset(Z2A[:, :], 0.0),
            v.memset(ZTMPB[:, :], 0.0),
            v.drain(),
            v.tensor_copy(out=ZTMPB[0:64, :].rearrange("p (g y) -> p g y", g=NCH)[:, :, 0:64],
                          in_=psI2[:, 0:512].rearrange("p (g n) -> p g n", g=NCH)),
            v.tensor_copy(out=Z2A[0:64, :].rearrange("p (g y) -> p g y", g=NCH)[:, :, 0:64],
                          in_=psI1[:, 0:512].rearrange("p (g n) -> p g n", g=NCH)))[-1], inc="vs")
        v_z2d = p.dma("sync", [("vs", v_z2e)], Z2A[64:128, :], ZTMPB[0:64, :], "dq")
        v_ones = p.emit("vector", [("dq", v_z2d)], lambda v: v.memset(
            Z2A[:, 64:520:65], 1.0), inc="vs")
        pa3.close()
        p.flush("pa3")
        pchk("pa3")
        if "pa3" in debug:
            dbg_list += [("pa3_z2a", Z2A)]

        pa4 = ExitStack()
        SOFT = p.sb("SOFT", [128, 2048], F32, pa4)
        psSV = p.ps("psSV", [128, 2048], F32, pa4)
        psN = p.ps("psN", [128, 512], F32, pa4)
        psN2 = p.ps("psN2", [128, 512], F32, pa4)
        s_soft_prev = None
        v_norm = {}
        for h in range(NH):
            r0 = (h % 2) * 64
            hg = h // 2

            def svmm(t, r0=r0, hg=hg):
                last = None
                for qg in range(4):
                    last = t.matmul(psSV[0:65, qg * 512:(qg + 1) * 512],
                                    Z2A[r0:r0 + 64, hg * 65:(hg + 1) * 65],
                                    BIG[r0:r0 + 64, hg * 2048 + qg * 512: hg * 2048 + (qg + 1) * 512],
                                    start=True, stop=True)
                return last
            w = [("vs", v_ones)] if h == 0 else [("ss", s_soft_prev)]
            tsv = p.emit("tensor", w, svmm, inc="ts")
            s_soft_prev = p.emit("scalar", [("ts", tsv)], lambda s: s.activation(
                out=SOFT[0:65, :], in_=psSV[0:65, :], func=AF.Copy), inc="ss")
            for qt in range(NTT):
                psn = psN if qt % 2 == 0 else psN2
                w = [("ss", s_soft_prev)]
                idx0 = h * NTT + qt
                if idx0 >= 2:
                    w.append(("vs", v_norm[idx0 - 2]))
                tn = p.emit("tensor", w, lambda t, qt=qt, psn=psn: t.transpose(
                    psn[:, 0:65], SOFT[0:65, qt * 128:(qt + 1) * 128],
                    IDF[0:65, 0:65]), inc="ts")

                def nrm(v, qt=qt, psn=psn, h=h):
                    mc = SM_MISC + (qt % 4)
                    v.reciprocal(out=SM[:, mc:mc + 1], in_=psn[:, 64:65])
                    v.drain()
                    return v.tensor_scalar(out=XNFM[:, qt * 1024 + h * 64: qt * 1024 + (h + 1) * 64],
                                           in0=psn[:, 0:64],
                                           scalar1=SM[:, mc:mc + 1], scalar2=None,
                                           op0=ALU.mult, op1=ALU.bypass)
                v_norm[h * NTT + qt] = p.emit("vector", [("ts", tn)], nrm, inc="vs")
        v_svtm = p.cnt["vs"]
        pa4.close()
        p.flush("pa4")
        pchk("pa4")
        if "pa4" in debug:
            dbg_list += [("pa4_svtm", XNFM)]

        pa5 = ExitStack()
        psb0b = p.ps("psb0b", [128, 1024], BF16, pa5)
        psb1b = p.ps("psb1b", [128, 1024], BF16, pa5)
        psP = [p.ps("psP%d" % i, [128, 512], F32, pa5) for i in range(4)]
        v_wp = p.dma("gpsimd", [], WBUF[:, :].rearrange("p (c n) -> p c n", c=NCH),
                     wproj.rearrange("(c p) n -> p c n", p=128), "dw")
        t_tr2, v_ev2 = {}, {}
        for tt in range(NTT):
            psb = psb0b if tt % 2 == 0 else psb1b

            def trfn(t, tt=tt, psb=psb):
                last = None
                for c in range(NCH):
                    last = t.transpose(psb[:, c * 128:(c + 1) * 128],
                                       XNFM[:, tt * 1024 + c * 128: tt * 1024 + (c + 1) * 128],
                                       IDB[:, :])
                return last
            w = [("vs", v_svtm)] if tt == 0 else []
            if tt >= 2:
                w.append(("vs", v_ev2[tt - 2]))
            t_tr2[tt] = p.emit("tensor", w, trfn, inc="ts")
            v_ev2[tt] = p.emit("vector", [("ts", t_tr2[tt])], lambda v, tt=tt, psb=psb: v.tensor_copy(
                out=BIG[:, :].rearrange("p (c t) -> p c t", c=NCH)[:, :, tt * 128:(tt + 1) * 128],
                in_=psb[:, :].rearrange("p (c n) -> p c n", c=NCH)), inc="vs")
        v_pr = {}
        for tt in range(NTT):
            for og in range(2):
                k = tt * 2 + og
                pp = psP[k % 4]
                w = [("vs", v_ev2[tt])]
                if k == 0:
                    w.append(("dw", v_wp))
                if k >= 4:
                    w.append(("vs", v_pr[k - 4]))

                def pmm(t, tt=tt, og=og, pp=pp):
                    last = None
                    for c in range(NCH):
                        last = t.matmul(pp[:, :], BIG[:, c * T + tt * 128: c * T + (tt + 1) * 128],
                                        WBUF[:, c * 1024 + og * 512: c * 1024 + (og + 1) * 512],
                                        start=(c == 0), stop=(c == NCH - 1))
                    return last
                tp = p.emit("tensor", w, pmm, inc="ts")
                v_pr[k] = p.emit("vector", [("ts", tp)], lambda v, tt=tt, og=og, pp=pp, k=k: (
                                 v.drain(),
                                 v.scalar_tensor_tensor(
                    out=X[:, tt * D + og * 512: tt * D + (og + 1) * 512], in0=pp[:, :],
                    scalar=1.0, in1=X[:, tt * D + og * 512: tt * D + (og + 1) * 512],
                    op0=ALU.mult, op1=ALU.add,
                    accum_out=SM[:, SM_A + k:SM_A + k + 1]))[-1], inc="vs")
        v_x1 = p.cnt["vs"]
        pa5.close()
        stA.close()
        p.emit("vector", [("vs", v_x1)], lambda v: v.tensor_tensor(
            out=SM[:, SM_SUM:SM_SUM + 16], in0=SM[:, SM_A:SM_A + 32:2],
            in1=SM[:, SM_A + 1:SM_A + 32:2], op=ALU.add), inc="vs")
        p.flush("pa5")
        pchk("pa5")
        if "pa5" in debug:
            dbg_list += [("pa5_x", X)]

        def phase_ffnq(tag, accum_for_next):
            st = ExitStack()
            WQT = p.sb(tag + "WQT", [128, 8192], BF16, st)
            XQB = p.sb(tag + "XQB", [128, 2048], BF16, st)
            WSB = p.sb(tag + "WSB", [128, 1024], F32, st)
            BB = p.sb(tag + "BB", [128, 1024], F32, st)
            ROWS = p.sb(tag + "ROWS", [128, 16], F32, st)
            ps0 = p.ps(tag + "ps0", [128, 512], F32, st)
            ps1 = p.ps(tag + "ps1", [128, 512], F32, st)
            ps2 = p.ps(tag + "ps2", [128, 512], F32, st)
            ps3 = p.ps(tag + "ps3", [128, 512], F32, st)
            psq = p.ps(tag + "psq", [128, 1024], BF16, st)
            psq2 = p.ps(tag + "psq2", [128, 1024], BF16, st)
            stq = ExitStack()
            FP32S = p.sb(tag + "FP32S", [128, 2048], F32, stq)

            v_fw = {0: p.dma("sync", [], FP32S[:, 0:1024], fw[0:128, :], "dh0")}
            v_amax, t_wq = {}, {}
            for c in range(NCH):
                slot = c % 2
                if c + 1 < NCH:
                    wd = [("vs", v_amax[c - 1])] if c >= 1 else []
                    v_fw[c + 1] = p.dma("sync", wd, FP32S[:, (1 - slot) * 1024:(2 - slot) * 1024],
                                        fw[(c + 1) * 128:(c + 2) * 128, :], "dh0" if (c + 1) % 2 == 0 else "dh1")
                src = FP32S[:, slot * 1024:(slot + 1) * 1024]
                w = [("dh0" if c % 2 == 0 else "dh1", v_fw[c])]
                if c >= 2:
                    w.append(("ts", t_wq[c - 2]))
                v_amax[c] = p.emit("vector", w, lambda v, c=c, src=src: (
                    v.reduce_max(out=ROWS[:, c:c + 1], in_=src, axis=AX.X,
                                 apply_absolute_value=True),
                    v.drain(),
                    v.reciprocal(out=SM[:, SM_MISC + (c % 2):SM_MISC + (c % 2) + 1],
                                 in_=ROWS[:, c:c + 1]),
                    v.drain(),
                    v.tensor_scalar(out=ROWS[:, 8 + c:9 + c],
                                    in0=SM[:, SM_MISC + (c % 2):SM_MISC + (c % 2) + 1],
                                    scalar1=127.0, scalar2=None, op0=ALU.mult, op1=ALU.bypass),
                    v.drain(),
                    v.tensor_scalar(out=XTMPF[:, 0:1024], in0=src,
                                    scalar1=ROWS[:, 8 + c:9 + c], scalar2=MAGIC,
                                    op0=ALU.mult, op1=ALU.add),
                    v.drain(),
                    v.tensor_scalar(out=XQB[:, 0:1024], in0=XTMPF[:, 0:1024],
                                    scalar1=-MAGIC, scalar2=None, op0=ALU.add, op1=ALU.bypass)
                    )[-1], inc="vs")
                tq = p.emit("tensor", [("vs", v_amax[c]), ("dc", v_consts)],
                            lambda t, slot=slot: [
                    t.transpose((psq if slot == 0 else psq2)[:, j * 128:(j + 1) * 128],
                                XQB[:, j * 128:(j + 1) * 128], IDB[:, :])
                    for j in range(NCH)][-1], inc="ts")
                t_wq[c] = tq
                p.emit("vector", [("ts", tq)], lambda v, c=c, slot=slot: v.tensor_copy(
                    out=WQT[:, :].rearrange("p (j n) -> p j n", j=NCH)[:, :, c * 128:(c + 1) * 128],
                    in_=(psq if slot == 0 else psq2)[:, :].rearrange("p (j n) -> p j n", j=NCH)),
                    inc="vs")
            v_wqt = p.cnt["vs"]
            stq.close()
            WSR = p.sb(tag + "WSR", [1, 2048], F32, st)
            def wsrow(t):
                last = None
                for c in range(NCH):
                    pb = ps2 if c < 4 else ps3
                    last = t.transpose(pb[0:1, (c % 4) * 128:((c % 4) + 1) * 128],
                                       ROWS[:, c:c + 1], IDF[:, :])
                return last
            tws = p.emit("tensor", [("vs", v_amax[NCH - 1])], wsrow, inc="ts")
            v_wsr = p.emit("vector", [("ts", tws)], lambda v: (
                v.tensor_copy(out=WSR[0:1, 0:512], in_=ps2[0:1, 0:512]),
                v.tensor_copy(out=WSR[0:1, 512:1024], in_=ps3[0:1, 0:512]))[-1], inc="vs")
            v_fb = p.dma("sync", [("dh0", v_fw[6]), ("dh1", v_fw[7]), ("vs", v_amax[NCH - 1])], WSR[0:1, 1024:2048], fb, "dc")

            def bcasts(t):
                t.matmul(ps0[:, :], ONEF[0:1, 0:128], WSR[0:1, 0:512], start=True, stop=True)
                t.matmul(ps1[:, :], ONEF[0:1, 0:128], WSR[0:1, 512:1024], start=True, stop=True)
                t.matmul(ps2[:, :], ONEF[0:1, 0:128], WSR[0:1, 1024:1536], start=True, stop=True)
                return t.matmul(ps3[:, :], ONEF[0:1, 0:128], WSR[0:1, 1536:2048],
                                start=True, stop=True)
            tbc = p.emit("tensor", [("vs", v_wsr), ("dc", v_fb)], bcasts, inc="ts")
            v_bc = p.emit("vector", [("ts", tbc)], lambda v: (
                v.tensor_scalar(out=WSB[:, 0:512], in0=ps0[:, :], scalar1=1.0 / 127.0,
                                scalar2=None, op0=ALU.mult, op1=ALU.bypass),
                v.tensor_scalar(out=WSB[:, 512:1024], in0=ps1[:, :], scalar1=1.0 / 127.0,
                                scalar2=None, op0=ALU.mult, op1=ALU.bypass),
                v.tensor_copy(out=BB[:, 0:512], in_=ps2[:, :]),
                v.tensor_copy(out=BB[:, 512:1024], in_=ps3[:, :]))[-1], inc="vs")

            v_inv2 = ln_stats(have_sums=True)
            v_xq_ev, v_res = {}, {}
            for tt in range(NTT):
                slot = tt % 2
                xq = XQB[:, slot * 1024:(slot + 1) * 1024]
                xf = XTMPF[:, slot * 1024:(slot + 1) * 1024]
                w = [("vs", v_inv2), ("vs", v_bc), ("vs", v_wqt)]
                if tt >= 2:
                    w.append(("ts", v_xq_ev[tt - 2]))
                rot = tt % 2
                va = p.emit("vector", w, lambda v, tt=tt, xf=xf, xq=xq, rot=rot: (
                    v.tensor_scalar(out=xf, in0=X[:, tt * D:(tt + 1) * D],
                                    scalar1=SM[:, SM_INV + tt:SM_INV + tt + 1],
                                    scalar2=SM[:, SM_NMI + tt:SM_NMI + tt + 1],
                                    op0=ALU.mult, op1=ALU.add),
                    v.drain(),
                    v.reduce_max(out=SM[:, SM_AM + rot:SM_AM + rot + 1], in_=xf, axis=AX.X,
                                 apply_absolute_value=True),
                    v.drain(),
                    v.reciprocal(out=SM[:, SM_RA + rot:SM_RA + rot + 1],
                                 in_=SM[:, SM_AM + rot:SM_AM + rot + 1]),
                    v.drain(),
                    v.tensor_scalar(out=SM[:, SM_XS + tt:SM_XS + tt + 1],
                                    in0=SM[:, SM_AM + rot:SM_AM + rot + 1],
                                    scalar1=1.0 / 127.0, scalar2=None, op0=ALU.mult, op1=ALU.bypass),
                    v.tensor_scalar(out=SM[:, SM_R127 + rot:SM_R127 + rot + 1],
                                    in0=SM[:, SM_RA + rot:SM_RA + rot + 1],
                                    scalar1=127.0, scalar2=None, op0=ALU.mult, op1=ALU.bypass),
                    v.drain(),
                    v.tensor_scalar(out=xf, in0=xf, scalar1=SM[:, SM_R127 + rot:SM_R127 + rot + 1],
                                    scalar2=MAGIC, op0=ALU.mult, op1=ALU.add),
                    v.drain(),
                    v.tensor_scalar(out=xq, in0=xf, scalar1=-MAGIC, scalar2=None,
                                    op0=ALU.add, op1=ALU.bypass))[-1], inc="vs")
                tq = p.emit("tensor", [("vs", va)], lambda t, xq=xq, slot=slot: [
                    t.transpose((psq if slot == 0 else psq2)[:, j * 128:(j + 1) * 128],
                                xq[:, j * 128:(j + 1) * 128], IDB[:, :]) for j in range(NCH)][-1],
                    inc="ts")
                v_xq_ev[tt] = tq
                v_xq = p.emit("vector", [("ts", tq)], lambda v, tt=tt, slot=slot: v.tensor_copy(
                    out=XNFM[:, :].rearrange("p (c t) -> p c t", c=NCH)[:, :, tt * 128:(tt + 1) * 128],
                    in_=(psq if slot == 0 else psq2)[:, :].rearrange("p (j n) -> p j n", j=NCH)),
                    inc="vs")
                for og in range(2):
                    pp = ps0 if og == 0 else ps1
                    w2 = [("vs", v_xq)]
                    if tt >= 1:
                        w2.append(("vs", v_res[(tt - 1) * 2 + og]))

                    def qmm(t, tt=tt, pp=pp, og=og):
                        last = None
                        for j in range(NCH):
                            last = t.matmul(pp[:, :],
                                            XNFM[:, j * T + tt * 128: j * T + (tt + 1) * 128],
                                            WQT[:, j * 1024 + og * 512: j * 1024 + (og + 1) * 512],
                                            start=(j == 0), stop=(j == NCH - 1))
                        return last
                    tmm = p.emit("tensor", w2, qmm, inc="ts")
                    s1 = p.emit("scalar", [("ts", tmm)], lambda s, pp=pp, tt=tt, og=og:
                                s.activation(
                        out=SOFT2[:, og * 512:(og + 1) * 512], in_=pp[:, :], func=AF.Identity,
                        scale=SM[:, SM_XS + tt:SM_XS + tt + 1], bias=CZCOL[:, 0:1]), inc="ss")

                    def resf(v, tt=tt, og=og):
                        sl = slice(og * 512, (og + 1) * 512)
                        xsl = slice(tt * D + og * 512, tt * D + (og + 1) * 512)
                        v.tensor_tensor(out=SOFT2[:, sl], in0=SOFT2[:, sl], in1=WSB[:, sl],
                                        op=ALU.mult)
                        v.drain()
                        v.tensor_tensor(out=SOFT2[:, sl], in0=SOFT2[:, sl], in1=BB[:, sl],
                                        op=ALU.add)
                        v.drain()
                        if accum_for_next:
                            return v.scalar_tensor_tensor(
                                out=X[:, xsl], in0=SOFT2[:, sl], scalar=1.0, in1=X[:, xsl],
                                op0=ALU.mult, op1=ALU.add,
                                accum_out=SM[:, SM_A + tt * 2 + og:SM_A + tt * 2 + og + 1])
                        v.drain()
                        return v.tensor_tensor(out=X[:, xsl], in0=SOFT2[:, sl], in1=X[:, xsl],
                                               op=ALU.add)
                    v_res[tt * 2 + og] = p.emit("vector", [("ss", s1)], resf, inc="vs")
            if accum_for_next:
                p.emit("vector", [("vs", p.cnt["vs"])], lambda v: v.tensor_tensor(
                    out=SM[:, SM_SUM:SM_SUM + 16], in0=SM[:, SM_A:SM_A + 32:2],
                    in1=SM[:, SM_A + 1:SM_A + 32:2], op=ALU.add), inc="vs")
            st.close()
            p.flush(tag)

        if nphases >= 2:
            phase_ffnq("pb", accum_for_next=True)
            if "pb" in debug:
                dbg_list += [("pb_x", X)]

        def phase_attnb(pccut=None):
            stC = ExitStack()
            T2K = p.sb("T2K", [128, T], F32, stC)
            NEGC = p.sb("cNEGC", [128, 4], F32, stC)
            GB = p.sb("cGB", [128, 64], F32, stC)
            B1C = p.sb("cB1C", [128, 32], F32, stC)
            BVR = p.sb("cBVR", [128, 1024], BF16, stC)

            p.dma("sync", [], T2K[:, :], c_t2, "dc")
            v_t2 = p.cnt["dc"]

            v_sp = p.dma("sync", [("vs", p.cnt["vs"]), ("ss", p.cnt["ss"])],
                         spill.rearrange("(t p) d -> p t d", p=128),
                         X[:, :].rearrange("p (t d) -> p t d", t=NTT), "dx")
            cut = [False]
            def cuts(name, *tens):
                if pccut == name:
                    for i, tn in enumerate(tens):
                        dbg_list.append((name + "_%d" % i, tn if isinstance(tn, bass.AP) else tn[:, :]))
                    cut[0] = True
                return cut[0]
            v_inv3 = ln_stats(have_sums=True)
            v_z = {}
            for tt in range(NTT):
                w = [("vs", v_inv3)]
                if tt == 0:
                    w.append(("dx", v_sp))
                v_z[tt] = p.emit("vector", w, lambda v, tt=tt: v.tensor_scalar(
                    out=X[:, tt * D:(tt + 1) * D], in0=X[:, tt * D:(tt + 1) * D],
                    scalar1=SM[:, SM_INV + tt:SM_INV + tt + 1],
                    scalar2=SM[:, SM_NMI + tt:SM_NMI + tt + 1],
                    op0=ALU.mult, op1=ALU.add), inc="vs")
            p.flush("pc0")
            if cuts("pc0", X):
                return
            for l in range(2):
                wpar = [("vs", p.cnt["vs"]), ("ss", p.cnt["ss"])] if l > 0 else []
                for i, (src, col) in enumerate([(cln1g, 0), (cln1b, 8), (cln2g, 16), (cln2b, 24),
                                                (cbq, 32), (cbk, 40)]):
                    p.dma("sync", wpar if i == 0 else [], GB[:, col:col + 8], src[l], "dc")
                p.dma("sync", [], B1C[:, :], cb1[l], "dc")
                v_par = p.cnt["dc"]
                v_bvr0 = p.emit("vector", [("vs", p.cnt["vs"])], lambda v: v.memset(
                    BVR[:, :], 0.0), inc="vs")
                v_bvr = p.dma("gpsimd", [("ss", p.cnt["ss"]), ("vs", v_bvr0)],
                              BVR[0:1, :], bbv[l:l + 1, :], "dw")
                v_bqs = p.emit("vector", [("dc", v_par)], lambda v: v.tensor_scalar(
                    out=GB[:, 48:56], in0=GB[:, 32:40], scalar1=0.125, scalar2=None,
                    op0=ALU.mult, op1=ALU.bypass), inc="vs")

                stP = ExitStack()
                psF1 = p.ps("psF1", [128, 1024], F32, stP)
                psF2 = p.ps("psF2", [128, 1024], F32, stP)
                v_inva = ln_stats(have_sums=False)
                t_f = {}
                v_fe = {}
                for tt in range(NTT):
                    slot = tt % 2
                    xf = XTMPF[:, slot * 1024:(slot + 1) * 1024]
                    w = [("vs", v_inva)]
                    if tt >= 2:
                        w.append(("ts", t_f[tt - 2]))
                    va = p.emit("vector", w, lambda v, tt=tt, xf=xf: v.tensor_scalar(
                        out=xf, in0=X[:, tt * D:(tt + 1) * D],
                        scalar1=SM[:, SM_INV + tt:SM_INV + tt + 1],
                        scalar2=SM[:, SM_NMI + tt:SM_NMI + tt + 1],
                        op0=ALU.mult, op1=ALU.add), inc="vs")

                    def tf(t, xf=xf, slot=slot):
                        psf = psF1 if slot == 0 else psF2
                        last = None
                        for c in range(NCH):
                            last = t.transpose(psf[:, (c % 4) * 256 + (c // 4) * 128:
                                                   (c % 4) * 256 + (c // 4) * 128 + 128]
                                               if False else psf[:, c * 128:(c + 1) * 128],
                                               xf[:, c * 128:(c + 1) * 128], IDF[:, :])
                        return last
                    w2 = [("vs", va)]
                    if tt >= 2:
                        w2.append(("vs", v_fe[tt - 2]))
                    t_f[tt] = p.emit("tensor", w2, tf, inc="ts")

                    def fe(v, tt=tt, slot=slot):
                        psf = psF1 if slot == 0 else psF2
                        last = None
                        for c in range(NCH):
                            last = v.tensor_scalar(
                                out=XNFM[:, c * T + tt * 128: c * T + (tt + 1) * 128],
                                in0=psf[:, c * 128:(c + 1) * 128],
                                scalar1=GB[:, 0 + c:1 + c], scalar2=GB[:, 8 + c:9 + c],
                                op0=ALU.mult, op1=ALU.add)
                        return last
                    v_fe[tt] = p.emit("vector", [("ts", t_f[tt])], fe, inc="vs")
                stP.close()
                p.flush("pc1_l%d" % l)
                if cuts("pc1", XNFM):
                    return

                stL = ExitStack()
                QBUF = p.sb("cQBUF", [128, T], BF16, stL)
                KBUF = p.sb("cKBUF", [128, T], BF16, stL)
                VBUF = p.sb("cVBUF", [128, 2 * 1040], BF16, stL)
                UT = p.sb("cUT", [128, 4096], BF16, stL)
                SOFT = p.sb("cSOFT", [128, 2048], F32, stL)
                stP = ExitStack()
                ps0 = p.ps("cps0", [128, 512], F32, stP)
                ps1 = p.ps("cps1", [128, 512], F32, stP)
                ps2 = p.ps("cps2", [128, 512], F32, stP)
                ps3 = p.ps("cps3", [128, 512], F32, stP)
                psPV = p.ps("cpsPV", [65, 2048], F32, stP)

                def dma_qkv_b(hg, slot, waits):
                    res = []
                    for vi, wsrc in enumerate([bwq, bwk, bwv]):
                        sem = ("dwa", "dwb", "dwc")[vi]
                        res.append((sem, p.dma("gpsimd", waits if vi == 0 else [],
                                    WBUF[:, slot * 4096 + vi * 1024: slot * 4096 + (vi + 1) * 1024]
                                    .rearrange("p (c n) -> p c n", c=NCH),
                                    wsrc[l].rearrange("(c p) n -> p c n", p=128)
                                    [:, :, hg * 128:(hg + 1) * 128], sem)))
                    return res
                v_w = {0: dma_qkv_b(0, 0, [("dw", v_bvr)])}
                fence = []
                v_norm = {}
                prev_soft = None
                for hg in range(NCH):
                    slot = hg % 2
                    WQ = WBUF[:, slot * 4096: slot * 4096 + 1024]
                    WK = WBUF[:, slot * 4096 + 1024: slot * 4096 + 2048]
                    WV = WBUF[:, slot * 4096 + 2048: slot * 4096 + 3072]
                    qk_ev = {}
                    for tg in range(4):
                        pq = ps0 if tg % 2 == 0 else ps1
                        pk = ps2 if tg % 2 == 0 else ps3
                        w = []
                        if tg == 0:
                            w += list(v_w[hg]) + [("vs", v_bqs)]
                            if fence:
                                w += [("vs", fence[-1][0]), ("ss", fence[-1][1])]
                        if tg >= 2:
                            w.append(("vs", qk_ev[tg - 2]))

                        def mmqk(t, tg=tg, pq=pq, pk=pk, WQ=WQ, WK=WK):
                            for c in range(NCH):
                                t.matmul(pq[:, :], WQ[:, c * 128:(c + 1) * 128],
                                         XNFM[:, c * T + tg * 512: c * T + (tg + 1) * 512],
                                         start=(c == 0), stop=(c == NCH - 1))
                            last = None
                            for c in range(NCH):
                                last = t.matmul(pk[:, :], WK[:, c * 128:(c + 1) * 128],
                                                XNFM[:, c * T + tg * 512: c * T + (tg + 1) * 512],
                                                start=(c == 0), stop=(c == NCH - 1))
                            return last
                        tq = p.emit("tensor", w, mmqk, inc="ts")
                        sq = p.emit("scalar", [("ts", tq)], lambda s, tg=tg, pq=pq, pk=pk, hg=hg: (
                            s.activation(out=XTMPF[:, (tg % 2) * 512:(tg % 2) * 512 + 512],
                                         in_=pq[:, :], func=AF.Identity, scale=0.125,
                                         bias=GB[:, 48 + hg:49 + hg]),
                            s.activation(out=XTMPF[:, 1024 + (tg % 2) * 512:1024 + (tg % 2) * 512 + 512],
                                         in_=pk[:, :], func=AF.Identity, scale=1.0,
                                         bias=GB[:, 40 + hg:41 + hg]))[-1], inc="ss")
                        qk_ev[tg] = p.emit("vector", [("ss", sq)], lambda v, tg=tg: (
                            v.tensor_tensor(out=QBUF[:, tg * 512:(tg + 1) * 512],
                                            in0=XTMPF[:, (tg % 2) * 512:(tg % 2) * 512 + 512],
                                            in1=T2K[:, tg * 512:(tg + 1) * 512], op=ALU.mult),
                            v.tensor_tensor(out=KBUF[:, tg * 512:(tg + 1) * 512],
                                            in0=XTMPF[:, 1024 + (tg % 2) * 512:1024 + (tg % 2) * 512 + 512],
                                            in1=T2K[:, tg * 512:(tg + 1) * 512], op=ALU.mult))[-1],
                            inc="vs")
                    v_vev = {}
                    for g in range(4):
                        pv = ps0 if g % 2 == 0 else ps1
                        w = [("ss", qk_ev[3] if False else qk_ev[3])] if g == 0 else []
                        if g == 0:
                            w = [("vs", qk_ev[3])]
                        if g >= 2:
                            w.append(("vs", v_vev[g - 2]))

                        def mmv(t, g=g, pv=pv, WV=WV, hg=hg):
                            last = None
                            for i in range(4):
                                tt = g * 4 + i
                                for c in range(NCH):
                                    t.matmul(pv[:, i * 128:(i + 1) * 128],
                                             XNFM[:, c * T + tt * 128: c * T + (tt + 1) * 128],
                                             WV[:, c * 128:(c + 1) * 128],
                                             start=(c == 0), stop=False)
                                last = t.matmul(pv[:, i * 128:(i + 1) * 128], ONEB[0:1, 0:128],
                                                BVR[0:1, hg * 128:(hg + 1) * 128],
                                                start=False, stop=True)
                            return last
                        tv = p.emit("tensor", w, mmv, inc="ts")

                        def vev(v, g=g, pv=pv):
                            v.tensor_copy(
                                out=VBUF[:, 0:1040].rearrange("p (t y) -> p t y", t=NTT)
                                [:, g * 4:(g + 1) * 4, 0:64],
                                in_=pv[:, :].rearrange("p (t n) -> p t n", t=4)[:, :, 0:64])
                            return v.tensor_copy(
                                out=VBUF[:, 1040:2080].rearrange("p (t y) -> p t y", t=NTT)
                                [:, g * 4:(g + 1) * 4, 0:64],
                                in_=pv[:, :].rearrange("p (t n) -> p t n", t=4)[:, :, 64:128])
                        v_vev[g] = p.emit("vector", [("ts", tv)], vev, inc="vs")
                    if hg == 0:
                        p.emit("vector", [], lambda v: (
                            v.memset(VBUF[:, 64:1040:65], 1.0),
                            v.memset(VBUF[:, 1104:2080:65], 1.0))[-1], inc="vs")
                    if hg + 1 < NCH:
                        v_w[hg + 1] = dma_qkv_b(hg + 1, (hg + 1) % 2,
                                                [("vs", qk_ev[3]), ("vs", v_vev[3])])

                    for hh in range(2):
                        h = hg * 2 + hh
                        r0 = hh * 64
                        w = [("vs", qk_ev[3]), ("vs", v_vev[3]), ("vs", v_vev[2])] if hh == 0 else \
                            [("ss", prev_soft)]
                        tc = p.emit("tensor", w, lambda t, r0=r0: t.matmul(
                            ps2[:, 0:128], KBUF[r0:r0 + 64, 0:T:16], QBUF[r0:r0 + 64, 0:T:16],
                            start=True, stop=True), inc="ts")
                        vc = p.emit("vector", [("ts", tc)], lambda v: v.reduce_max(
                            out=SM[:, SM_MISC:SM_MISC + 1], in_=ps2[:, 0:128], axis=AX.X), inc="vs")
                        tc2 = p.emit("tensor", [("vs", vc)], lambda t: t.transpose(
                            ps3[0:1, 0:128], SM[:, SM_MISC:SM_MISC + 1], IDF[:, :]), inc="ts")
                        vc2 = p.emit("vector", [("ts", tc2)], lambda v: v.reduce_max(
                            out=SM[0:1, SM_MISC + 1:SM_MISC + 2], in_=ps3[0:1, 0:128], axis=AX.X),
                            inc="vs")
                        tc3 = p.emit("tensor", [("vs", vc2)], lambda t: t.matmul(
                            ps3[:, 128:129], ONEF[0:1, 0:128], SM[0:1, SM_MISC + 1:SM_MISC + 2],
                            start=True, stop=True), inc="ts")
                        s_c = p.emit("scalar", [("ts", tc3)], lambda s, hh=hh: s.activation(
                            out=NEGC[:, hh:hh + 1], in_=ps3[:, 128:129], func=AF.Copy,
                            scale=-1.0, bias=-CSLACK), inc="ss")

                        e_ev = {}
                        for kb in range(NTT):
                            uts = UT[:, (kb % 2) * 2048:(kb % 2 + 1) * 2048]
                            for qg in range(4):
                                m = kb * 4 + qg
                                pse2 = ps2 if m % 2 == 0 else ps3
                                w2 = [("ss", s_c)] if m == 0 else []
                                if m >= 2:
                                    w2.append(("ss", e_ev[m - 2]))
                                tsm = p.emit("tensor", w2, lambda t, kb=kb, qg=qg, pse2=pse2, r0=r0:
                                             t.matmul(pse2[:, :],
                                                      KBUF[r0:r0 + 64, kb * 128:(kb + 1) * 128],
                                                      QBUF[r0:r0 + 64, qg * 512:(qg + 1) * 512],
                                                      start=True, stop=True), inc="ts")
                                e_ev[m] = p.emit("scalar", [("ts", tsm)],
                                                 lambda s, pse2=pse2, uts=uts, qg=qg, hh=hh:
                                                 s.activation(
                                    out=uts[:, qg * 512:(qg + 1) * 512], in_=pse2[:, :], func=AF.Exp,
                                    bias=NEGC[:, hh:hh + 1]), inc="ss")

                            def pvmm(t, kb=kb, uts=uts, hh=hh):
                                last = None
                                for qg in range(4):
                                    last = t.matmul(psPV[0:65, qg * 512:(qg + 1) * 512],
                                                    VBUF[:, hh * 1040 + kb * 65:
                                                         hh * 1040 + (kb + 1) * 65],
                                                    uts[:, qg * 512:(qg + 1) * 512],
                                                    start=(kb == 0), stop=(kb == NTT - 1))
                                return last
                            p.emit("tensor", [("ss", e_ev[kb * 4 + 3])], pvmm, inc="ts")
                        tpv = p.cnt["ts"]
                        s_soft = p.emit("scalar", [("ts", tpv)], lambda s: s.activation(
                            out=SOFT[0:65, :], in_=psPV[0:65, :], func=AF.Copy), inc="ss")
                        prev_soft = s_soft
                        for qt in range(NTT):
                            psn = ps0 if qt % 2 == 0 else ps1
                            w3 = [("ss", s_soft)]
                            if qt == 0 and hh == 0:
                                w3 += [("vs", v_vev[3]), ("vs", v_vev[2])]
                            idx0 = h * NTT + qt
                            if idx0 - 2 >= 0 and (idx0 - 2) in v_norm:
                                w3.append(("vs", v_norm[idx0 - 2]))
                            tn = p.emit("tensor", w3, lambda t, qt=qt, psn=psn: t.transpose(
                                psn[:, 0:65], SOFT[0:65, qt * 128:(qt + 1) * 128],
                                IDF[0:65, 0:65]), inc="ts")

                            def nrm(v, qt=qt, psn=psn, h=h):
                                mc = SM_MISC + 2 + (qt % 4)
                                v.reciprocal(out=SM[:, mc:mc + 1], in_=psn[:, 64:65])
                                v.drain()
                                return v.tensor_scalar(
                                    out=BIG[:, qt * 1024 + h * 64: qt * 1024 + (h + 1) * 64],
                                    in0=psn[:, 0:64],
                                    scalar1=SM[:, mc:mc + 1], scalar2=None,
                                    op0=ALU.mult, op1=ALU.bypass)
                            v_norm[h * NTT + qt] = p.emit("vector", [("ts", tn)], nrm, inc="vs")
                    fence.append((p.cnt["vs"], p.cnt["ss"]))
                v_att = p.cnt["vs"]
                v_wo = p.dma("gpsimd", [("vs", fence[-1][0])],
                             WBUF[:, :].rearrange("p (c n) -> p c n", c=NCH),
                             bwo[l].rearrange("(c p) n -> p c n", p=128), "dw")
                stP.close()
                stL.close()
                p.flush("pc23_l%d" % l)
                if cuts("pc3", BIG):
                    return

                stP = ExitStack()
                psb0c = p.ps("cpsb0", [128, 1024], BF16, stP)
                psb1c = p.ps("cpsb1", [128, 1024], BF16, stP)
                psW = [p.ps("cpsW%d" % i, [128, 512], F32, stP) for i in range(4)]
                t_tr3, v_ev3, v_wr = {}, {}, {}
                for tt in range(NTT):
                    psb = psb0c if tt % 2 == 0 else psb1c

                    def trf(t, tt=tt, psb=psb):
                        last = None
                        for c in range(NCH):
                            last = t.transpose(psb[:, c * 128:(c + 1) * 128],
                                               BIG[:, tt * 1024 + c * 128: tt * 1024 + (c + 1) * 128],
                                               IDB[:, :])
                        return last
                    w = [("vs", v_att)] if tt == 0 else []
                    if tt >= 2:
                        w.append(("vs", v_ev3[tt - 2]))
                    t_tr3[tt] = p.emit("tensor", w, trf, inc="ts")
                    v_ev3[tt] = p.emit("vector", [("ts", t_tr3[tt])], lambda v, tt=tt, psb=psb:
                                       v.tensor_copy(
                        out=XNFM[:, :].rearrange("p (c t) -> p c t", c=NCH)[:, :, tt * 128:(tt + 1) * 128],
                        in_=psb[:, :].rearrange("p (c n) -> p c n", c=NCH)), inc="vs")
                for tt in range(NTT):
                    for og in range(2):
                        k = tt * 2 + og
                        pp = psW[k % 4]
                        w = [("vs", v_ev3[tt])]
                        if k == 0:
                            w.append(("dw", v_wo))
                        if k >= 4:
                            w.append(("vs", v_wr[k - 4]))

                        def wmm(t, tt=tt, og=og, pp=pp):
                            last = None
                            for c in range(NCH):
                                last = t.matmul(pp[:, :],
                                                XNFM[:, c * T + tt * 128: c * T + (tt + 1) * 128],
                                                WBUF[:, c * 1024 + og * 512: c * 1024 + (og + 1) * 512],
                                                start=(c == 0), stop=(c == NCH - 1))
                            return last
                        tw = p.emit("tensor", w, wmm, inc="ts")
                        v_wr[k] = p.emit("vector", [("ts", tw)], lambda v, tt=tt, og=og, pp=pp, k=k: (
                            v.drain(),
                            v.scalar_tensor_tensor(
                                out=X[:, tt * D + og * 512: tt * D + (og + 1) * 512], in0=pp[:, :],
                                scalar=1.0, in1=X[:, tt * D + og * 512: tt * D + (og + 1) * 512],
                                op0=ALU.mult, op1=ALU.add,
                                accum_out=SM[:, SM_A + k:SM_A + k + 1]))[-1], inc="vs")
                p.emit("vector", [("vs", p.cnt["vs"])], lambda v: v.tensor_tensor(
                    out=SM[:, SM_SUM:SM_SUM + 16], in0=SM[:, SM_A:SM_A + 32:2],
                    in1=SM[:, SM_A + 1:SM_A + 32:2], op=ALU.add), inc="vs")
                stP.close()
                p.flush("pc4_l%d" % l)
                if cuts("pc4", X):
                    return

                v_invb = ln_stats(have_sums=True)
                stP = ExitStack()
                psF1 = p.ps("dpsF1", [128, 1024], F32, stP)
                psF2 = p.ps("dpsF2", [128, 1024], F32, stP)
                psH0 = p.ps("dpsH0", [128, 512], F32, stP)
                psH1 = p.ps("dpsH1", [128, 512], F32, stP)
                HID2 = p.sb("cHID2", [128, 16384], BF16, stP)

                def hid_t(hc):
                    buf = BIG if hc < 16 else HID2
                    return buf[:, (hc % 16) * 1024:(hc % 16 + 1) * 1024]

                for half in range(2):
                    t_f2, v_fe2 = {}, {}
                    for i in range(8):
                        tt = half * 8 + i
                        slot = i % 2
                        xf = XTMPF[:, slot * 1024:(slot + 1) * 1024]
                        w = [("vs", v_invb)]
                        if i >= 2:
                            w.append(("ts", t_f2[i - 2]))
                        va = p.emit("vector", w, lambda v, tt=tt, xf=xf: v.tensor_scalar(
                            out=xf, in0=X[:, tt * D:(tt + 1) * D],
                            scalar1=SM[:, SM_INV + tt:SM_INV + tt + 1],
                            scalar2=SM[:, SM_NMI + tt:SM_NMI + tt + 1],
                            op0=ALU.mult, op1=ALU.add), inc="vs")

                        def tf2(t, xf=xf, slot=slot):
                            psf = psF1 if slot == 0 else psF2
                            last = None
                            for c in range(NCH):
                                last = t.transpose(psf[:, c * 128:(c + 1) * 128],
                                                   xf[:, c * 128:(c + 1) * 128], IDF[:, :])
                            return last
                        w2 = [("vs", va)]
                        if i >= 2:
                            w2.append(("vs", v_fe2[i - 2]))
                        t_f2[i] = p.emit("tensor", w2, tf2, inc="ts")

                        def fe2(v, i=i, slot=slot, half=half):
                            psf = psF1 if slot == 0 else psF2
                            last = None
                            for c in range(NCH):
                                last = v.tensor_scalar(
                                    out=XNFM[:, c * T + half * 1024 + i * 128:
                                             c * T + half * 1024 + (i + 1) * 128],
                                    in0=psf[:, c * 128:(c + 1) * 128],
                                    scalar1=GB[:, 16 + c:17 + c], scalar2=GB[:, 24 + c:25 + c],
                                    op0=ALU.mult, op1=ALU.add)
                            return last
                        v_fe2[i] = p.emit("vector", [("ts", t_f2[i])], fe2, inc="vs")
                    v_xl2 = p.cnt["vs"]

                    v_w1 = {}
                    v_w1[0] = ("dwa", p.dma("gpsimd", [("vs", v_xl2)],
                                    WBUF[:, 0:4096].rearrange("p (c n) -> p c n", c=NCH),
                                    bw1[l].rearrange("(c p) n -> p c n", p=128)[:, :, 0:512], "dwa"))
                    hid_ev = {}
                    for sl8 in range(8):
                        slot = sl8 % 2
                        if sl8 + 1 < 8:
                            wd = [("ss", hid_ev[sl8 * 8 - 1])] if sl8 >= 1 else []
                            sem = "dwa" if (sl8 + 1) % 2 == 0 else "dwb"
                            v_w1[sl8 + 1] = (sem, p.dma("gpsimd", wd,
                                                  WBUF[:, (1 - slot) * 4096:(2 - slot) * 4096]
                                                  .rearrange("p (c n) -> p c n", c=NCH),
                                                  bw1[l].rearrange("(c p) n -> p c n", p=128)
                                                  [:, :, (sl8 + 1) * 512:(sl8 + 2) * 512], sem))
                        for j in range(4):
                            hc = sl8 * 4 + j
                            for tg in range(2):
                                m = hc * 2 + tg
                                ph_ = psH0 if m % 2 == 0 else psH1
                                w = []
                                if j == 0 and tg == 0:
                                    w.append(v_w1[sl8])
                                if m >= 2:
                                    w.append(("ss", hid_ev[m - 2]))

                                def hmm(t, hc=hc, tg=tg, ph_=ph_, slot=slot, half=half):
                                    last = None
                                    for c in range(NCH):
                                        last = t.matmul(
                                            ph_[:, :],
                                            WBUF[:, slot * 4096 + c * 512 + (hc % 4) * 128:
                                                 slot * 4096 + c * 512 + (hc % 4) * 128 + 128],
                                            XNFM[:, c * T + half * 1024 + tg * 512:
                                                 c * T + half * 1024 + (tg + 1) * 512],
                                            start=(c == 0), stop=(c == NCH - 1))
                                    return last
                                th = p.emit("tensor", w, hmm, inc="ts")
                                hid_ev[hc * 2 + tg] = p.emit(
                                    "scalar", [("ts", th)],
                                    lambda s, hc=hc, tg=tg, ph_=ph_: s.activation(
                                        out=hid_t(hc)[:, tg * 512:(tg + 1) * 512], in_=ph_[:, :],
                                        func=AF.Gelu_apprx_sigmoid,
                                        bias=B1C[:, hc:hc + 1]), inc="ss")
                    v_hid_done = p.cnt["ss"]
                    if half == 0 and cuts("pc5h", BIG, HID2, XNFM):
                        stP.close()
                        p.flush("pc5h")
                        return

                    v_w2 = {}
                    v_w2[0] = ("dwa", p.dma("gpsimd", [("ss", v_hid_done)],
                                    WBUF[:, 0:4096].rearrange("p (c n) -> p c n", c=32),
                                    bw2[l].rearrange("(c p) n -> p c n", p=128)[:, :, 0:128], "dwa"))
                    v_w2r = {}
                    for og in range(8):
                        slot = og % 2
                        if og + 1 < 8:
                            wd = [("ts", v_w2r[(og - 1) * 2 + 1][0])] if og >= 1 else []
                            sem = "dwa" if (og + 1) % 2 == 0 else "dwb"
                            v_w2[og + 1] = (sem, p.dma("gpsimd", wd,
                                                 WBUF[:, (1 - slot) * 4096:(2 - slot) * 4096]
                                                 .rearrange("p (c n) -> p c n", c=32),
                                                 bw2[l].rearrange("(c p) n -> p c n", p=128)
                                                 [:, :, (og + 1) * 128:(og + 2) * 128], sem))
                        for tg in range(2):
                            ph_ = psH0 if tg % 2 == 0 else psH1
                            w = [("ss", v_hid_done)]
                            if tg == 0:
                                w.append(v_w2[og])
                            if (og * 2 + tg) >= 2:
                                w.append(("vs", v_w2r[(og * 2 + tg) - 2][1]))

                            def wmm2(t, og=og, tg=tg, ph_=ph_, slot=slot):
                                last = None
                                for hc in range(32):
                                    last = t.matmul(
                                        ph_[:, :],
                                        WBUF[:, slot * 4096 + hc * 128: slot * 4096 + (hc + 1) * 128],
                                        hid_t(hc)[:, tg * 512:(tg + 1) * 512],
                                        start=(hc == 0), stop=(hc == 31))
                                return last
                            tw2 = p.emit("tensor", w, wmm2, inc="ts")
                            sv = p.emit("vector", [("ts", tw2)], lambda v, ph_=ph_: v.tensor_copy(
                                out=SOFT2[:, 0:512] if ph_ is psH0 else SOFT2[:, 512:1024],
                                in_=ph_[:, :]), inc="vs")

                            def tr2(t, og=og, tg=tg, ph_=ph_):
                                src = SOFT2[:, 0:512] if ph_ is psH0 else SOFT2[:, 512:1024]
                                psf = psF1 if tg % 2 == 0 else psF2
                                last = None
                                for i in range(4):
                                    last = t.transpose(psf[:, i * 128:(i + 1) * 128],
                                                       src[:, i * 128:(i + 1) * 128], IDF[:, :])
                                return last
                            w3 = [("vs", sv)]
                            if (og * 2 + tg) >= 2:
                                w3.append(("vs", v_w2r[(og * 2 + tg) - 2][1]))
                            tt2 = p.emit("tensor", w3, tr2, inc="ts")

                            def radd(v, og=og, tg=tg, half=half):
                                psf = psF1 if tg % 2 == 0 else psF2
                                last = None
                                for i in range(4):
                                    tt = half * 8 + tg * 4 + i
                                    last = v.tensor_tensor(
                                        out=X[:, tt * D + og * 128: tt * D + (og + 1) * 128],
                                        in0=X[:, tt * D + og * 128: tt * D + (og + 1) * 128],
                                        in1=psf[:, i * 128:(i + 1) * 128], op=ALU.add)
                                return last
                            vr = p.emit("vector", [("ts", tt2)], radd, inc="vs")
                            v_w2r[og * 2 + tg] = (tw2, vr)
                            if half == 0 and og == 0 and tg == 0 and cuts("pc5w", SOFT2[:, 0:512]):
                                stP.close()
                                p.flush("pc5w")
                                return
                    if half == 0 and cuts("pc5a", X):
                        stP.close()
                        p.flush("pc5a")
                        return
                stP.close()
                p.flush("pc5_l%d" % l)
                if cuts("pc5", X):
                    return
            v_rd = {}
            v_x3 = {}
            for tt in range(NTT):
                slot = tt % 2
                sem = "dh0" if slot == 0 else "dh1"
                wd = [("vs", v_x3[tt - 2])] if tt >= 2 else []
                v_rd[tt] = p.dma("sync", wd, XTMPF[:, slot * 1024:(slot + 1) * 1024],
                                 spill.rearrange("(t p) d -> p t d", p=128)[:, tt], sem)
                v_x3[tt] = p.emit("vector", [(sem, v_rd[tt])], lambda v, tt=tt, slot=slot: (
                    v.drain(),
                    v.scalar_tensor_tensor(
                        out=X[:, tt * D:(tt + 1) * D], in0=XTMPF[:, slot * 1024:(slot + 1) * 1024],
                        scalar=1.0, in1=X[:, tt * D:(tt + 1) * D], op0=ALU.mult, op1=ALU.add,
                        accum_out=SM[:, SM_A + tt:SM_A + tt + 1]))[-1], inc="vs")
            p.emit("vector", [("vs", p.cnt["vs"])], lambda v: v.tensor_copy(
                out=SM[:, SM_SUM:SM_SUM + 16], in_=SM[:, SM_A:SM_A + 16]), inc="vs")
            stC.close()
            p.flush("pc_end")

        if nphases >= 3:
            phase_attnb(pccut)
            if "pc" in debug:
                dbg_list += [("pc_x", X)]

        if nphases >= 4:
            phase_ffnq("pd", accum_for_next=False)


    except _Cut:
        pass
    wfin = [(k, v) for k, v in p.cnt.items() if v > 0 and k != "dq"]
    p.emit("sync", wfin, lambda s: s.dma_start(
        out=y_out.rearrange("(t p) d -> p t d", p=128),
        in_=X[:, :].rearrange("p (t d) -> p t d", t=NTT)), inc="dq", incby=16)
    wdbg = [("vs", p.cnt["vs"]), ("ss", p.cnt["ss"]), ("ts", p.cnt["ts"])]
    for name, ten in dbg_list:
        ap = ten if isinstance(ten, bass.AP) else ten[:, :]
        o = nc.dram_tensor("dbg_" + name, list(ap.shape), ap.dtype, kind="ExternalOutput").ap()
        p.emit("sync", wdbg, lambda s, o=o, ap=ap: s.dma_start(out=o, in_=ap),
               inc="dq", incby=16)
        wdbg = []
    vq = p.cnt["dq"]
    if _os2.environ.get("NONOP"):
        p.emit("sync", [("dq", vq)], lambda s: None)
    else:
        p.emit("sync", [("dq", vq)], lambda s: s.nop())
    p.flush("fin")
    return p


def make_inputs(inputs):
    consts = host_consts()
    base = {
        "wqkv": inputs["L0_attn_a_w_qkv"], "wproj": inputs["L0_attn_a_w_proj"],
        "fw": inputs["L0_ffn_weight"], "fb": np.asarray(inputs["L0_ffn_bias"]).reshape(1, -1),
        "bln1g": inputs["L0_attn_b_ln1_g"], "bln1b": inputs["L0_attn_b_ln1_b"],
        "bln2g": inputs["L0_attn_b_ln2_g"], "bln2b": inputs["L0_attn_b_ln2_b"],
        "bwq": inputs["L0_attn_b_wq"], "bwk": inputs["L0_attn_b_wk"],
        "bwv": inputs["L0_attn_b_wv"], "bwo": inputs["L0_attn_b_wo"],
        "bbq": inputs["L0_attn_b_bq"], "bbk": inputs["L0_attn_b_bk"],
        "bbv": inputs["L0_attn_b_bv"],
        "bw1": inputs["L0_attn_b_w1"], "bb1": inputs["L0_attn_b_b1"],
        "bw2": inputs["L0_attn_b_w2"],
    }
    def cols(a, n):
        a = np.asarray(a, dtype=np.float32)
        return np.ascontiguousarray(a.reshape(2, n, 128).transpose(0, 2, 1))
    base["cln1g"] = cols(inputs["L0_attn_b_ln1_g"], 8)
    base["cln1b"] = cols(inputs["L0_attn_b_ln1_b"], 8)
    base["cln2g"] = cols(inputs["L0_attn_b_ln2_g"], 8)
    base["cln2b"] = cols(inputs["L0_attn_b_ln2_b"], 8)
    base["cbq"] = cols(inputs["L0_attn_b_bq"], 8)
    base["cbk"] = cols(inputs["L0_attn_b_bk"], 8)
    base["cb1"] = cols(inputs["L0_attn_b_b1"], 32)
    base = {k: np.ascontiguousarray(np.asarray(v, dtype=np.float32)) for k, v in base.items()}
    base.update(consts)
    x = np.asarray(inputs["L0_attn_a_x"], dtype=np.float32)
    return [dict(base, x=np.ascontiguousarray(x[b])) for b in range(x.shape[0])]


def kernel(**inputs):
    prog = build()
    in_maps = make_inputs(inputs)
    res = run_bass_kernel_spmd(prog.nc, in_maps, core_ids=list(range(8)))
    return np.stack([r["y"] for r in res.results], axis=0)


# version 04
# speedup vs baseline: 1.0200x; 1.0200x over previous
import math
import numpy as np
import ml_dtypes
from contextlib import ExitStack

import concourse.bass as bass
import concourse.mybir as mybir
from concourse.bass_utils import run_bass_kernel_spmd

F32 = mybir.dt.float32
BF16 = mybir.dt.bfloat16
AF = mybir.ActivationFunctionType
ALU = mybir.AluOpType
AX = mybir.AxisListType

T = 2048
D = 1024
NTT = 16
NCH = 8
NH = 16
MAGIC = 12582912.0
EPS = 1e-5
CSLACK = 8.0


def host_consts():
    bf = ml_dtypes.bfloat16
    c = {}
    c["C_IDB"] = np.eye(128, dtype=bf)
    c["C_IDF"] = np.eye(128, dtype=np.float32)
    g = np.zeros((128, NTT * 64), dtype=np.float32)
    for tt in range(NTT):
        for n in range(128):
            g[n, tt * 64 + 4 * tt + n // 32] = 1.0 / 32.0
    c["C_G"] = g.astype(bf)
    freqs = 2.0 ** np.linspace(0.0, math.log2(5.0), 16).astype(np.float32)
    pos = 1.0 + np.arange(T, dtype=np.float32)
    rad = pos[None, :] * freqs[:, None] * math.pi
    sinr, cosr = np.sin(rad).astype(np.float32), np.cos(rad).astype(np.float32)
    t2 = np.ones((128, T), dtype=np.float32)
    for d in range(32):
        alt = -1.0 if d % 2 == 0 else 1.0
        t2[d] = sinr[d % 16] + alt * cosr[d % 16]
        t2[64 + d] = t2[d]
    c["C_T2"] = t2
    e = np.zeros((128, 1024), dtype=np.float32)
    for pp in range(128):
        e[pp, pp % 64::64] = 1.0
    c["C_EYE"] = e
    c["C_ONEF"] = np.ones((128, 128), dtype=np.float32)
    c["C_ONEB"] = np.ones((128, 128), dtype=bf)
    return c


class Prog:
    def __init__(self):
        self.nc = bass.Bass("TRN2", target_bir_lowering=False, debug=False)
        self.es = ExitStack()
        self.sems = {}
        self.cnt = {}
        self.steps = []
        self._names = {}

    def mksem(self, *names):
        for n in names:
            self.sems[n] = self.es.enter_context(self.nc.semaphore(n))
            self.cnt[n] = 0

    def _uniq(self, name):
        n = self._names.get(name, 0)
        self._names[name] = n + 1
        return name if n == 0 else "%s_%d" % (name, n)

    def sb(self, name, shape, dt, stack=None):
        return (stack or self.es).enter_context(
            self.nc.sbuf_tensor(self._uniq(name), shape, dt))

    def ps(self, name, shape, dt, stack=None):
        return (stack or self.es).enter_context(
            self.nc.psum_tensor(self._uniq(name), shape, dt))

    def bump(self, name, by=1):
        self.cnt[name] += by
        return self.cnt[name]

    def emit(self, eng_name, waits, fn, inc=None, incby=1):
        post = self.bump(inc, incby) if inc else None
        semtab = self.sems

        def run(eng, waits=tuple(waits), fn=fn, inc=inc, incby=incby):
            for n, v in waits:
                eng.wait_ge(semtab[n], v)
            last = fn(eng)
            if inc is not None:
                assert last is not None, "step with inc returned no instruction"
                last.then_inc(semtab[inc], incby)
        self.steps.append((eng_name, run))
        return post

    def dma(self, eng_name, waits, out, in_, sem="dw"):
        pre = self.cnt[sem]
        post = self.bump(sem, 16)
        semtab = self.sems
        waits = list(waits)
        if pre > 0:
            waits.append((sem, pre))

        def run(eng, waits=tuple(waits)):
            for n, v in waits:
                eng.wait_ge(semtab[n], v)
            eng.dma_start(out=out, in_=in_).then_inc(semtab[sem], 16)
        self.steps.append((eng_name, run))
        return post

    def flush(self, name=None):
        by_eng = {}
        for e, fn in self.steps:
            by_eng.setdefault(e, []).append(fn)
        self.steps = []
        if not by_eng:
            return
        import os as _os
        with self.nc.Block(None if _os.environ.get("NOBLOCKNAME") else name) as block:
            for e, fns in by_eng.items():
                def body(eng, fns=fns):
                    for fn in fns:
                        fn(eng)
                getattr(block, e)(body)


class _Cut(Exception):
    pass


def build(nphases=99, debug=(), pccut=None, pacut=None):
    p = Prog()
    nc = p.nc
    dbg_list = []

    def pchk(tag):
        if pacut == tag:
            raise _Cut()

    def din(name, shape, dt=F32):
        return nc.dram_tensor(name, shape, dt, kind="ExternalInput").ap()

    x_in = din("x", [T, D])
    wqkv = din("wqkv", [D, 3 * D])
    wproj = din("wproj", [D, D])
    fw = din("fw", [D, D])
    fb = din("fb", [1, D])
    bln1g = din("bln1g", [2, D]); bln1b = din("bln1b", [2, D])
    bln2g = din("bln2g", [2, D]); bln2b = din("bln2b", [2, D])
    bwq = din("bwq", [2, D, D]); bwk = din("bwk", [2, D, D])
    bwv = din("bwv", [2, D, D]); bwo = din("bwo", [2, D, D])
    bbq = din("bbq", [2, D]); bbk = din("bbk", [2, D]); bbv = din("bbv", [2, D])
    bw1 = din("bw1", [2, D, 4 * D]); bb1 = din("bb1", [2, 4 * D])
    bw2 = din("bw2", [2, 4 * D, D])
    cln1g = din("cln1g", [2, 128, 8]); cln1b = din("cln1b", [2, 128, 8])
    cln2g = din("cln2g", [2, 128, 8]); cln2b = din("cln2b", [2, 128, 8])
    cbq = din("cbq", [2, 128, 8]); cbk = din("cbk", [2, 128, 8])
    cb1 = din("cb1", [2, 128, 32])
    c_idb = din("C_IDB", [128, 128], BF16)
    c_idf = din("C_IDF", [128, 128], F32)
    c_g = din("C_G", [128, NTT * 64], BF16)
    c_t2 = din("C_T2", [128, T], F32)
    c_eye = din("C_EYE", [128, 1024], F32)
    c_onef = din("C_ONEF", [128, 128], F32)
    c_oneb = din("C_ONEB", [128, 128], BF16)

    y_out = nc.dram_tensor("y", [T, D], F32, kind="ExternalOutput").ap()
    import os as _os
    if _os.environ.get("NOSPILL"):
        spill = nc.dram_tensor("spill", [T, D], F32, kind="ExternalInput").ap()
    else:
        spill = nc.dram_tensor("spill", [T, D], F32).ap()

    X = p.sb("X", [128, NTT * D], F32)
    XNFM = p.sb("XNFM", [128, NCH * T], BF16)
    BIG = p.sb("BIG", [128, 16384], BF16)
    WBUF = p.sb("WBUF", [128, 8192], BF16)
    IDB = p.sb("IDB", [128, 128], BF16)
    IDF = p.sb("IDF", [128, 128], F32)
    CG = p.sb("CG", [128, NTT * 64], BF16)
    CEYE = p.sb("CEYE", [128, 1024], F32)
    ONEF = p.sb("ONEF", [128, 128], F32)
    ONEB = p.sb("ONEB", [128, 128], BF16)
    SM = p.sb("SMALL", [128, 256], F32)
    CZCOL = p.sb("CZCOL", [128, 2], F32)
    XTMPF = p.sb("XTMPF", [128, 2048], F32)
    SOFT2 = p.sb("SOFT2", [128, 1024], F32)

    SM_SUM, SM_SQ, SM_MM, SM_VAR, SM_INV, SM_NMI = 0, 16, 32, 48, 64, 80
    SM_A, SM_AM, SM_RA, SM_R127, SM_XS, SM_MISC = 96, 128, 144, 160, 176, 192

    p.mksem("dx", "dw", "dwa", "dwb", "dwc", "dh0", "dh1", "dc", "dq", "ts", "vs", "ss", "gs")

    v_x = p.dma("sync", [], X[:, :].rearrange("p (t d) -> p t d", t=NTT),
                x_in.rearrange("(t p) d -> p t d", p=128), "dx")
    import os as _os2
    for dst, src in [(IDB, c_idb), (IDF, c_idf), (CG, c_g),
                     (CEYE, c_eye), (ONEF, c_onef), (ONEB, c_oneb)]:
        if not _os2.environ.get("NOCONST"):
            p.dma("sync", [], dst[:, :], src, "dc")
    v_consts = p.cnt["dc"]
    if not _os2.environ.get("NOMEM"):
        p.emit("vector", [], lambda v: (
            v.memset(CZCOL[:, 0:1], 0.0),
            v.memset(CZCOL[:, 1:2], EPS * 1024.0 * 1024.0))[-1], inc="vs")
    v_czcol = p.cnt["vs"]
    p.flush("p0")
    try:
        pchk("p0")

        def ln_stats(have_sums):
            for tt in range(NTT):
                if not have_sums:
                    w = [("dx", v_x)] if tt == 0 else []
                    if p.cnt["ss"] >= 1:
                        w.append(("ss", p.cnt["ss"]))
                    p.emit("scalar", w, lambda s, tt=tt: s.activation(
                        out=XTMPF[:, 0:1024], in_=X[:, tt * D:(tt + 1) * D], func=AF.Copy,
                        accum_out=SM[:, SM_SUM + tt:SM_SUM + tt + 1]), inc="ss")
                w = [("dx", v_x)] if (tt == 0 and have_sums) else []
                if p.cnt["ss"] >= 1:
                    w.append(("ss", p.cnt["ss"]))
                p.emit("scalar", w, lambda s, tt=tt: s.activation(
                    out=XTMPF[:, 1024:2048], in_=X[:, tt * D:(tt + 1) * D], func=AF.Square,
                    accum_out=SM[:, SM_SQ + tt:SM_SQ + tt + 1]), inc="ss")
            vss = p.cnt["ss"]

            def m1(v):
                v.tensor_tensor(out=SM[:, SM_MM:SM_MM + 16], in0=SM[:, SM_SUM:SM_SUM + 16],
                                in1=SM[:, SM_SUM:SM_SUM + 16], op=ALU.mult)
                v.drain()
                return v.scalar_tensor_tensor(out=SM[:, SM_VAR:SM_VAR + 16],
                                              in0=SM[:, SM_SQ:SM_SQ + 16], scalar=1024.0,
                                              in1=SM[:, SM_MM:SM_MM + 16],
                                              op0=ALU.mult, op1=ALU.subtract)
            vv = p.emit("vector", [("ss", vss)], m1, inc="vs")
            sv = p.emit("scalar", [("vs", vv)], lambda s: s.activation(
                out=SM[:, SM_MM:SM_MM + 16], in_=SM[:, SM_VAR:SM_VAR + 16], func=AF.Sqrt,
                bias=CZCOL[:, 1:2]), inc="ss")

            def m2(v):
                v.reciprocal(out=SM[:, SM_VAR:SM_VAR + 16], in_=SM[:, SM_MM:SM_MM + 16])
                v.drain()
                v.tensor_scalar(out=SM[:, SM_INV:SM_INV + 16], in0=SM[:, SM_VAR:SM_VAR + 16],
                                scalar1=1024.0, scalar2=None, op0=ALU.mult, op1=ALU.bypass)
                v.drain()
                return v.scalar_tensor_tensor(out=SM[:, SM_NMI:SM_NMI + 16],
                                              in0=SM[:, SM_SUM:SM_SUM + 16], scalar=-1.0 / 1024.0,
                                              in1=SM[:, SM_INV:SM_INV + 16],
                                              op0=ALU.mult, op1=ALU.mult)
            return p.emit("vector", [("ss", sv)], m2, inc="vs")

        stA = ExitStack()
        stA2 = ExitStack()
        QLB = p.sb("QLB", [128, 512], BF16, stA)
        KLB = p.sb("KLB", [128, 512], BF16, stA)
        XNLF = p.sb("XNLF", [128, 512], BF16, stA)
        K3VB = p.sb("K3VB", [128, 1040], BF16, stA)
        Z2A = p.sb("Z2A", [128, 520], BF16, stA)
        QBUF = p.sb("QBUF", [128, T], BF16, stA2)
        KBUF = p.sb("KBUF", [128, T], BF16, stA2)
        VBUF = p.sb("VBUF", [128, 2 * 1040], BF16, stA2)
        XTMPB = p.sb("XTMPB", [128, 2048], BF16, stA2)
        AE3 = p.sb("AE3", [128, 2048], BF16, stA2)
        XNL = p.sb("XNL", [128, 1024], BF16, stA2)
        NEGC = p.sb("NEGC", [128, 8], F32, stA2)

        pa1 = ExitStack()
        psT = p.ps("psT", [128, 1024], BF16, pa1)
        psT2 = p.ps("psT2", [128, 1024], BF16, pa1)
        psLM = p.ps("psLM", [64, 1024], F32, pa1)
        v_inv = ln_stats(have_sums=False)
        t_tp, v_ev = {}, {}
        for tt in range(NTT):
            slot = tt % 2
            w = [("vs", v_inv)]
            if tt >= 2:
                w.append(("ts", t_tp[tt - 2]))
            va = p.emit("vector", w, lambda v, tt=tt, slot=slot: v.tensor_scalar(
                out=XTMPB[:, slot * 1024:(slot + 1) * 1024], in0=X[:, tt * D:(tt + 1) * D],
                scalar1=SM[:, SM_INV + tt:SM_INV + tt + 1],
                scalar2=SM[:, SM_NMI + tt:SM_NMI + tt + 1],
                op0=ALU.mult, op1=ALU.add), inc="vs")

            def tfn(t, tt=tt, slot=slot):
                xn = XTMPB[:, slot * 1024:(slot + 1) * 1024]
                pst = psT if slot == 0 else psT2
                for c in range(NCH):
                    t.transpose(pst[:, c * 128:(c + 1) * 128], xn[:, c * 128:(c + 1) * 128], IDB[:, :])
                t.matmul(psLM[:, 0:512], CG[:, tt * 64:(tt + 1) * 64], xn[:, 0:512],
                         start=(tt == 0), stop=(tt == NTT - 1))
                return t.matmul(psLM[:, 512:1024], CG[:, tt * 64:(tt + 1) * 64], xn[:, 512:1024],
                                start=(tt == 0), stop=(tt == NTT - 1))
            w = [("vs", va), ("dc", v_consts)]
            if tt >= 2:
                w.append(("vs", v_ev[tt - 2]))
            t_tp[tt] = p.emit("tensor", w, tfn, inc="ts")
            v_ev[tt] = p.emit("vector", [("ts", t_tp[tt])], lambda v, slot=slot, tt=tt: v.tensor_copy(
                out=XNFM[:, :].rearrange("p (c t) -> p c t", c=NCH)[:, :, tt * 128:(tt + 1) * 128],
                in_=(psT if slot == 0 else psT2)[:, :].rearrange("p (c n) -> p c n", c=NCH)),
                inc="vs")
        s_xnl = p.emit("scalar", [("ts", t_tp[NTT - 1])], lambda s: s.activation(
            out=XNL[0:64, :], in_=psLM[:, :], func=AF.Copy), inc="ss")

        def t_xnl(t):
            last = None
            for c in range(NCH):
                last = t.transpose(psT[:, c * 64:(c + 1) * 64], XNL[0:64, c * 128:(c + 1) * 128],
                                   IDB[0:64, 0:64])
            return last
        tx = p.emit("tensor", [("ss", s_xnl), ("vs", v_ev[NTT - 1]), ("vs", v_ev[NTT - 2])],
                    t_xnl, inc="ts")
        v_xnlf = p.emit("vector", [("ts", tx)], lambda v: v.tensor_copy(
            out=XNLF[:, :], in_=psT[:, 0:512]), inc="vs")
        pa1.close()
        p.flush("pa1")
        pchk("pa1")
        if "pa1" in debug:
            dbg_list += [("pa1_xnfm", XNFM), ("pa1_xnlf", XNLF)]

        pa2 = ExitStack()
        ps0 = p.ps("ps0", [128, 512], F32, pa2)
        ps1 = p.ps("ps1", [128, 512], F32, pa2)
        ps2 = p.ps("ps2", [128, 512], F32, pa2)
        ps3 = p.ps("ps3", [128, 512], F32, pa2)
        psx = p.ps("psx", [128, 512], F32, pa2)
        psk = p.ps("psk", [128, 512], F32, pa2)
        psb0 = p.ps("psb0", [128, 1024], BF16, pa2)
        psb1 = p.ps("psb1", [128, 1024], BF16, pa2)

        def dma_qkv(hg, slot, waits):
            res = []
            for vi in range(3):
                sem = ("dwa", "dwb", "dwc")[vi]
                res.append((sem, p.dma("gpsimd", waits if vi == 0 else [],
                            WBUF[:, slot * 4096 + vi * 1024: slot * 4096 + (vi + 1) * 1024]
                            .rearrange("p (c n) -> p c n", c=NCH),
                            wqkv.rearrange("(c p) n -> p c n", p=128)
                            [:, :, vi * D + hg * 128: vi * D + (hg + 1) * 128], sem)))
            return res

        v_w = {0: dma_qkv(0, 0, [])}
        prev_tb = None
        v_vbuf_ones = None
        v_k3n_all = []
        hg_fence = []
        for hg in range(NCH):
            slot = hg % 2
            WQ = WBUF[:, slot * 4096: slot * 4096 + 1024]
            WK = WBUF[:, slot * 4096 + 1024: slot * 4096 + 2048]
            WV = WBUF[:, slot * 4096 + 2048: slot * 4096 + 3072]

            qk_ev = {}
            for tg in range(4):
                pq = ps0 if tg % 2 == 0 else ps1
                pk = ps2 if tg % 2 == 0 else ps3
                w = []
                if tg == 0:
                    w += list(v_w[hg]) + [("vs", v_xnlf)]
                    if hg_fence:
                        w += [("vs", hg_fence[-1][0]), ("ss", hg_fence[-1][1])]
                if tg >= 2:
                    w.append(("ss", qk_ev[tg - 2]))

                def mmqk(t, tg=tg, pq=pq, pk=pk, WQ=WQ, WK=WK):
                    for c in range(NCH):
                        t.matmul(pq[:, :], WQ[:, c * 128:(c + 1) * 128],
                                 XNFM[:, c * T + tg * 512: c * T + (tg + 1) * 512],
                                 start=(c == 0), stop=(c == NCH - 1))
                    last = None
                    for c in range(NCH):
                        last = t.matmul(pk[:, :], WK[:, c * 128:(c + 1) * 128],
                                        XNFM[:, c * T + tg * 512: c * T + (tg + 1) * 512],
                                        start=(c == 0), stop=(c == NCH - 1))
                    return last
                tq = p.emit("tensor", w, mmqk, inc="ts")
                qk_ev[tg] = p.emit("scalar", [("ts", tq)], lambda s, tg=tg, pq=pq, pk=pk: (
                    s.activation(out=QBUF[:, tg * 512:(tg + 1) * 512], in_=pq[:, :],
                                 func=AF.Copy, scale=0.125),
                    s.activation(out=KBUF[:, tg * 512:(tg + 1) * 512], in_=pk[:, :],
                                 func=AF.Copy))[-1], inc="ss")

            def mmql(t, WQ=WQ, WK=WK):
                for c in range(NCH):
                    t.matmul(psx[:, 0:64], WQ[:, c * 128:(c + 1) * 128], XNLF[:, c * 64:(c + 1) * 64],
                             start=(c == 0), stop=(c == NCH - 1))
                last = None
                for c in range(NCH):
                    last = t.matmul(psx[:, 64:128], WK[:, c * 128:(c + 1) * 128],
                                    XNLF[:, c * 64:(c + 1) * 64],
                                    start=(c == 0), stop=(c == NCH - 1))
                return last
            wql = list(v_w[hg]) + [("vs", v_xnlf)]
            if hg_fence:
                wql.append(("vs", hg_fence[-1][0]))
            tql = p.emit("tensor", wql, mmql, inc="ts")
            v_ql = p.emit("vector", [("ts", tql)], lambda v, hg=hg: (
                v.tensor_scalar(out=QLB[:, hg * 64:(hg + 1) * 64], in0=psx[:, 0:64],
                                scalar1=0.125, scalar2=None, op0=ALU.mult, op1=ALU.bypass),
                v.tensor_copy(out=KLB[:, hg * 64:(hg + 1) * 64], in_=psx[:, 64:128]))[-1], inc="vs")

            v_vev = {}
            for g in range(4):
                pv = psk if g % 2 == 0 else psx
                w = []
                if g == 0:
                    w += list(v_w[hg])
                    if hg_fence:
                        w.append(("vs", hg_fence[-1][0]))
                if g == 1:
                    w.append(("vs", v_ql))
                if g >= 2:
                    w.append(("vs", v_vev[g - 2]))

                def mmv(t, g=g, pv=pv, WV=WV):
                    last = None
                    for i in range(4):
                        tt = g * 4 + i
                        for c in range(NCH):
                            last = t.matmul(pv[:, i * 128:(i + 1) * 128],
                                            XNFM[:, c * T + tt * 128: c * T + (tt + 1) * 128],
                                            WV[:, c * 128:(c + 1) * 128],
                                            start=(c == 0), stop=(c == NCH - 1))
                    return last
                tv = p.emit("tensor", w, mmv, inc="ts")

                def vev(v, g=g, pv=pv):
                    v.tensor_copy(
                        out=VBUF[:, 0:1040].rearrange("p (t y) -> p t y", t=NTT)
                        [:, g * 4:(g + 1) * 4, 0:64],
                        in_=pv[:, :].rearrange("p (t n) -> p t n", t=4)[:, :, 0:64])
                    return v.tensor_copy(
                        out=VBUF[:, 1040:2080].rearrange("p (t y) -> p t y", t=NTT)
                        [:, g * 4:(g + 1) * 4, 0:64],
                        in_=pv[:, :].rearrange("p (t n) -> p t n", t=4)[:, :, 64:128])
                v_vev[g] = p.emit("vector", [("ts", tv)], vev, inc="vs")
            if hg == 0:
                v_vbuf_ones = p.emit("vector", [], lambda v: (
                    v.memset(VBUF[:, 64:1040:65], 1.0),
                    v.memset(VBUF[:, 1104:2080:65], 1.0))[-1], inc="vs")

            if hg + 1 < NCH:
                v_w[hg + 1] = dma_qkv(hg + 1, (hg + 1) % 2,
                                      [("ss", qk_ev[3]), ("vs", v_vev[3]), ("vs", v_ql)])
            if hg == 0:
                pchk("pa2a")

            base_w = [("ss", qk_ev[3]), ("vs", v_ql), ("vs", v_vev[2]), ("vs", v_vev[3])]
            tc = p.emit("tensor", base_w, lambda t, hg=hg: (
                t.matmul(ps0[:, 0:64], QBUF[0:64, 0:T:16],
                         KLB[0:64, hg * 64:(hg + 1) * 64], start=True, stop=True),
                t.matmul(ps1[:, 0:64], QBUF[64:128, 0:T:16],
                         KLB[64:128, hg * 64:(hg + 1) * 64], start=True, stop=True))[-1], inc="ts")
            vc = p.emit("vector", [("ts", tc)], lambda v: (
                v.reduce_max(out=SM[:, SM_MISC:SM_MISC + 1], in_=ps0[:, 0:64], axis=AX.X),
                v.reduce_max(out=SM[:, SM_MISC + 1:SM_MISC + 2], in_=ps1[:, 0:64], axis=AX.X),
                v.drain(),
                v.tensor_tensor(out=SM[:, SM_MISC:SM_MISC + 1], in0=SM[:, SM_MISC:SM_MISC + 1],
                                in1=SM[:, SM_MISC + 1:SM_MISC + 2], op=ALU.max))[-1], inc="vs")
            tc2 = p.emit("tensor", [("vs", vc)], lambda t: t.transpose(
                ps1[0:1, 0:128], SM[:, SM_MISC:SM_MISC + 1], IDF[:, :]), inc="ts")
            vc2 = p.emit("vector", [("ts", tc2)], lambda v: v.reduce_max(
                out=SM[0:1, SM_MISC + 1:SM_MISC + 2], in_=ps1[0:1, 0:128], axis=AX.X), inc="vs")
            tc3 = p.emit("tensor", [("vs", vc2)], lambda t: t.matmul(
                ps1[:, 128:129], ONEF[0:1, 0:128], SM[0:1, SM_MISC + 1:SM_MISC + 2],
                start=True, stop=True), inc="ts")
            s_c1 = p.emit("scalar", [("ts", tc3)], lambda s: s.activation(
                out=NEGC[:, 0:1], in_=ps1[:, 128:129], func=AF.Copy, scale=-1.0, bias=-CSLACK),
                inc="ss")

            if hg == 0:
                pchk("pa2b")
            e1_ev = {}
            for qg in range(4):
                pse = ps2 if qg % 2 == 0 else ps3
                pso = psx if qg % 2 == 0 else psk
                w = [("ss", s_c1), ("vs", v_vev[2]), ("vs", v_vev[3])] if qg == 0 else []
                if qg >= 2:
                    w.append(("ss", e1_ev[qg - 2]))

                def s1mm(t, qg=qg, pse=pse, pso=pso, hg=hg):
                    last = None
                    for i in range(4):
                        qt = qg * 4 + i
                        t.matmul(pse[:, i * 64:(i + 1) * 64],
                                 QBUF[0:64, qt * 128:(qt + 1) * 128],
                                 KLB[0:64, hg * 64:(hg + 1) * 64], start=True, stop=True)
                        last = t.matmul(pso[:, i * 64:(i + 1) * 64],
                                        QBUF[64:128, qt * 128:(qt + 1) * 128],
                                        KLB[64:128, hg * 64:(hg + 1) * 64], start=True, stop=True)
                    return last
                ts1 = p.emit("tensor", w, s1mm, inc="ts")
                e1_ev[qg] = p.emit("scalar", [("ts", ts1)], lambda s, qg=qg, pse=pse, pso=pso: (
                    s.activation(out=XTMPB[:, qg * 512:(qg + 1) * 512]
                                 .rearrange("p (i n) -> p i n", i=4)[:, :, 0:64],
                                 in_=pse[:, 0:256].rearrange("p (i n) -> p i n", i=4),
                                 func=AF.Exp, bias=NEGC[:, 0:1]),
                    s.activation(out=XTMPB[:, qg * 512:(qg + 1) * 512]
                                 .rearrange("p (i n) -> p i n", i=4)[:, :, 64:128],
                                 in_=pso[:, 0:256].rearrange("p (i n) -> p i n", i=4),
                                 func=AF.Exp, bias=NEGC[:, 0:1]))[-1], inc="ss")
            tb_ev = {}
            for half in range(2):
                psb = psb0 if half == 0 else psb1
                w = [("ss", e1_ev[half * 2 + 1])]
                if prev_tb is not None:
                    w.append(("vs", prev_tb[half]))

                def e1tr(t, half=half, psb=psb):
                    last = None
                    for i in range(8):
                        qt = half * 8 + i
                        last = t.transpose(psb[:, i * 128:(i + 1) * 128],
                                           XTMPB[:, qt * 128:(qt + 1) * 128], IDB[:, :])
                    return last
                tt1 = p.emit("tensor", w, e1tr, inc="ts")
                tb_ev[half] = p.emit("vector", [("ts", tt1)],
                                     lambda v, half=half, psb=psb, hg=hg: v.tensor_copy(
                    out=BIG[:, hg * 2048 + half * 1024: hg * 2048 + (half + 1) * 1024],
                    in_=psb[:, :]), inc="vs")
            prev_tb = tb_ev

            if hg == 0:
                pchk("pa2c")
            tc = p.emit("tensor", [("ss", s_c1)], lambda t, hg=hg: (
                t.matmul(ps0[0:64, 128:256], QLB[0:64, hg * 64:(hg + 1) * 64],
                         KBUF[0:64, 0:T:16], start=True, stop=True),
                t.matmul(ps0[64:128, 128:256], QLB[64:128, hg * 64:(hg + 1) * 64],
                         KBUF[64:128, 0:T:16], start=True, stop=True))[-1], inc="ts")
            vc = p.emit("vector", [("ts", tc)], lambda v: v.reduce_max(
                out=SM[:, SM_MISC:SM_MISC + 1], in_=ps0[:, 128:256], axis=AX.X), inc="vs")
            tc2 = p.emit("tensor", [("vs", vc)], lambda t: t.transpose(
                ps1[0:1, 0:128], SM[:, SM_MISC:SM_MISC + 1], IDF[:, :]), inc="ts")
            vc2 = p.emit("vector", [("ts", tc2)], lambda v: v.reduce_max(
                out=SM[0:1, SM_MISC + 1:SM_MISC + 2], in_=ps1[0:1, 0:128], axis=AX.X), inc="vs")
            tc3 = p.emit("tensor", [("vs", vc2)], lambda t: t.matmul(
                ps1[:, 128:129], ONEF[0:1, 0:128], SM[0:1, SM_MISC + 1:SM_MISC + 2],
                start=True, stop=True), inc="ts")
            s_c3 = p.emit("scalar", [("ts", tc3)], lambda s: s.activation(
                out=NEGC[:, 1:2], in_=ps1[:, 128:129], func=AF.Copy, scale=-1.0, bias=-CSLACK),
                inc="ss")

            e3_ev = {}
            for ng in range(4):
                pse = ps2 if ng % 2 == 0 else ps3
                pso = psx if ng % 2 == 0 else psk
                w = [("ss", s_c3), ("ss", e1_ev[3])] if ng == 0 else []
                if ng >= 2:
                    w.append(("ss", e3_ev[ng - 2]))

                def s3mm(t, ng=ng, pse=pse, pso=pso, hg=hg):
                    last = None
                    for i in range(4):
                        nb = ng * 4 + i
                        t.matmul(pse[:, i * 64:(i + 1) * 64],
                                 KBUF[0:64, nb * 128:(nb + 1) * 128],
                                 QLB[0:64, hg * 64:(hg + 1) * 64], start=True, stop=True)
                        last = t.matmul(pso[:, i * 64:(i + 1) * 64],
                                        KBUF[64:128, nb * 128:(nb + 1) * 128],
                                        QLB[64:128, hg * 64:(hg + 1) * 64], start=True, stop=True)
                    return last
                ts3 = p.emit("tensor", w, s3mm, inc="ts")
                e3_ev[ng] = p.emit("scalar", [("ts", ts3)], lambda s, ng=ng, pse=pse, pso=pso: (
                    s.activation(out=AE3[:, ng * 512:(ng + 1) * 512]
                                 .rearrange("p (i n) -> p i n", i=4)[:, :, 0:64],
                                 in_=pse[:, 0:256].rearrange("p (i n) -> p i n", i=4),
                                 func=AF.Exp, bias=NEGC[:, 1:2]),
                    s.activation(out=AE3[:, ng * 512:(ng + 1) * 512]
                                 .rearrange("p (i n) -> p i n", i=4)[:, :, 64:128],
                                 in_=pso[:, 0:256].rearrange("p (i n) -> p i n", i=4),
                                 func=AF.Exp, bias=NEGC[:, 1:2]))[-1], inc="ss")

            if hg == 0:
                pchk("pa2d")
            def k3vmm(t, hg=hg):
                last = None
                for hh in range(2):
                    for nb in range(NTT):
                        last = t.matmul(psk[0:64, hh * 65:(hh + 1) * 65],
                                        AE3[:, nb * 128 + hh * 64: nb * 128 + hh * 64 + 64],
                                        VBUF[:, hh * 1040 + nb * 65: hh * 1040 + (nb + 1) * 65],
                                        start=(nb == 0), stop=(nb == NTT - 1))
                return last
            tk3 = p.emit("tensor", [("ss", e3_ev[3]), ("vs", v_vev[2]), ("vs", v_vev[3]),
                                    ("vs", v_vbuf_ones)], k3vmm, inc="ts")

            def k3norm(v, hg=hg):
                v.reciprocal(out=SM[0:64, SM_MISC + 4:SM_MISC + 5], in_=psk[0:64, 64:65])
                v.reciprocal(out=SM[0:64, SM_MISC + 5:SM_MISC + 6], in_=psk[0:64, 129:130])
                v.drain()
                v.tensor_scalar(out=K3VB[0:64, (2 * hg) * 65:(2 * hg + 1) * 65],
                                in0=psk[0:64, 0:65],
                                scalar1=SM[0:64, SM_MISC + 4:SM_MISC + 5], scalar2=None,
                                op0=ALU.mult, op1=ALU.bypass)
                return v.tensor_scalar(out=K3VB[0:64, (2 * hg + 1) * 65:(2 * hg + 2) * 65],
                                       in0=psk[0:64, 65:130],
                                       scalar1=SM[0:64, SM_MISC + 5:SM_MISC + 6], scalar2=None,
                                       op0=ALU.mult, op1=ALU.bypass)
            v_k3n_all.append(p.emit("vector", [("ts", tk3)], k3norm, inc="vs"))
            hg_fence.append((p.cnt["vs"], p.cnt["ss"]))
        v_k3vb = p.cnt["vs"]
        pa2.close()
        p.flush("pa2")
        pchk("pa2")
        if "pa2" in debug:
            dbg_list += [("pa2_qbuf", QBUF), ("pa2_kbuf", KBUF), ("pa2_vbuf", VBUF),
                         ("pa2_qlb", QLB), ("pa2_klb", KLB), ("pa2_k3vb", K3VB),
                         ("pa2_big", BIG), ("pa2_negc", NEGC)]

        stA2.close()
        pa3 = ExitStack()
        SLAB = p.sb("SLAB", [64, 8192], F32, pa3)
        sK2, sKT, sA, sAT, sT1, sT2, sVN, sVNT = [SLAB[0:64, i * 1024:(i + 1) * 1024]
                                                  for i in range(8)]
        VNTB = p.sb("VNTB", [128, 1024], BF16, pa3)
        ZTMPB = p.sb("ZTMPB", [128, 520], BF16, pa3)
        psI1 = p.ps("psI1", [64, 1024], F32, pa3)
        psI2 = p.ps("psI2", [64, 1024], F32, pa3)
        psI3 = p.ps("psI3", [64, 1024], F32, pa3)
        psI4 = p.ps("psI4", [64, 1024], F32, pa3)
        E2B = XTMPF[0:64, 0:1024]

        def mcol(h):
            return SM_MISC + (h % 2) * 8 + h // 2

        def k2mm(t):
            last = None
            for h in range(NH):
                r0 = (h % 2) * 64
                hg = h // 2
                pb = psI1 if h % 2 == 0 else psI2
                last = t.matmul(pb[:, hg * 64:(hg + 1) * 64],
                                QLB[r0:r0 + 64, hg * 64:(hg + 1) * 64],
                                KLB[r0:r0 + 64, hg * 64:(hg + 1) * 64], start=True, stop=True)
            return last
        tk2 = p.emit("tensor", [("vs", v_k3vb)], k2mm, inc="ts")
        v_mx = p.emit("vector", [("ts", tk2)], lambda v: (
            v.reduce_max(out=SM[0:64, SM_MISC:SM_MISC + 8],
                         in_=psI1[:, 0:512].rearrange("p (g n) -> p g n", g=8),
                         axis=AX.X, negate=True),
            v.reduce_max(out=SM[0:64, SM_MISC + 8:SM_MISC + 16],
                         in_=psI2[:, 0:512].rearrange("p (g n) -> p g n", g=8),
                         axis=AX.X, negate=True))[-1], inc="vs")
        s_e2 = None
        for h in range(NH):
            pb2 = psI1 if h % 2 == 0 else psI2
            s_e2 = p.emit("scalar", [("vs", v_mx)], lambda s, h=h, pb2=pb2: s.activation(
                out=E2B[:, h * 64:(h + 1) * 64],
                in_=pb2[:, (h // 2) * 64:(h // 2 + 1) * 64], func=AF.Exp,
                bias=SM[0:64, mcol(h):mcol(h) + 1]), inc="ss")
        pchk("pa3a2")
        v_k2n = p.emit("vector", [("ss", s_e2)], lambda v: (
            v.reduce_sum(out=SM[0:64, SM_MISC + 16:SM_MISC + 32],
                         in_=E2B[:, :].rearrange("p (g n) -> p g n", g=16), axis=AX.X),
            v.drain(),
            v.reciprocal(out=SM[0:64, SM_MISC + 32:SM_MISC + 48],
                         in_=SM[0:64, SM_MISC + 16:SM_MISC + 32]))[-1], inc="vs")
        v_k2f = None
        for h in range(NH):
            v_k2f = p.emit("vector", [("vs", v_k2n)], lambda v, h=h: v.tensor_scalar(
                out=sK2[:, h * 64:(h + 1) * 64], in0=E2B[:, h * 64:(h + 1) * 64],
                scalar1=SM[0:64, SM_MISC + 32 + h:SM_MISC + 33 + h], scalar2=None,
                op0=ALU.mult, op1=ALU.bypass), inc="vs")

        pchk("pa3a3")
        def k2tr(t):
            last = None
            for h in range(NH):
                last = t.transpose(psI2[:, h * 64:(h + 1) * 64], sK2[:, h * 64:(h + 1) * 64],
                                   IDF[0:64, 0:64])
            return last
        tkt = p.emit("tensor", [("vs", v_k2f)], k2tr, inc="ts")
        v_kt = p.emit("vector", [("ts", tkt)], lambda v: v.tensor_copy(
            out=sKT[:, :], in_=psI2[:, :]), inc="vs")

        pchk("pa3a")
        def csmm(t):
            last = None
            for h in range(NH):
                last = t.matmul(psI3[0:1, h * 64:(h + 1) * 64], ONEF[0:64, 0:1],
                                sK2[:, h * 64:(h + 1) * 64], start=True, stop=True)
            return last
        tcs = p.emit("tensor", [("vs", v_kt)], csmm, inc="ts")
        v_cs = p.emit("vector", [("ts", tcs)], lambda v: (
            v.reduce_max(out=SM[0:1, SM_MISC + 48:SM_MISC + 64],
                         in_=psI3[0:1, :].rearrange("p (g n) -> p g n", g=16), axis=AX.X),
            v.drain(),
            v.reciprocal(out=SM[0:1, SM_MISC:SM_MISC + 16],
                         in_=SM[0:1, SM_MISC + 48:SM_MISC + 64]))[-1], inc="vs")

        def rdmm(t):
            last = None
            for h in range(NH):
                last = t.matmul(psI4[:, h:h + 1], ONEF[0:1, 0:64],
                                SM[0:1, SM_MISC + h:SM_MISC + h + 1], start=True, stop=True)
            return last
        trd = p.emit("tensor", [("vs", v_cs)], rdmm, inc="ts")
        v_rd = p.emit("vector", [("ts", trd)], lambda v: v.tensor_copy(
            out=SM[0:64, SM_MISC + 48:SM_MISC + 64], in_=psI4[:, 0:16]), inc="vs")
        v_v0 = None
        for h in range(NH):
            v_v0 = p.emit("vector", [("vs", v_rd)], lambda v, h=h: (
                v.tensor_scalar(out=sVN[:, h * 64:(h + 1) * 64], in0=sKT[:, h * 64:(h + 1) * 64],
                                scalar1=SM[0:64, SM_MISC + 48 + h:SM_MISC + 49 + h], scalar2=None,
                                op0=ALU.mult, op1=ALU.bypass),
                v.tensor_scalar(out=sVNT[:, h * 64:(h + 1) * 64], in0=sK2[:, h * 64:(h + 1) * 64],
                                scalar1=SM[0:64, SM_MISC + 48 + h:SM_MISC + 49 + h], scalar2=None,
                                op0=ALU.mult, op1=ALU.bypass))[-1], inc="vs")

        pchk("pa3b")
        def allmm(t, pso, lhs, rhs):
            last = None
            for h in range(NH):
                last = t.matmul(pso[:, h * 64:(h + 1) * 64], lhs[:, h * 64:(h + 1) * 64],
                                rhs[:, h * 64:(h + 1) * 64], start=True, stop=True)
            return last
        v_prev = v_v0
        for it in range(6):
            t1 = p.emit("tensor", [("vs", v_prev)], lambda t: (
                allmm(t, psI1, sKT, sVN), allmm(t, psI2, sVN, sKT))[-1], inc="ts")
            v1 = p.emit("vector", [("ts", t1)], lambda v: (
                v.tensor_copy(out=sA[:, :], in_=psI1[:, :]),
                v.tensor_copy(out=sAT[:, :], in_=psI2[:, :]),
                v.scalar_tensor_tensor(out=sT1[:, :], in0=CEYE[0:64, :], scalar=7.0,
                                       in1=psI1[:, :], op0=ALU.mult, op1=ALU.subtract),
                v.scalar_tensor_tensor(out=sT2[:, :], in0=CEYE[0:64, :], scalar=7.0,
                                       in1=psI2[:, :], op0=ALU.mult, op1=ALU.subtract))[-1], inc="vs")
            t2 = p.emit("tensor", [("vs", v1)], lambda t: (
                allmm(t, psI3, sAT, sT1), allmm(t, psI4, sT1, sAT))[-1], inc="ts")
            v2 = p.emit("vector", [("ts", t2)], lambda v: (
                v.scalar_tensor_tensor(out=sT1[:, :], in0=CEYE[0:64, :], scalar=15.0,
                                       in1=psI3[:, :], op0=ALU.mult, op1=ALU.subtract),
                v.scalar_tensor_tensor(out=sT2[:, :], in0=CEYE[0:64, :], scalar=15.0,
                                       in1=psI4[:, :], op0=ALU.mult, op1=ALU.subtract))[-1], inc="vs")
            t3 = p.emit("tensor", [("vs", v2)], lambda t: (
                allmm(t, psI1, sAT, sT1), allmm(t, psI2, sT1, sAT))[-1], inc="ts")
            v3 = p.emit("vector", [("ts", t3)], lambda v: (
                v.scalar_tensor_tensor(out=sT1[:, :], in0=CEYE[0:64, :], scalar=13.0,
                                       in1=psI1[:, :], op0=ALU.mult, op1=ALU.subtract),
                v.scalar_tensor_tensor(out=sT2[:, :], in0=CEYE[0:64, :], scalar=13.0,
                                       in1=psI2[:, :], op0=ALU.mult, op1=ALU.subtract))[-1], inc="vs")
            t4 = p.emit("tensor", [("vs", v3)], lambda t: (
                allmm(t, psI3, sVNT, sT1), allmm(t, psI4, sT1, sVNT))[-1], inc="ts")
            v_prev = p.emit("vector", [("ts", t4)], lambda v: (
                v.tensor_scalar(out=sVN[:, :], in0=psI3[:, :], scalar1=0.25, scalar2=None,
                                op0=ALU.mult, op1=ALU.bypass),
                v.tensor_scalar(out=sVNT[:, :], in0=psI4[:, :], scalar1=0.25, scalar2=None,
                                op0=ALU.mult, op1=ALU.bypass))[-1], inc="vs")

        pchk("pa3c")
        v_vntb = p.emit("vector", [("vs", v_prev)], lambda v: v.tensor_copy(
            out=VNTB[0:64, :], in_=sVNT[:, :]), inc="vs")

        def z2mm(t):
            last = None
            for h in range(NH):
                pso = psI1 if h % 2 == 0 else psI2
                last = t.matmul(pso[:, (h // 2) * 64:(h // 2 + 1) * 64],
                                VNTB[0:64, h * 64:(h + 1) * 64],
                                K3VB[0:64, h * 65:h * 65 + 64], start=True, stop=True)
            return last
        tz2 = p.emit("tensor", [("vs", v_vntb)], z2mm, inc="ts")
        v_z2e = p.emit("vector", [("ts", tz2)], lambda v: (
            v.memset(Z2A[:, :], 0.0),
            v.memset(ZTMPB[:, :], 0.0),
            v.drain(),
            v.tensor_copy(out=ZTMPB[0:64, :].rearrange("p (g y) -> p g y", g=NCH)[:, :, 0:64],
                          in_=psI2[:, 0:512].rearrange("p (g n) -> p g n", g=NCH)),
            v.tensor_copy(out=Z2A[0:64, :].rearrange("p (g y) -> p g y", g=NCH)[:, :, 0:64],
                          in_=psI1[:, 0:512].rearrange("p (g n) -> p g n", g=NCH)))[-1], inc="vs")
        v_z2d = p.dma("sync", [("vs", v_z2e)], Z2A[64:128, :], ZTMPB[0:64, :], "dq")
        v_ones = p.emit("vector", [("dq", v_z2d)], lambda v: v.memset(
            Z2A[:, 64:520:65], 1.0), inc="vs")
        pa3.close()
        p.flush("pa3")
        pchk("pa3")
        if "pa3" in debug:
            dbg_list += [("pa3_z2a", Z2A)]

        pa4 = ExitStack()
        SOFT = p.sb("SOFT", [128, 2048], F32, pa4)
        psSV = p.ps("psSV", [128, 2048], F32, pa4)
        psN = p.ps("psN", [128, 512], F32, pa4)
        psN2 = p.ps("psN2", [128, 512], F32, pa4)
        s_soft_prev = None
        v_norm = {}
        for h in range(NH):
            r0 = (h % 2) * 64
            hg = h // 2

            def svmm(t, r0=r0, hg=hg):
                last = None
                for qg in range(4):
                    last = t.matmul(psSV[0:65, qg * 512:(qg + 1) * 512],
                                    Z2A[r0:r0 + 64, hg * 65:(hg + 1) * 65],
                                    BIG[r0:r0 + 64, hg * 2048 + qg * 512: hg * 2048 + (qg + 1) * 512],
                                    start=True, stop=True)
                return last
            w = [("vs", v_ones)] if h == 0 else [("ss", s_soft_prev)]
            tsv = p.emit("tensor", w, svmm, inc="ts")
            s_soft_prev = p.emit("scalar", [("ts", tsv)], lambda s: s.activation(
                out=SOFT[0:65, :], in_=psSV[0:65, :], func=AF.Copy), inc="ss")
            for qt in range(NTT):
                psn = psN if qt % 2 == 0 else psN2
                w = [("ss", s_soft_prev)]
                idx0 = h * NTT + qt
                if idx0 >= 2:
                    w.append(("vs", v_norm[idx0 - 2]))
                tn = p.emit("tensor", w, lambda t, qt=qt, psn=psn: t.transpose(
                    psn[:, 0:65], SOFT[0:65, qt * 128:(qt + 1) * 128],
                    IDF[0:65, 0:65]), inc="ts")

                def nrm(v, qt=qt, psn=psn, h=h):
                    mc = SM_MISC + (qt % 4)
                    v.reciprocal(out=SM[:, mc:mc + 1], in_=psn[:, 64:65])
                    v.drain()
                    return v.tensor_scalar(out=XNFM[:, qt * 1024 + h * 64: qt * 1024 + (h + 1) * 64],
                                           in0=psn[:, 0:64],
                                           scalar1=SM[:, mc:mc + 1], scalar2=None,
                                           op0=ALU.mult, op1=ALU.bypass)
                v_norm[h * NTT + qt] = p.emit("vector", [("ts", tn)], nrm, inc="vs")
        v_svtm = p.cnt["vs"]
        pa4.close()
        p.flush("pa4")
        pchk("pa4")
        if "pa4" in debug:
            dbg_list += [("pa4_svtm", XNFM)]

        pa5 = ExitStack()
        psb0b = p.ps("psb0b", [128, 1024], BF16, pa5)
        psb1b = p.ps("psb1b", [128, 1024], BF16, pa5)
        psP = [p.ps("psP%d" % i, [128, 512], F32, pa5) for i in range(4)]
        v_wp = p.dma("gpsimd", [], WBUF[:, :].rearrange("p (c n) -> p c n", c=NCH),
                     wproj.rearrange("(c p) n -> p c n", p=128), "dw")
        t_tr2, v_ev2 = {}, {}
        for tt in range(NTT):
            psb = psb0b if tt % 2 == 0 else psb1b

            def trfn(t, tt=tt, psb=psb):
                last = None
                for c in range(NCH):
                    last = t.transpose(psb[:, c * 128:(c + 1) * 128],
                                       XNFM[:, tt * 1024 + c * 128: tt * 1024 + (c + 1) * 128],
                                       IDB[:, :])
                return last
            w = [("vs", v_svtm)] if tt == 0 else []
            if tt >= 2:
                w.append(("vs", v_ev2[tt - 2]))
            t_tr2[tt] = p.emit("tensor", w, trfn, inc="ts")
            v_ev2[tt] = p.emit("vector", [("ts", t_tr2[tt])], lambda v, tt=tt, psb=psb: v.tensor_copy(
                out=BIG[:, :].rearrange("p (c t) -> p c t", c=NCH)[:, :, tt * 128:(tt + 1) * 128],
                in_=psb[:, :].rearrange("p (c n) -> p c n", c=NCH)), inc="vs")
        v_pr = {}
        for tt in range(NTT):
            for og in range(2):
                k = tt * 2 + og
                pp = psP[k % 4]
                w = [("vs", v_ev2[tt])]
                if k == 0:
                    w.append(("dw", v_wp))
                if k >= 4:
                    w.append(("vs", v_pr[k - 4]))

                def pmm(t, tt=tt, og=og, pp=pp):
                    last = None
                    for c in range(NCH):
                        last = t.matmul(pp[:, :], BIG[:, c * T + tt * 128: c * T + (tt + 1) * 128],
                                        WBUF[:, c * 1024 + og * 512: c * 1024 + (og + 1) * 512],
                                        start=(c == 0), stop=(c == NCH - 1))
                    return last
                tp = p.emit("tensor", w, pmm, inc="ts")
                v_pr[k] = p.emit("vector", [("ts", tp)], lambda v, tt=tt, og=og, pp=pp, k=k: (
                                 v.drain(),
                                 v.scalar_tensor_tensor(
                    out=X[:, tt * D + og * 512: tt * D + (og + 1) * 512], in0=pp[:, :],
                    scalar=1.0, in1=X[:, tt * D + og * 512: tt * D + (og + 1) * 512],
                    op0=ALU.mult, op1=ALU.add,
                    accum_out=SM[:, SM_A + k:SM_A + k + 1]))[-1], inc="vs")
        v_x1 = p.cnt["vs"]
        pa5.close()
        stA.close()
        p.emit("vector", [("vs", v_x1)], lambda v: v.tensor_tensor(
            out=SM[:, SM_SUM:SM_SUM + 16], in0=SM[:, SM_A:SM_A + 32:2],
            in1=SM[:, SM_A + 1:SM_A + 32:2], op=ALU.add), inc="vs")
        p.flush("pa5")
        pchk("pa5")
        if "pa5" in debug:
            dbg_list += [("pa5_x", X)]

        def phase_ffnq(tag, accum_for_next):
            st = ExitStack()
            WQT = p.sb(tag + "WQT", [128, 8192], BF16, st)
            XQB = p.sb(tag + "XQB", [128, 2048], BF16, st)
            WSB = p.sb(tag + "WSB", [128, 1024], F32, st)
            BB = p.sb(tag + "BB", [128, 1024], F32, st)
            ROWS = p.sb(tag + "ROWS", [128, 16], F32, st)
            ps0 = p.ps(tag + "ps0", [128, 512], F32, st)
            ps1 = p.ps(tag + "ps1", [128, 512], F32, st)
            ps2 = p.ps(tag + "ps2", [128, 512], F32, st)
            ps3 = p.ps(tag + "ps3", [128, 512], F32, st)
            psq = p.ps(tag + "psq", [128, 1024], BF16, st)
            psq2 = p.ps(tag + "psq2", [128, 1024], BF16, st)
            stq = ExitStack()
            FP32S = p.sb(tag + "FP32S", [128, 2048], F32, stq)

            v_fw = {0: p.dma("sync", [], FP32S[:, 0:1024], fw[0:128, :], "dh0")}
            v_amax, t_wq = {}, {}
            for c in range(NCH):
                slot = c % 2
                if c + 1 < NCH:
                    wd = [("vs", v_amax[c - 1])] if c >= 1 else []
                    v_fw[c + 1] = p.dma("sync", wd, FP32S[:, (1 - slot) * 1024:(2 - slot) * 1024],
                                        fw[(c + 1) * 128:(c + 2) * 128, :], "dh0" if (c + 1) % 2 == 0 else "dh1")
                src = FP32S[:, slot * 1024:(slot + 1) * 1024]
                w = [("dh0" if c % 2 == 0 else "dh1", v_fw[c])]
                if c >= 2:
                    w.append(("ts", t_wq[c - 2]))
                v_amax[c] = p.emit("vector", w, lambda v, c=c, src=src: (
                    v.reduce_max(out=ROWS[:, c:c + 1], in_=src, axis=AX.X,
                                 apply_absolute_value=True),
                    v.drain(),
                    v.reciprocal(out=SM[:, SM_MISC + (c % 2):SM_MISC + (c % 2) + 1],
                                 in_=ROWS[:, c:c + 1]),
                    v.drain(),
                    v.tensor_scalar(out=ROWS[:, 8 + c:9 + c],
                                    in0=SM[:, SM_MISC + (c % 2):SM_MISC + (c % 2) + 1],
                                    scalar1=127.0, scalar2=None, op0=ALU.mult, op1=ALU.bypass),
                    v.drain(),
                    v.tensor_scalar(out=XTMPF[:, 0:1024], in0=src,
                                    scalar1=ROWS[:, 8 + c:9 + c], scalar2=MAGIC,
                                    op0=ALU.mult, op1=ALU.add),
                    v.drain(),
                    v.tensor_scalar(out=XQB[:, 0:1024], in0=XTMPF[:, 0:1024],
                                    scalar1=-MAGIC, scalar2=None, op0=ALU.add, op1=ALU.bypass)
                    )[-1], inc="vs")
                tq = p.emit("tensor", [("vs", v_amax[c]), ("dc", v_consts)],
                            lambda t, slot=slot: [
                    t.transpose((psq if slot == 0 else psq2)[:, j * 128:(j + 1) * 128],
                                XQB[:, j * 128:(j + 1) * 128], IDB[:, :])
                    for j in range(NCH)][-1], inc="ts")
                t_wq[c] = tq
                p.emit("vector", [("ts", tq)], lambda v, c=c, slot=slot: v.tensor_copy(
                    out=WQT[:, :].rearrange("p (j n) -> p j n", j=NCH)[:, :, c * 128:(c + 1) * 128],
                    in_=(psq if slot == 0 else psq2)[:, :].rearrange("p (j n) -> p j n", j=NCH)),
                    inc="vs")
            v_wqt = p.cnt["vs"]
            stq.close()
            WSR = p.sb(tag + "WSR", [1, 2048], F32, st)
            def wsrow(t):
                last = None
                for c in range(NCH):
                    pb = ps2 if c < 4 else ps3
                    last = t.transpose(pb[0:1, (c % 4) * 128:((c % 4) + 1) * 128],
                                       ROWS[:, c:c + 1], IDF[:, :])
                return last
            tws = p.emit("tensor", [("vs", v_amax[NCH - 1])], wsrow, inc="ts")
            v_wsr = p.emit("vector", [("ts", tws)], lambda v: (
                v.tensor_copy(out=WSR[0:1, 0:512], in_=ps2[0:1, 0:512]),
                v.tensor_copy(out=WSR[0:1, 512:1024], in_=ps3[0:1, 0:512]))[-1], inc="vs")
            v_fb = p.dma("sync", [("dh0", v_fw[6]), ("dh1", v_fw[7]), ("vs", v_amax[NCH - 1])], WSR[0:1, 1024:2048], fb, "dc")

            def bcasts(t):
                t.matmul(ps0[:, :], ONEF[0:1, 0:128], WSR[0:1, 0:512], start=True, stop=True)
                t.matmul(ps1[:, :], ONEF[0:1, 0:128], WSR[0:1, 512:1024], start=True, stop=True)
                t.matmul(ps2[:, :], ONEF[0:1, 0:128], WSR[0:1, 1024:1536], start=True, stop=True)
                return t.matmul(ps3[:, :], ONEF[0:1, 0:128], WSR[0:1, 1536:2048],
                                start=True, stop=True)
            tbc = p.emit("tensor", [("vs", v_wsr), ("dc", v_fb)], bcasts, inc="ts")
            v_bc = p.emit("vector", [("ts", tbc)], lambda v: (
                v.tensor_scalar(out=WSB[:, 0:512], in0=ps0[:, :], scalar1=1.0 / 127.0,
                                scalar2=None, op0=ALU.mult, op1=ALU.bypass),
                v.tensor_scalar(out=WSB[:, 512:1024], in0=ps1[:, :], scalar1=1.0 / 127.0,
                                scalar2=None, op0=ALU.mult, op1=ALU.bypass),
                v.tensor_copy(out=BB[:, 0:512], in_=ps2[:, :]),
                v.tensor_copy(out=BB[:, 512:1024], in_=ps3[:, :]))[-1], inc="vs")

            v_inv2 = ln_stats(have_sums=True)
            v_xq_ev, v_res = {}, {}
            for tt in range(NTT):
                slot = tt % 2
                xq = XQB[:, slot * 1024:(slot + 1) * 1024]
                xf = XTMPF[:, slot * 1024:(slot + 1) * 1024]
                w = [("vs", v_inv2), ("vs", v_bc), ("vs", v_wqt)]
                if tt >= 2:
                    w.append(("ts", v_xq_ev[tt - 2]))
                rot = tt % 2
                va = p.emit("vector", w, lambda v, tt=tt, xf=xf, xq=xq, rot=rot: (
                    v.tensor_scalar(out=xf, in0=X[:, tt * D:(tt + 1) * D],
                                    scalar1=SM[:, SM_INV + tt:SM_INV + tt + 1],
                                    scalar2=SM[:, SM_NMI + tt:SM_NMI + tt + 1],
                                    op0=ALU.mult, op1=ALU.add),
                    v.drain(),
                    v.reduce_max(out=SM[:, SM_AM + rot:SM_AM + rot + 1], in_=xf, axis=AX.X,
                                 apply_absolute_value=True),
                    v.drain(),
                    v.reciprocal(out=SM[:, SM_RA + rot:SM_RA + rot + 1],
                                 in_=SM[:, SM_AM + rot:SM_AM + rot + 1]),
                    v.drain(),
                    v.tensor_scalar(out=SM[:, SM_XS + tt:SM_XS + tt + 1],
                                    in0=SM[:, SM_AM + rot:SM_AM + rot + 1],
                                    scalar1=1.0 / 127.0, scalar2=None, op0=ALU.mult, op1=ALU.bypass),
                    v.tensor_scalar(out=SM[:, SM_R127 + rot:SM_R127 + rot + 1],
                                    in0=SM[:, SM_RA + rot:SM_RA + rot + 1],
                                    scalar1=127.0, scalar2=None, op0=ALU.mult, op1=ALU.bypass),
                    v.drain(),
                    v.tensor_scalar(out=xf, in0=xf, scalar1=SM[:, SM_R127 + rot:SM_R127 + rot + 1],
                                    scalar2=MAGIC, op0=ALU.mult, op1=ALU.add),
                    v.drain(),
                    v.tensor_scalar(out=xq, in0=xf, scalar1=-MAGIC, scalar2=None,
                                    op0=ALU.add, op1=ALU.bypass))[-1], inc="vs")
                tq = p.emit("tensor", [("vs", va)], lambda t, xq=xq, slot=slot: [
                    t.transpose((psq if slot == 0 else psq2)[:, j * 128:(j + 1) * 128],
                                xq[:, j * 128:(j + 1) * 128], IDB[:, :]) for j in range(NCH)][-1],
                    inc="ts")
                v_xq_ev[tt] = tq
                v_xq = p.emit("vector", [("ts", tq)], lambda v, tt=tt, slot=slot: v.tensor_copy(
                    out=XNFM[:, :].rearrange("p (c t) -> p c t", c=NCH)[:, :, tt * 128:(tt + 1) * 128],
                    in_=(psq if slot == 0 else psq2)[:, :].rearrange("p (j n) -> p j n", j=NCH)),
                    inc="vs")
                for og in range(2):
                    pp = ps0 if og == 0 else ps1
                    w2 = [("vs", v_xq)]
                    if tt >= 1:
                        w2.append(("vs", v_res[(tt - 1) * 2 + og]))

                    def qmm(t, tt=tt, pp=pp, og=og):
                        last = None
                        for j in range(NCH):
                            last = t.matmul(pp[:, :],
                                            XNFM[:, j * T + tt * 128: j * T + (tt + 1) * 128],
                                            WQT[:, j * 1024 + og * 512: j * 1024 + (og + 1) * 512],
                                            start=(j == 0), stop=(j == NCH - 1))
                        return last
                    tmm = p.emit("tensor", w2, qmm, inc="ts")
                    s1 = p.emit("scalar", [("ts", tmm)], lambda s, pp=pp, tt=tt, og=og:
                                s.activation(
                        out=SOFT2[:, og * 512:(og + 1) * 512], in_=pp[:, :], func=AF.Identity,
                        scale=SM[:, SM_XS + tt:SM_XS + tt + 1], bias=CZCOL[:, 0:1]), inc="ss")

                    def resf(v, tt=tt, og=og):
                        sl = slice(og * 512, (og + 1) * 512)
                        xsl = slice(tt * D + og * 512, tt * D + (og + 1) * 512)
                        v.tensor_tensor(out=SOFT2[:, sl], in0=SOFT2[:, sl], in1=WSB[:, sl],
                                        op=ALU.mult)
                        v.drain()
                        v.tensor_tensor(out=SOFT2[:, sl], in0=SOFT2[:, sl], in1=BB[:, sl],
                                        op=ALU.add)
                        v.drain()
                        if accum_for_next:
                            return v.scalar_tensor_tensor(
                                out=X[:, xsl], in0=SOFT2[:, sl], scalar=1.0, in1=X[:, xsl],
                                op0=ALU.mult, op1=ALU.add,
                                accum_out=SM[:, SM_A + tt * 2 + og:SM_A + tt * 2 + og + 1])
                        v.drain()
                        return v.tensor_tensor(out=X[:, xsl], in0=SOFT2[:, sl], in1=X[:, xsl],
                                               op=ALU.add)
                    v_res[tt * 2 + og] = p.emit("vector", [("ss", s1)], resf, inc="vs")
            if accum_for_next:
                p.emit("vector", [("vs", p.cnt["vs"])], lambda v: v.tensor_tensor(
                    out=SM[:, SM_SUM:SM_SUM + 16], in0=SM[:, SM_A:SM_A + 32:2],
                    in1=SM[:, SM_A + 1:SM_A + 32:2], op=ALU.add), inc="vs")
            st.close()
            p.flush(tag)

        if nphases >= 2:
            phase_ffnq("pb", accum_for_next=True)
            if "pb" in debug:
                dbg_list += [("pb_x", X)]

        def phase_attnb(pccut=None):
            stC = ExitStack()
            T2K = p.sb("T2K", [128, T], F32, stC)
            NEGC = p.sb("cNEGC", [128, 4], F32, stC)
            GB = p.sb("cGB", [128, 64], F32, stC)
            B1C = p.sb("cB1C", [128, 32], F32, stC)
            BVR = p.sb("cBVR", [128, 1024], BF16, stC)

            p.dma("sync", [], T2K[:, :], c_t2, "dc")
            v_t2 = p.cnt["dc"]

            v_sp = p.dma("sync", [("vs", p.cnt["vs"]), ("ss", p.cnt["ss"])],
                         spill.rearrange("(t p) d -> p t d", p=128),
                         X[:, :].rearrange("p (t d) -> p t d", t=NTT), "dx")
            cut = [False]
            def cuts(name, *tens):
                if pccut == name:
                    for i, tn in enumerate(tens):
                        dbg_list.append((name + "_%d" % i, tn if isinstance(tn, bass.AP) else tn[:, :]))
                    cut[0] = True
                return cut[0]
            v_inv3 = ln_stats(have_sums=True)
            v_z = {}
            for tt in range(NTT):
                w = [("vs", v_inv3)]
                if tt == 0:
                    w.append(("dx", v_sp))
                v_z[tt] = p.emit("vector", w, lambda v, tt=tt: v.tensor_scalar(
                    out=X[:, tt * D:(tt + 1) * D], in0=X[:, tt * D:(tt + 1) * D],
                    scalar1=SM[:, SM_INV + tt:SM_INV + tt + 1],
                    scalar2=SM[:, SM_NMI + tt:SM_NMI + tt + 1],
                    op0=ALU.mult, op1=ALU.add), inc="vs")
            p.flush("pc0")
            if cuts("pc0", X):
                return
            for l in range(2):
                wpar = [("vs", p.cnt["vs"]), ("ss", p.cnt["ss"])] if l > 0 else []
                for i, (src, col) in enumerate([(cln1g, 0), (cln1b, 8), (cln2g, 16), (cln2b, 24),
                                                (cbq, 32), (cbk, 40)]):
                    p.dma("sync", wpar if i == 0 else [], GB[:, col:col + 8], src[l], "dc")
                p.dma("sync", [], B1C[:, :], cb1[l], "dc")
                v_par = p.cnt["dc"]
                v_bvr0 = p.emit("vector", [("vs", p.cnt["vs"])], lambda v: v.memset(
                    BVR[:, :], 0.0), inc="vs")
                v_bvr = p.dma("gpsimd", [("ss", p.cnt["ss"]), ("vs", v_bvr0)],
                              BVR[0:1, :], bbv[l:l + 1, :], "dw")
                v_bqs = p.emit("vector", [("dc", v_par)], lambda v: v.tensor_scalar(
                    out=GB[:, 48:56], in0=GB[:, 32:40], scalar1=0.125, scalar2=None,
                    op0=ALU.mult, op1=ALU.bypass), inc="vs")

                stP = ExitStack()
                psF1 = p.ps("psF1", [128, 1024], F32, stP)
                psF2 = p.ps("psF2", [128, 1024], F32, stP)
                v_inva = ln_stats(have_sums=False)
                t_f = {}
                v_fe = {}
                for tt in range(NTT):
                    slot = tt % 2
                    xf = XTMPF[:, slot * 1024:(slot + 1) * 1024]
                    w = [("vs", v_inva)]
                    if tt >= 2:
                        w.append(("ts", t_f[tt - 2]))
                    va = p.emit("vector", w, lambda v, tt=tt, xf=xf: v.tensor_scalar(
                        out=xf, in0=X[:, tt * D:(tt + 1) * D],
                        scalar1=SM[:, SM_INV + tt:SM_INV + tt + 1],
                        scalar2=SM[:, SM_NMI + tt:SM_NMI + tt + 1],
                        op0=ALU.mult, op1=ALU.add), inc="vs")

                    def tf(t, xf=xf, slot=slot):
                        psf = psF1 if slot == 0 else psF2
                        last = None
                        for c in range(NCH):
                            last = t.transpose(psf[:, (c % 4) * 256 + (c // 4) * 128:
                                                   (c % 4) * 256 + (c // 4) * 128 + 128]
                                               if False else psf[:, c * 128:(c + 1) * 128],
                                               xf[:, c * 128:(c + 1) * 128], IDF[:, :])
                        return last
                    w2 = [("vs", va)]
                    if tt >= 2:
                        w2.append(("vs", v_fe[tt - 2]))
                    t_f[tt] = p.emit("tensor", w2, tf, inc="ts")

                    def fe(v, tt=tt, slot=slot):
                        psf = psF1 if slot == 0 else psF2
                        last = None
                        for c in range(NCH):
                            last = v.tensor_scalar(
                                out=XNFM[:, c * T + tt * 128: c * T + (tt + 1) * 128],
                                in0=psf[:, c * 128:(c + 1) * 128],
                                scalar1=GB[:, 0 + c:1 + c], scalar2=GB[:, 8 + c:9 + c],
                                op0=ALU.mult, op1=ALU.add)
                        return last
                    v_fe[tt] = p.emit("vector", [("ts", t_f[tt])], fe, inc="vs")
                stP.close()
                p.flush("pc1_l%d" % l)
                if cuts("pc1", XNFM):
                    return

                stL = ExitStack()
                QBUF = p.sb("cQBUF", [128, T], BF16, stL)
                KBUF = p.sb("cKBUF", [128, T], BF16, stL)
                VBUF = p.sb("cVBUF", [128, 2 * 1040], BF16, stL)
                UT = p.sb("cUT", [128, 4096], BF16, stL)
                SOFT = p.sb("cSOFT", [128, 2048], F32, stL)
                stP = ExitStack()
                ps0 = p.ps("cps0", [128, 512], F32, stP)
                ps1 = p.ps("cps1", [128, 512], F32, stP)
                ps2 = p.ps("cps2", [128, 512], F32, stP)
                ps3 = p.ps("cps3", [128, 512], F32, stP)
                psPV = p.ps("cpsPV", [65, 2048], F32, stP)

                def dma_qkv_b(hg, slot, waits):
                    res = []
                    for vi, wsrc in enumerate([bwq, bwk, bwv]):
                        sem = ("dwa", "dwb", "dwc")[vi]
                        res.append((sem, p.dma("gpsimd", waits if vi == 0 else [],
                                    WBUF[:, slot * 4096 + vi * 1024: slot * 4096 + (vi + 1) * 1024]
                                    .rearrange("p (c n) -> p c n", c=NCH),
                                    wsrc[l].rearrange("(c p) n -> p c n", p=128)
                                    [:, :, hg * 128:(hg + 1) * 128], sem)))
                    return res
                v_w = {0: dma_qkv_b(0, 0, [("dw", v_bvr)])}
                fence = []
                v_norm = {}
                prev_soft = None
                for hg in range(NCH):
                    slot = hg % 2
                    WQ = WBUF[:, slot * 4096: slot * 4096 + 1024]
                    WK = WBUF[:, slot * 4096 + 1024: slot * 4096 + 2048]
                    WV = WBUF[:, slot * 4096 + 2048: slot * 4096 + 3072]
                    qk_ev = {}
                    for tg in range(4):
                        pq = ps0 if tg % 2 == 0 else ps1
                        pk = ps2 if tg % 2 == 0 else ps3
                        w = []
                        if tg == 0:
                            w += list(v_w[hg]) + [("vs", v_bqs)]
                            if fence:
                                w += [("vs", fence[-1][0]), ("ss", fence[-1][1])]
                        if tg >= 2:
                            w.append(("vs", qk_ev[tg - 2]))

                        def mmqk(t, tg=tg, pq=pq, pk=pk, WQ=WQ, WK=WK):
                            for c in range(NCH):
                                t.matmul(pq[:, :], WQ[:, c * 128:(c + 1) * 128],
                                         XNFM[:, c * T + tg * 512: c * T + (tg + 1) * 512],
                                         start=(c == 0), stop=(c == NCH - 1))
                            last = None
                            for c in range(NCH):
                                last = t.matmul(pk[:, :], WK[:, c * 128:(c + 1) * 128],
                                                XNFM[:, c * T + tg * 512: c * T + (tg + 1) * 512],
                                                start=(c == 0), stop=(c == NCH - 1))
                            return last
                        tq = p.emit("tensor", w, mmqk, inc="ts")
                        sq = p.emit("scalar", [("ts", tq)], lambda s, tg=tg, pq=pq, pk=pk, hg=hg: (
                            s.activation(out=XTMPF[:, (tg % 2) * 512:(tg % 2) * 512 + 512],
                                         in_=pq[:, :], func=AF.Identity, scale=0.125,
                                         bias=GB[:, 48 + hg:49 + hg]),
                            s.activation(out=XTMPF[:, 1024 + (tg % 2) * 512:1024 + (tg % 2) * 512 + 512],
                                         in_=pk[:, :], func=AF.Identity, scale=1.0,
                                         bias=GB[:, 40 + hg:41 + hg]))[-1], inc="ss")
                        qk_ev[tg] = p.emit("vector", [("ss", sq)], lambda v, tg=tg: (
                            v.tensor_tensor(out=QBUF[:, tg * 512:(tg + 1) * 512],
                                            in0=XTMPF[:, (tg % 2) * 512:(tg % 2) * 512 + 512],
                                            in1=T2K[:, tg * 512:(tg + 1) * 512], op=ALU.mult),
                            v.tensor_tensor(out=KBUF[:, tg * 512:(tg + 1) * 512],
                                            in0=XTMPF[:, 1024 + (tg % 2) * 512:1024 + (tg % 2) * 512 + 512],
                                            in1=T2K[:, tg * 512:(tg + 1) * 512], op=ALU.mult))[-1],
                            inc="vs")
                    v_vev = {}
                    for g in range(4):
                        pv = ps0 if g % 2 == 0 else ps1
                        w = [("ss", qk_ev[3] if False else qk_ev[3])] if g == 0 else []
                        if g == 0:
                            w = [("vs", qk_ev[3])]
                        if g >= 2:
                            w.append(("vs", v_vev[g - 2]))

                        def mmv(t, g=g, pv=pv, WV=WV, hg=hg):
                            last = None
                            for i in range(4):
                                tt = g * 4 + i
                                for c in range(NCH):
                                    t.matmul(pv[:, i * 128:(i + 1) * 128],
                                             XNFM[:, c * T + tt * 128: c * T + (tt + 1) * 128],
                                             WV[:, c * 128:(c + 1) * 128],
                                             start=(c == 0), stop=False)
                                last = t.matmul(pv[:, i * 128:(i + 1) * 128], ONEB[0:1, 0:128],
                                                BVR[0:1, hg * 128:(hg + 1) * 128],
                                                start=False, stop=True)
                            return last
                        tv = p.emit("tensor", w, mmv, inc="ts")

                        def vev(v, g=g, pv=pv):
                            v.tensor_copy(
                                out=VBUF[:, 0:1040].rearrange("p (t y) -> p t y", t=NTT)
                                [:, g * 4:(g + 1) * 4, 0:64],
                                in_=pv[:, :].rearrange("p (t n) -> p t n", t=4)[:, :, 0:64])
                            return v.tensor_copy(
                                out=VBUF[:, 1040:2080].rearrange("p (t y) -> p t y", t=NTT)
                                [:, g * 4:(g + 1) * 4, 0:64],
                                in_=pv[:, :].rearrange("p (t n) -> p t n", t=4)[:, :, 64:128])
                        v_vev[g] = p.emit("vector", [("ts", tv)], vev, inc="vs")
                    if hg == 0:
                        p.emit("vector", [], lambda v: (
                            v.memset(VBUF[:, 64:1040:65], 1.0),
                            v.memset(VBUF[:, 1104:2080:65], 1.0))[-1], inc="vs")
                    if hg + 1 < NCH:
                        v_w[hg + 1] = dma_qkv_b(hg + 1, (hg + 1) % 2,
                                                [("vs", qk_ev[3]), ("vs", v_vev[3])])

                    for hh in range(2):
                        h = hg * 2 + hh
                        r0 = hh * 64
                        w = [("vs", qk_ev[3]), ("vs", v_vev[3]), ("vs", v_vev[2])] if hh == 0 else \
                            [("ss", prev_soft)]
                        tc = p.emit("tensor", w, lambda t, r0=r0: t.matmul(
                            ps2[:, 0:128], KBUF[r0:r0 + 64, 0:T:16], QBUF[r0:r0 + 64, 0:T:16],
                            start=True, stop=True), inc="ts")
                        vc = p.emit("vector", [("ts", tc)], lambda v: v.reduce_max(
                            out=SM[:, SM_MISC:SM_MISC + 1], in_=ps2[:, 0:128], axis=AX.X), inc="vs")
                        tc2 = p.emit("tensor", [("vs", vc)], lambda t: t.transpose(
                            ps3[0:1, 0:128], SM[:, SM_MISC:SM_MISC + 1], IDF[:, :]), inc="ts")
                        vc2 = p.emit("vector", [("ts", tc2)], lambda v: v.reduce_max(
                            out=SM[0:1, SM_MISC + 1:SM_MISC + 2], in_=ps3[0:1, 0:128], axis=AX.X),
                            inc="vs")
                        tc3 = p.emit("tensor", [("vs", vc2)], lambda t: t.matmul(
                            ps3[:, 128:129], ONEF[0:1, 0:128], SM[0:1, SM_MISC + 1:SM_MISC + 2],
                            start=True, stop=True), inc="ts")
                        s_c = p.emit("scalar", [("ts", tc3)], lambda s, hh=hh: s.activation(
                            out=NEGC[:, hh:hh + 1], in_=ps3[:, 128:129], func=AF.Copy,
                            scale=-1.0, bias=-CSLACK), inc="ss")

                        e_ev = {}
                        for kb in range(NTT):
                            uts = UT[:, (kb % 2) * 2048:(kb % 2 + 1) * 2048]
                            for qg in range(4):
                                m = kb * 4 + qg
                                pse2 = [ps2, ps3, ps0, ps1][m % 4]
                                w2 = [("ss", s_c)] if m == 0 else []
                                if m < 4 and h > 0:
                                    w2.append(("vs", v_norm[h * NTT - 1]))
                                if m >= 4:
                                    w2.append(("ss", e_ev[m - 4]))
                                tsm = p.emit("tensor", w2, lambda t, kb=kb, qg=qg, pse2=pse2, r0=r0:
                                             t.matmul(pse2[:, :],
                                                      KBUF[r0:r0 + 64, kb * 128:(kb + 1) * 128],
                                                      QBUF[r0:r0 + 64, qg * 512:(qg + 1) * 512],
                                                      start=True, stop=True), inc="ts")
                                e_ev[m] = p.emit("scalar", [("ts", tsm)],
                                                 lambda s, pse2=pse2, uts=uts, qg=qg, hh=hh:
                                                 s.activation(
                                    out=uts[:, qg * 512:(qg + 1) * 512], in_=pse2[:, :], func=AF.Exp,
                                    bias=NEGC[:, hh:hh + 1]), inc="ss")

                            def pvmm(t, kb=kb, uts=uts, hh=hh):
                                last = None
                                for qg in range(4):
                                    last = t.matmul(psPV[0:65, qg * 512:(qg + 1) * 512],
                                                    VBUF[:, hh * 1040 + kb * 65:
                                                         hh * 1040 + (kb + 1) * 65],
                                                    uts[:, qg * 512:(qg + 1) * 512],
                                                    start=(kb == 0), stop=(kb == NTT - 1))
                                return last
                            p.emit("tensor", [("ss", e_ev[kb * 4 + 3])], pvmm, inc="ts")
                        tpv = p.cnt["ts"]
                        s_soft = p.emit("scalar", [("ts", tpv)], lambda s: s.activation(
                            out=SOFT[0:65, :], in_=psPV[0:65, :], func=AF.Copy), inc="ss")
                        prev_soft = s_soft
                        for qt in range(NTT):
                            psn = ps0 if qt % 2 == 0 else ps1
                            w3 = [("ss", s_soft)]
                            if qt == 0 and hh == 0:
                                w3 += [("vs", v_vev[3]), ("vs", v_vev[2])]
                            idx0 = h * NTT + qt
                            if idx0 - 2 >= 0 and (idx0 - 2) in v_norm:
                                w3.append(("vs", v_norm[idx0 - 2]))
                            tn = p.emit("tensor", w3, lambda t, qt=qt, psn=psn: t.transpose(
                                psn[:, 0:65], SOFT[0:65, qt * 128:(qt + 1) * 128],
                                IDF[0:65, 0:65]), inc="ts")

                            def nrm(v, qt=qt, psn=psn, h=h):
                                mc = SM_MISC + 2 + (qt % 4)
                                v.reciprocal(out=SM[:, mc:mc + 1], in_=psn[:, 64:65])
                                v.drain()
                                return v.tensor_scalar(
                                    out=BIG[:, qt * 1024 + h * 64: qt * 1024 + (h + 1) * 64],
                                    in0=psn[:, 0:64],
                                    scalar1=SM[:, mc:mc + 1], scalar2=None,
                                    op0=ALU.mult, op1=ALU.bypass)
                            v_norm[h * NTT + qt] = p.emit("vector", [("ts", tn)], nrm, inc="vs")
                    fence.append((p.cnt["vs"], p.cnt["ss"]))
                v_att = p.cnt["vs"]
                v_wo = p.dma("gpsimd", [("vs", fence[-1][0])],
                             WBUF[:, :].rearrange("p (c n) -> p c n", c=NCH),
                             bwo[l].rearrange("(c p) n -> p c n", p=128), "dw")
                stP.close()
                stL.close()
                p.flush("pc23_l%d" % l)
                if cuts("pc3", BIG):
                    return

                stP = ExitStack()
                psb0c = p.ps("cpsb0", [128, 1024], BF16, stP)
                psb1c = p.ps("cpsb1", [128, 1024], BF16, stP)
                psW = [p.ps("cpsW%d" % i, [128, 512], F32, stP) for i in range(4)]
                t_tr3, v_ev3, v_wr = {}, {}, {}
                for tt in range(NTT):
                    psb = psb0c if tt % 2 == 0 else psb1c

                    def trf(t, tt=tt, psb=psb):
                        last = None
                        for c in range(NCH):
                            last = t.transpose(psb[:, c * 128:(c + 1) * 128],
                                               BIG[:, tt * 1024 + c * 128: tt * 1024 + (c + 1) * 128],
                                               IDB[:, :])
                        return last
                    w = [("vs", v_att)] if tt == 0 else []
                    if tt >= 2:
                        w.append(("vs", v_ev3[tt - 2]))
                    t_tr3[tt] = p.emit("tensor", w, trf, inc="ts")
                    v_ev3[tt] = p.emit("vector", [("ts", t_tr3[tt])], lambda v, tt=tt, psb=psb:
                                       v.tensor_copy(
                        out=XNFM[:, :].rearrange("p (c t) -> p c t", c=NCH)[:, :, tt * 128:(tt + 1) * 128],
                        in_=psb[:, :].rearrange("p (c n) -> p c n", c=NCH)), inc="vs")
                for tt in range(NTT):
                    for og in range(2):
                        k = tt * 2 + og
                        pp = psW[k % 4]
                        w = [("vs", v_ev3[tt])]
                        if k == 0:
                            w.append(("dw", v_wo))
                        if k >= 4:
                            w.append(("vs", v_wr[k - 4]))

                        def wmm(t, tt=tt, og=og, pp=pp):
                            last = None
                            for c in range(NCH):
                                last = t.matmul(pp[:, :],
                                                XNFM[:, c * T + tt * 128: c * T + (tt + 1) * 128],
                                                WBUF[:, c * 1024 + og * 512: c * 1024 + (og + 1) * 512],
                                                start=(c == 0), stop=(c == NCH - 1))
                            return last
                        tw = p.emit("tensor", w, wmm, inc="ts")
                        v_wr[k] = p.emit("vector", [("ts", tw)], lambda v, tt=tt, og=og, pp=pp, k=k: (
                            v.drain(),
                            v.scalar_tensor_tensor(
                                out=X[:, tt * D + og * 512: tt * D + (og + 1) * 512], in0=pp[:, :],
                                scalar=1.0, in1=X[:, tt * D + og * 512: tt * D + (og + 1) * 512],
                                op0=ALU.mult, op1=ALU.add,
                                accum_out=SM[:, SM_A + k:SM_A + k + 1]))[-1], inc="vs")
                p.emit("vector", [("vs", p.cnt["vs"])], lambda v: v.tensor_tensor(
                    out=SM[:, SM_SUM:SM_SUM + 16], in0=SM[:, SM_A:SM_A + 32:2],
                    in1=SM[:, SM_A + 1:SM_A + 32:2], op=ALU.add), inc="vs")
                stP.close()
                p.flush("pc4_l%d" % l)
                if cuts("pc4", X):
                    return

                v_invb = ln_stats(have_sums=True)
                stP = ExitStack()
                psF1 = p.ps("dpsF1", [128, 1024], F32, stP)
                psF2 = p.ps("dpsF2", [128, 1024], F32, stP)
                psH0 = p.ps("dpsH0", [128, 512], F32, stP)
                psH1 = p.ps("dpsH1", [128, 512], F32, stP)
                HID2 = p.sb("cHID2", [128, 16384], BF16, stP)

                def hid_t(hc):
                    buf = BIG if hc < 16 else HID2
                    return buf[:, (hc % 16) * 1024:(hc % 16 + 1) * 1024]

                for half in range(2):
                    t_f2, v_fe2 = {}, {}
                    for i in range(8):
                        tt = half * 8 + i
                        slot = i % 2
                        xf = XTMPF[:, slot * 1024:(slot + 1) * 1024]
                        w = [("vs", v_invb)]
                        if i >= 2:
                            w.append(("ts", t_f2[i - 2]))
                        va = p.emit("vector", w, lambda v, tt=tt, xf=xf: v.tensor_scalar(
                            out=xf, in0=X[:, tt * D:(tt + 1) * D],
                            scalar1=SM[:, SM_INV + tt:SM_INV + tt + 1],
                            scalar2=SM[:, SM_NMI + tt:SM_NMI + tt + 1],
                            op0=ALU.mult, op1=ALU.add), inc="vs")

                        def tf2(t, xf=xf, slot=slot):
                            psf = psF1 if slot == 0 else psF2
                            last = None
                            for c in range(NCH):
                                last = t.transpose(psf[:, c * 128:(c + 1) * 128],
                                                   xf[:, c * 128:(c + 1) * 128], IDF[:, :])
                            return last
                        w2 = [("vs", va)]
                        if i >= 2:
                            w2.append(("vs", v_fe2[i - 2]))
                        t_f2[i] = p.emit("tensor", w2, tf2, inc="ts")

                        def fe2(v, i=i, slot=slot, half=half):
                            psf = psF1 if slot == 0 else psF2
                            last = None
                            for c in range(NCH):
                                last = v.tensor_scalar(
                                    out=XNFM[:, c * T + half * 1024 + i * 128:
                                             c * T + half * 1024 + (i + 1) * 128],
                                    in0=psf[:, c * 128:(c + 1) * 128],
                                    scalar1=GB[:, 16 + c:17 + c], scalar2=GB[:, 24 + c:25 + c],
                                    op0=ALU.mult, op1=ALU.add)
                            return last
                        v_fe2[i] = p.emit("vector", [("ts", t_f2[i])], fe2, inc="vs")
                    v_xl2 = p.cnt["vs"]

                    v_w1 = {}
                    v_w1[0] = ("dwa", p.dma("gpsimd", [("vs", v_xl2)],
                                    WBUF[:, 0:4096].rearrange("p (c n) -> p c n", c=NCH),
                                    bw1[l].rearrange("(c p) n -> p c n", p=128)[:, :, 0:512], "dwa"))
                    hid_ev = {}
                    for sl8 in range(8):
                        slot = sl8 % 2
                        if sl8 + 1 < 8:
                            wd = [("ss", hid_ev[sl8 * 8 - 1])] if sl8 >= 1 else []
                            sem = "dwa" if (sl8 + 1) % 2 == 0 else "dwb"
                            v_w1[sl8 + 1] = (sem, p.dma("gpsimd", wd,
                                                  WBUF[:, (1 - slot) * 4096:(2 - slot) * 4096]
                                                  .rearrange("p (c n) -> p c n", c=NCH),
                                                  bw1[l].rearrange("(c p) n -> p c n", p=128)
                                                  [:, :, (sl8 + 1) * 512:(sl8 + 2) * 512], sem))
                        for j in range(4):
                            hc = sl8 * 4 + j
                            for tg in range(2):
                                m = hc * 2 + tg
                                ph_ = psH0 if m % 2 == 0 else psH1
                                w = []
                                if j == 0 and tg == 0:
                                    w.append(v_w1[sl8])
                                if m >= 2:
                                    w.append(("ss", hid_ev[m - 2]))

                                def hmm(t, hc=hc, tg=tg, ph_=ph_, slot=slot, half=half):
                                    last = None
                                    for c in range(NCH):
                                        last = t.matmul(
                                            ph_[:, :],
                                            WBUF[:, slot * 4096 + c * 512 + (hc % 4) * 128:
                                                 slot * 4096 + c * 512 + (hc % 4) * 128 + 128],
                                            XNFM[:, c * T + half * 1024 + tg * 512:
                                                 c * T + half * 1024 + (tg + 1) * 512],
                                            start=(c == 0), stop=(c == NCH - 1))
                                    return last
                                th = p.emit("tensor", w, hmm, inc="ts")
                                hid_ev[hc * 2 + tg] = p.emit(
                                    "scalar", [("ts", th)],
                                    lambda s, hc=hc, tg=tg, ph_=ph_: s.activation(
                                        out=hid_t(hc)[:, tg * 512:(tg + 1) * 512], in_=ph_[:, :],
                                        func=AF.Gelu_apprx_sigmoid,
                                        bias=B1C[:, hc:hc + 1]), inc="ss")
                    v_hid_done = p.cnt["ss"]
                    if half == 0 and cuts("pc5h", BIG, HID2, XNFM):
                        stP.close()
                        p.flush("pc5h")
                        return

                    v_w2 = {}
                    v_w2[0] = ("dwa", p.dma("gpsimd", [("ss", v_hid_done)],
                                    WBUF[:, 0:4096].rearrange("p (c n) -> p c n", c=32),
                                    bw2[l].rearrange("(c p) n -> p c n", p=128)[:, :, 0:128], "dwa"))
                    v_w2r = {}
                    for og in range(8):
                        slot = og % 2
                        if og + 1 < 8:
                            wd = [("ts", v_w2r[(og - 1) * 2 + 1][0])] if og >= 1 else []
                            sem = "dwa" if (og + 1) % 2 == 0 else "dwb"
                            v_w2[og + 1] = (sem, p.dma("gpsimd", wd,
                                                 WBUF[:, (1 - slot) * 4096:(2 - slot) * 4096]
                                                 .rearrange("p (c n) -> p c n", c=32),
                                                 bw2[l].rearrange("(c p) n -> p c n", p=128)
                                                 [:, :, (og + 1) * 128:(og + 2) * 128], sem))
                        for tg in range(2):
                            ph_ = psH0 if tg % 2 == 0 else psH1
                            w = [("ss", v_hid_done)]
                            if tg == 0:
                                w.append(v_w2[og])
                            if (og * 2 + tg) >= 2:
                                w.append(("vs", v_w2r[(og * 2 + tg) - 2][1]))

                            def wmm2(t, og=og, tg=tg, ph_=ph_, slot=slot):
                                last = None
                                for hc in range(32):
                                    last = t.matmul(
                                        ph_[:, :],
                                        WBUF[:, slot * 4096 + hc * 128: slot * 4096 + (hc + 1) * 128],
                                        hid_t(hc)[:, tg * 512:(tg + 1) * 512],
                                        start=(hc == 0), stop=(hc == 31))
                                return last
                            tw2 = p.emit("tensor", w, wmm2, inc="ts")
                            sv = p.emit("vector", [("ts", tw2)], lambda v, ph_=ph_: v.tensor_copy(
                                out=SOFT2[:, 0:512] if ph_ is psH0 else SOFT2[:, 512:1024],
                                in_=ph_[:, :]), inc="vs")

                            def tr2(t, og=og, tg=tg, ph_=ph_):
                                src = SOFT2[:, 0:512] if ph_ is psH0 else SOFT2[:, 512:1024]
                                psf = psF1 if tg % 2 == 0 else psF2
                                last = None
                                for i in range(4):
                                    last = t.transpose(psf[:, i * 128:(i + 1) * 128],
                                                       src[:, i * 128:(i + 1) * 128], IDF[:, :])
                                return last
                            w3 = [("vs", sv)]
                            if (og * 2 + tg) >= 2:
                                w3.append(("vs", v_w2r[(og * 2 + tg) - 2][1]))
                            tt2 = p.emit("tensor", w3, tr2, inc="ts")

                            def radd(v, og=og, tg=tg, half=half):
                                psf = psF1 if tg % 2 == 0 else psF2
                                last = None
                                for i in range(4):
                                    tt = half * 8 + tg * 4 + i
                                    last = v.tensor_tensor(
                                        out=X[:, tt * D + og * 128: tt * D + (og + 1) * 128],
                                        in0=X[:, tt * D + og * 128: tt * D + (og + 1) * 128],
                                        in1=psf[:, i * 128:(i + 1) * 128], op=ALU.add)
                                return last
                            vr = p.emit("vector", [("ts", tt2)], radd, inc="vs")
                            v_w2r[og * 2 + tg] = (tw2, vr)
                            if half == 0 and og == 0 and tg == 0 and cuts("pc5w", SOFT2[:, 0:512]):
                                stP.close()
                                p.flush("pc5w")
                                return
                    if half == 0 and cuts("pc5a", X):
                        stP.close()
                        p.flush("pc5a")
                        return
                stP.close()
                p.flush("pc5_l%d" % l)
                if cuts("pc5", X):
                    return
            v_rd = {}
            v_x3 = {}
            for tt in range(NTT):
                slot = tt % 2
                sem = "dh0" if slot == 0 else "dh1"
                wd = [("vs", v_x3[tt - 2])] if tt >= 2 else []
                v_rd[tt] = p.dma("sync", wd, XTMPF[:, slot * 1024:(slot + 1) * 1024],
                                 spill.rearrange("(t p) d -> p t d", p=128)[:, tt], sem)
                v_x3[tt] = p.emit("vector", [(sem, v_rd[tt])], lambda v, tt=tt, slot=slot: (
                    v.drain(),
                    v.scalar_tensor_tensor(
                        out=X[:, tt * D:(tt + 1) * D], in0=XTMPF[:, slot * 1024:(slot + 1) * 1024],
                        scalar=1.0, in1=X[:, tt * D:(tt + 1) * D], op0=ALU.mult, op1=ALU.add,
                        accum_out=SM[:, SM_A + tt:SM_A + tt + 1]))[-1], inc="vs")
            p.emit("vector", [("vs", p.cnt["vs"])], lambda v: v.tensor_copy(
                out=SM[:, SM_SUM:SM_SUM + 16], in_=SM[:, SM_A:SM_A + 16]), inc="vs")
            stC.close()
            p.flush("pc_end")

        if nphases >= 3:
            phase_attnb(pccut)
            if "pc" in debug:
                dbg_list += [("pc_x", X)]

        if nphases >= 4:
            phase_ffnq("pd", accum_for_next=False)


    except _Cut:
        pass
    wfin = [(k, v) for k, v in p.cnt.items() if v > 0 and k != "dq"]
    p.emit("sync", wfin, lambda s: s.dma_start(
        out=y_out.rearrange("(t p) d -> p t d", p=128),
        in_=X[:, :].rearrange("p (t d) -> p t d", t=NTT)), inc="dq", incby=16)
    wdbg = [("vs", p.cnt["vs"]), ("ss", p.cnt["ss"]), ("ts", p.cnt["ts"])]
    for name, ten in dbg_list:
        ap = ten if isinstance(ten, bass.AP) else ten[:, :]
        o = nc.dram_tensor("dbg_" + name, list(ap.shape), ap.dtype, kind="ExternalOutput").ap()
        p.emit("sync", wdbg, lambda s, o=o, ap=ap: s.dma_start(out=o, in_=ap),
               inc="dq", incby=16)
        wdbg = []
    vq = p.cnt["dq"]
    if _os2.environ.get("NONOP"):
        p.emit("sync", [("dq", vq)], lambda s: None)
    else:
        p.emit("sync", [("dq", vq)], lambda s: s.nop())
    p.flush("fin")
    return p


def make_inputs(inputs):
    consts = host_consts()
    base = {
        "wqkv": inputs["L0_attn_a_w_qkv"], "wproj": inputs["L0_attn_a_w_proj"],
        "fw": inputs["L0_ffn_weight"], "fb": np.asarray(inputs["L0_ffn_bias"]).reshape(1, -1),
        "bln1g": inputs["L0_attn_b_ln1_g"], "bln1b": inputs["L0_attn_b_ln1_b"],
        "bln2g": inputs["L0_attn_b_ln2_g"], "bln2b": inputs["L0_attn_b_ln2_b"],
        "bwq": inputs["L0_attn_b_wq"], "bwk": inputs["L0_attn_b_wk"],
        "bwv": inputs["L0_attn_b_wv"], "bwo": inputs["L0_attn_b_wo"],
        "bbq": inputs["L0_attn_b_bq"], "bbk": inputs["L0_attn_b_bk"],
        "bbv": inputs["L0_attn_b_bv"],
        "bw1": inputs["L0_attn_b_w1"], "bb1": inputs["L0_attn_b_b1"],
        "bw2": inputs["L0_attn_b_w2"],
    }
    def cols(a, n):
        a = np.asarray(a, dtype=np.float32)
        return np.ascontiguousarray(a.reshape(2, n, 128).transpose(0, 2, 1))
    base["cln1g"] = cols(inputs["L0_attn_b_ln1_g"], 8)
    base["cln1b"] = cols(inputs["L0_attn_b_ln1_b"], 8)
    base["cln2g"] = cols(inputs["L0_attn_b_ln2_g"], 8)
    base["cln2b"] = cols(inputs["L0_attn_b_ln2_b"], 8)
    base["cbq"] = cols(inputs["L0_attn_b_bq"], 8)
    base["cbk"] = cols(inputs["L0_attn_b_bk"], 8)
    base["cb1"] = cols(inputs["L0_attn_b_b1"], 32)
    base = {k: np.ascontiguousarray(np.asarray(v, dtype=np.float32)) for k, v in base.items()}
    base.update(consts)
    x = np.asarray(inputs["L0_attn_a_x"], dtype=np.float32)
    return [dict(base, x=np.ascontiguousarray(x[b])) for b in range(x.shape[0])]


def kernel(**inputs):
    prog = build()
    in_maps = make_inputs(inputs)
    res = run_bass_kernel_spmd(prog.nc, in_maps, core_ids=list(range(8)))
    return np.stack([r["y"] for r in res.results], axis=0)


# version 06
# speedup vs baseline: 1.0236x; 1.0034x over previous
import math
import numpy as np
import ml_dtypes
from contextlib import ExitStack

import concourse.bass as bass
import concourse.mybir as mybir
from concourse.bass_utils import run_bass_kernel_spmd

F32 = mybir.dt.float32
BF16 = mybir.dt.bfloat16
AF = mybir.ActivationFunctionType
ALU = mybir.AluOpType
AX = mybir.AxisListType

T = 2048
D = 1024
NTT = 16
NCH = 8
NH = 16
MAGIC = 12582912.0
EPS = 1e-5
CSLACK = 8.0


def host_consts():
    bf = ml_dtypes.bfloat16
    c = {}
    c["C_IDB"] = np.eye(128, dtype=bf)
    c["C_IDF"] = np.eye(128, dtype=np.float32)
    g = np.zeros((128, NTT * 64), dtype=np.float32)
    for tt in range(NTT):
        for n in range(128):
            g[n, tt * 64 + 4 * tt + n // 32] = 1.0 / 32.0
    c["C_G"] = g.astype(bf)
    freqs = 2.0 ** np.linspace(0.0, math.log2(5.0), 16).astype(np.float32)
    pos = 1.0 + np.arange(T, dtype=np.float32)
    rad = pos[None, :] * freqs[:, None] * math.pi
    sinr, cosr = np.sin(rad).astype(np.float32), np.cos(rad).astype(np.float32)
    t2 = np.ones((128, T), dtype=np.float32)
    for d in range(32):
        alt = -1.0 if d % 2 == 0 else 1.0
        t2[d] = sinr[d % 16] + alt * cosr[d % 16]
        t2[64 + d] = t2[d]
    c["C_T2"] = t2
    e = np.zeros((128, 1024), dtype=np.float32)
    for pp in range(128):
        e[pp, pp % 64::64] = 1.0
    c["C_EYE"] = e
    c["C_ONEF"] = np.ones((128, 128), dtype=np.float32)
    c["C_ONEB"] = np.ones((128, 128), dtype=bf)
    return c


class Prog:
    def __init__(self):
        self.nc = bass.Bass("TRN2", target_bir_lowering=False, debug=False)
        self.es = ExitStack()
        self.sems = {}
        self.cnt = {}
        self.steps = []
        self._names = {}

    def mksem(self, *names):
        for n in names:
            self.sems[n] = self.es.enter_context(self.nc.semaphore(n))
            self.cnt[n] = 0

    def _uniq(self, name):
        n = self._names.get(name, 0)
        self._names[name] = n + 1
        return name if n == 0 else "%s_%d" % (name, n)

    def sb(self, name, shape, dt, stack=None):
        return (stack or self.es).enter_context(
            self.nc.sbuf_tensor(self._uniq(name), shape, dt))

    def ps(self, name, shape, dt, stack=None):
        return (stack or self.es).enter_context(
            self.nc.psum_tensor(self._uniq(name), shape, dt))

    def bump(self, name, by=1):
        self.cnt[name] += by
        return self.cnt[name]

    def emit(self, eng_name, waits, fn, inc=None, incby=1):
        post = self.bump(inc, incby) if inc else None
        semtab = self.sems

        def run(eng, waits=tuple(waits), fn=fn, inc=inc, incby=incby):
            for n, v in waits:
                eng.wait_ge(semtab[n], v)
            last = fn(eng)
            if inc is not None:
                assert last is not None, "step with inc returned no instruction"
                last.then_inc(semtab[inc], incby)
        self.steps.append((eng_name, run))
        return post

    def dma(self, eng_name, waits, out, in_, sem="dw"):
        pre = self.cnt[sem]
        post = self.bump(sem, 16)
        semtab = self.sems
        waits = list(waits)
        if pre > 0:
            waits.append((sem, pre))

        def run(eng, waits=tuple(waits)):
            for n, v in waits:
                eng.wait_ge(semtab[n], v)
            eng.dma_start(out=out, in_=in_).then_inc(semtab[sem], 16)
        self.steps.append((eng_name, run))
        return post

    def flush(self, name=None):
        by_eng = {}
        for e, fn in self.steps:
            by_eng.setdefault(e, []).append(fn)
        self.steps = []
        if not by_eng:
            return
        import os as _os
        with self.nc.Block(None if _os.environ.get("NOBLOCKNAME") else name) as block:
            for e, fns in by_eng.items():
                def body(eng, fns=fns):
                    for fn in fns:
                        fn(eng)
                getattr(block, e)(body)


class _Cut(Exception):
    pass


def build(nphases=99, debug=(), pccut=None, pacut=None):
    p = Prog()
    nc = p.nc
    dbg_list = []

    def pchk(tag):
        if pacut == tag:
            raise _Cut()

    def din(name, shape, dt=F32):
        return nc.dram_tensor(name, shape, dt, kind="ExternalInput").ap()

    x_in = din("x", [T, D])
    wqkv = din("wqkv", [D, 3 * D])
    wproj = din("wproj", [D, D])
    fw = din("fw", [D, D])
    fb = din("fb", [1, D])
    bln1g = din("bln1g", [2, D]); bln1b = din("bln1b", [2, D])
    bln2g = din("bln2g", [2, D]); bln2b = din("bln2b", [2, D])
    bwq = din("bwq", [2, D, D]); bwk = din("bwk", [2, D, D])
    bwv = din("bwv", [2, D, D]); bwo = din("bwo", [2, D, D])
    bbq = din("bbq", [2, D]); bbk = din("bbk", [2, D]); bbv = din("bbv", [2, D])
    bw1 = din("bw1", [2, D, 4 * D]); bb1 = din("bb1", [2, 4 * D])
    bw2 = din("bw2", [2, 4 * D, D])
    cln1g = din("cln1g", [2, 128, 8]); cln1b = din("cln1b", [2, 128, 8])
    cln2g = din("cln2g", [2, 128, 8]); cln2b = din("cln2b", [2, 128, 8])
    cbq = din("cbq", [2, 128, 8]); cbk = din("cbk", [2, 128, 8])
    cb1 = din("cb1", [2, 128, 32])
    c_idb = din("C_IDB", [128, 128], BF16)
    c_idf = din("C_IDF", [128, 128], F32)
    c_g = din("C_G", [128, NTT * 64], BF16)
    c_t2 = din("C_T2", [128, T], F32)
    c_eye = din("C_EYE", [128, 1024], F32)
    c_onef = din("C_ONEF", [128, 128], F32)
    c_oneb = din("C_ONEB", [128, 128], BF16)

    y_out = nc.dram_tensor("y", [T, D], F32, kind="ExternalOutput").ap()
    import os as _os
    if _os.environ.get("NOSPILL"):
        spill = nc.dram_tensor("spill", [T, D], F32, kind="ExternalInput").ap()
    else:
        spill = nc.dram_tensor("spill", [T, D], F32).ap()

    X = p.sb("X", [128, NTT * D], F32)
    XNFM = p.sb("XNFM", [128, NCH * T], BF16)
    BIG = p.sb("BIG", [128, 16384], BF16)
    WBUF = p.sb("WBUF", [128, 8192], BF16)
    IDB = p.sb("IDB", [128, 128], BF16)
    IDF = p.sb("IDF", [128, 128], F32)
    CG = p.sb("CG", [128, NTT * 64], BF16)
    CEYE = p.sb("CEYE", [128, 1024], F32)
    ONEF = p.sb("ONEF", [128, 128], F32)
    ONEB = p.sb("ONEB", [128, 128], BF16)
    SM = p.sb("SMALL", [128, 256], F32)
    CZCOL = p.sb("CZCOL", [128, 2], F32)
    XTMPF = p.sb("XTMPF", [128, 2048], F32)
    SOFT2 = p.sb("SOFT2", [128, 1024], F32)

    SM_SUM, SM_SQ, SM_MM, SM_VAR, SM_INV, SM_NMI = 0, 16, 32, 48, 64, 80
    SM_A, SM_AM, SM_RA, SM_R127, SM_XS, SM_MISC = 96, 128, 144, 160, 176, 192

    p.mksem("dx", "dw", "dwa", "dwb", "dwc", "dh0", "dh1", "dc", "dq", "ts", "vs", "ss", "gs")

    v_x = p.dma("sync", [], X[:, :].rearrange("p (t d) -> p t d", t=NTT),
                x_in.rearrange("(t p) d -> p t d", p=128), "dx")
    import os as _os2
    for dst, src in [(IDB, c_idb), (IDF, c_idf), (CG, c_g),
                     (CEYE, c_eye), (ONEF, c_onef), (ONEB, c_oneb)]:
        if not _os2.environ.get("NOCONST"):
            p.dma("sync", [], dst[:, :], src, "dc")
    v_consts = p.cnt["dc"]
    if not _os2.environ.get("NOMEM"):
        p.emit("vector", [], lambda v: (
            v.memset(CZCOL[:, 0:1], 0.0),
            v.memset(CZCOL[:, 1:2], EPS * 1024.0 * 1024.0))[-1], inc="vs")
    v_czcol = p.cnt["vs"]
    p.flush("p0")
    try:
        pchk("p0")

        def ln_stats(have_sums):
            for tt in range(NTT):
                if not have_sums:
                    w = [("dx", v_x)] if tt == 0 else []
                    if p.cnt["ss"] >= 1:
                        w.append(("ss", p.cnt["ss"]))
                    p.emit("scalar", w, lambda s, tt=tt: s.activation(
                        out=XTMPF[:, 0:1024], in_=X[:, tt * D:(tt + 1) * D], func=AF.Copy,
                        accum_out=SM[:, SM_SUM + tt:SM_SUM + tt + 1]), inc="ss")
                w = [("dx", v_x)] if (tt == 0 and have_sums) else []
                if p.cnt["ss"] >= 1:
                    w.append(("ss", p.cnt["ss"]))
                p.emit("scalar", w, lambda s, tt=tt: s.activation(
                    out=XTMPF[:, 1024:2048], in_=X[:, tt * D:(tt + 1) * D], func=AF.Square,
                    accum_out=SM[:, SM_SQ + tt:SM_SQ + tt + 1]), inc="ss")
            vss = p.cnt["ss"]

            def m1(v):
                v.tensor_tensor(out=SM[:, SM_MM:SM_MM + 16], in0=SM[:, SM_SUM:SM_SUM + 16],
                                in1=SM[:, SM_SUM:SM_SUM + 16], op=ALU.mult)
                v.drain()
                return v.scalar_tensor_tensor(out=SM[:, SM_VAR:SM_VAR + 16],
                                              in0=SM[:, SM_SQ:SM_SQ + 16], scalar=1024.0,
                                              in1=SM[:, SM_MM:SM_MM + 16],
                                              op0=ALU.mult, op1=ALU.subtract)
            vv = p.emit("vector", [("ss", vss)], m1, inc="vs")
            sv = p.emit("scalar", [("vs", vv)], lambda s: s.activation(
                out=SM[:, SM_MM:SM_MM + 16], in_=SM[:, SM_VAR:SM_VAR + 16], func=AF.Sqrt,
                bias=CZCOL[:, 1:2]), inc="ss")

            def m2(v):
                v.reciprocal(out=SM[:, SM_VAR:SM_VAR + 16], in_=SM[:, SM_MM:SM_MM + 16])
                v.drain()
                v.tensor_scalar(out=SM[:, SM_INV:SM_INV + 16], in0=SM[:, SM_VAR:SM_VAR + 16],
                                scalar1=1024.0, scalar2=None, op0=ALU.mult, op1=ALU.bypass)
                v.drain()
                return v.scalar_tensor_tensor(out=SM[:, SM_NMI:SM_NMI + 16],
                                              in0=SM[:, SM_SUM:SM_SUM + 16], scalar=-1.0 / 1024.0,
                                              in1=SM[:, SM_INV:SM_INV + 16],
                                              op0=ALU.mult, op1=ALU.mult)
            return p.emit("vector", [("ss", sv)], m2, inc="vs")

        stA = ExitStack()
        stA2 = ExitStack()
        QLB = p.sb("QLB", [128, 512], BF16, stA)
        KLB = p.sb("KLB", [128, 512], BF16, stA)
        XNLF = p.sb("XNLF", [128, 512], BF16, stA)
        K3VB = p.sb("K3VB", [128, 1040], BF16, stA)
        Z2A = p.sb("Z2A", [128, 520], BF16, stA)
        QBUF = p.sb("QBUF", [128, T], BF16, stA2)
        KBUF = p.sb("KBUF", [128, T], BF16, stA2)
        VBUF = p.sb("VBUF", [128, 2 * 1040], BF16, stA2)
        XTMPB = p.sb("XTMPB", [128, 2048], BF16, stA2)
        AE3 = p.sb("AE3", [128, 2048], BF16, stA2)
        XNL = p.sb("XNL", [128, 1024], BF16, stA2)
        NEGC = p.sb("NEGC", [128, 8], F32, stA2)

        pa1 = ExitStack()
        psT = p.ps("psT", [128, 1024], BF16, pa1)
        psT2 = p.ps("psT2", [128, 1024], BF16, pa1)
        psLM = p.ps("psLM", [64, 1024], F32, pa1)
        v_inv = ln_stats(have_sums=False)
        t_tp, v_ev = {}, {}
        for tt in range(NTT):
            slot = tt % 2
            w = [("vs", v_inv)]
            if tt >= 2:
                w.append(("ts", t_tp[tt - 2]))
            va = p.emit("vector", w, lambda v, tt=tt, slot=slot: v.tensor_scalar(
                out=XTMPB[:, slot * 1024:(slot + 1) * 1024], in0=X[:, tt * D:(tt + 1) * D],
                scalar1=SM[:, SM_INV + tt:SM_INV + tt + 1],
                scalar2=SM[:, SM_NMI + tt:SM_NMI + tt + 1],
                op0=ALU.mult, op1=ALU.add), inc="vs")

            def tfn(t, tt=tt, slot=slot):
                xn = XTMPB[:, slot * 1024:(slot + 1) * 1024]
                pst = psT if slot == 0 else psT2
                for c in range(NCH):
                    t.transpose(pst[:, c * 128:(c + 1) * 128], xn[:, c * 128:(c + 1) * 128], IDB[:, :])
                t.matmul(psLM[:, 0:512], CG[:, tt * 64:(tt + 1) * 64], xn[:, 0:512],
                         start=(tt == 0), stop=(tt == NTT - 1))
                return t.matmul(psLM[:, 512:1024], CG[:, tt * 64:(tt + 1) * 64], xn[:, 512:1024],
                                start=(tt == 0), stop=(tt == NTT - 1))
            w = [("vs", va), ("dc", v_consts)]
            if tt >= 2:
                w.append(("vs", v_ev[tt - 2]))
            t_tp[tt] = p.emit("tensor", w, tfn, inc="ts")
            v_ev[tt] = p.emit("vector", [("ts", t_tp[tt])], lambda v, slot=slot, tt=tt: v.tensor_copy(
                out=XNFM[:, :].rearrange("p (c t) -> p c t", c=NCH)[:, :, tt * 128:(tt + 1) * 128],
                in_=(psT if slot == 0 else psT2)[:, :].rearrange("p (c n) -> p c n", c=NCH)),
                inc="vs")
        s_xnl = p.emit("scalar", [("ts", t_tp[NTT - 1])], lambda s: s.activation(
            out=XNL[0:64, :], in_=psLM[:, :], func=AF.Copy), inc="ss")

        def t_xnl(t):
            last = None
            for c in range(NCH):
                last = t.transpose(psT[:, c * 64:(c + 1) * 64], XNL[0:64, c * 128:(c + 1) * 128],
                                   IDB[0:64, 0:64])
            return last
        tx = p.emit("tensor", [("ss", s_xnl), ("vs", v_ev[NTT - 1]), ("vs", v_ev[NTT - 2])],
                    t_xnl, inc="ts")
        v_xnlf = p.emit("vector", [("ts", tx)], lambda v: v.tensor_copy(
            out=XNLF[:, :], in_=psT[:, 0:512]), inc="vs")
        pa1.close()
        p.flush("pa1")
        pchk("pa1")
        if "pa1" in debug:
            dbg_list += [("pa1_xnfm", XNFM), ("pa1_xnlf", XNLF)]

        pa2 = ExitStack()
        ps0 = p.ps("ps0", [128, 512], F32, pa2)
        ps1 = p.ps("ps1", [128, 512], F32, pa2)
        ps2 = p.ps("ps2", [128, 512], F32, pa2)
        ps3 = p.ps("ps3", [128, 512], F32, pa2)
        psx = p.ps("psx", [128, 512], F32, pa2)
        psk = p.ps("psk", [128, 512], F32, pa2)
        psb0 = p.ps("psb0", [128, 1024], BF16, pa2)
        psb1 = p.ps("psb1", [128, 1024], BF16, pa2)

        def dma_qkv(hg, slot, waits):
            res = []
            for vi in range(3):
                sem = ("dwa", "dwb", "dwc")[vi]
                res.append((sem, p.dma("gpsimd", waits if vi == 0 else [],
                            WBUF[:, slot * 4096 + vi * 1024: slot * 4096 + (vi + 1) * 1024]
                            .rearrange("p (c n) -> p c n", c=NCH),
                            wqkv.rearrange("(c p) n -> p c n", p=128)
                            [:, :, vi * D + hg * 128: vi * D + (hg + 1) * 128], sem)))
            return res

        v_w = {0: dma_qkv(0, 0, [])}
        prev_tb = None
        v_vbuf_ones = None
        v_k3n_all = []
        hg_fence = []
        for hg in range(NCH):
            slot = hg % 2
            WQ = WBUF[:, slot * 4096: slot * 4096 + 1024]
            WK = WBUF[:, slot * 4096 + 1024: slot * 4096 + 2048]
            WV = WBUF[:, slot * 4096 + 2048: slot * 4096 + 3072]

            qk_ev = {}
            for tg in range(4):
                pq = ps0 if tg % 2 == 0 else ps1
                pk = ps2 if tg % 2 == 0 else ps3
                w = []
                if tg == 0:
                    w += list(v_w[hg]) + [("vs", v_xnlf)]
                    if hg_fence:
                        w += [("vs", hg_fence[-1][0]), ("ss", hg_fence[-1][1])]
                if tg >= 2:
                    w.append(("ss", qk_ev[tg - 2]))

                def mmqk(t, tg=tg, pq=pq, pk=pk, WQ=WQ, WK=WK):
                    for c in range(NCH):
                        t.matmul(pq[:, :], WQ[:, c * 128:(c + 1) * 128],
                                 XNFM[:, c * T + tg * 512: c * T + (tg + 1) * 512],
                                 start=(c == 0), stop=(c == NCH - 1))
                    last = None
                    for c in range(NCH):
                        last = t.matmul(pk[:, :], WK[:, c * 128:(c + 1) * 128],
                                        XNFM[:, c * T + tg * 512: c * T + (tg + 1) * 512],
                                        start=(c == 0), stop=(c == NCH - 1))
                    return last
                tq = p.emit("tensor", w, mmqk, inc="ts")
                qk_ev[tg] = p.emit("scalar", [("ts", tq)], lambda s, tg=tg, pq=pq, pk=pk: (
                    s.activation(out=QBUF[:, tg * 512:(tg + 1) * 512], in_=pq[:, :],
                                 func=AF.Copy, scale=0.125),
                    s.activation(out=KBUF[:, tg * 512:(tg + 1) * 512], in_=pk[:, :],
                                 func=AF.Copy))[-1], inc="ss")

            def mmql(t, WQ=WQ, WK=WK):
                for c in range(NCH):
                    t.matmul(psx[:, 0:64], WQ[:, c * 128:(c + 1) * 128], XNLF[:, c * 64:(c + 1) * 64],
                             start=(c == 0), stop=(c == NCH - 1))
                last = None
                for c in range(NCH):
                    last = t.matmul(psx[:, 64:128], WK[:, c * 128:(c + 1) * 128],
                                    XNLF[:, c * 64:(c + 1) * 64],
                                    start=(c == 0), stop=(c == NCH - 1))
                return last
            wql = list(v_w[hg]) + [("vs", v_xnlf)]
            if hg_fence:
                wql.append(("vs", hg_fence[-1][0]))
            tql = p.emit("tensor", wql, mmql, inc="ts")
            v_ql = p.emit("vector", [("ts", tql)], lambda v, hg=hg: (
                v.tensor_scalar(out=QLB[:, hg * 64:(hg + 1) * 64], in0=psx[:, 0:64],
                                scalar1=0.125, scalar2=None, op0=ALU.mult, op1=ALU.bypass),
                v.tensor_copy(out=KLB[:, hg * 64:(hg + 1) * 64], in_=psx[:, 64:128]))[-1], inc="vs")

            v_vev = {}
            for g in range(4):
                pv = psk if g % 2 == 0 else psx
                w = []
                if g == 0:
                    w += list(v_w[hg])
                    if hg_fence:
                        w.append(("vs", hg_fence[-1][0]))
                if g == 1:
                    w.append(("vs", v_ql))
                if g >= 2:
                    w.append(("vs", v_vev[g - 2]))

                def mmv(t, g=g, pv=pv, WV=WV):
                    last = None
                    for i in range(4):
                        tt = g * 4 + i
                        for c in range(NCH):
                            last = t.matmul(pv[:, i * 128:(i + 1) * 128],
                                            XNFM[:, c * T + tt * 128: c * T + (tt + 1) * 128],
                                            WV[:, c * 128:(c + 1) * 128],
                                            start=(c == 0), stop=(c == NCH - 1))
                    return last
                tv = p.emit("tensor", w, mmv, inc="ts")

                def vev(v, g=g, pv=pv):
                    v.tensor_copy(
                        out=VBUF[:, 0:1040].rearrange("p (t y) -> p t y", t=NTT)
                        [:, g * 4:(g + 1) * 4, 0:64],
                        in_=pv[:, :].rearrange("p (t n) -> p t n", t=4)[:, :, 0:64])
                    return v.tensor_copy(
                        out=VBUF[:, 1040:2080].rearrange("p (t y) -> p t y", t=NTT)
                        [:, g * 4:(g + 1) * 4, 0:64],
                        in_=pv[:, :].rearrange("p (t n) -> p t n", t=4)[:, :, 64:128])
                v_vev[g] = p.emit("vector", [("ts", tv)], vev, inc="vs")
            if hg == 0:
                v_vbuf_ones = p.emit("vector", [], lambda v: (
                    v.memset(VBUF[:, 64:1040:65], 1.0),
                    v.memset(VBUF[:, 1104:2080:65], 1.0))[-1], inc="vs")

            if hg + 1 < NCH:
                v_w[hg + 1] = dma_qkv(hg + 1, (hg + 1) % 2,
                                      [("ss", qk_ev[3]), ("vs", v_vev[3]), ("vs", v_ql)])
            if hg == 0:
                pchk("pa2a")

            base_w = [("ss", qk_ev[3]), ("vs", v_ql), ("vs", v_vev[2]), ("vs", v_vev[3])]
            tc = p.emit("tensor", base_w, lambda t, hg=hg: (
                t.matmul(ps0[:, 0:64], QBUF[0:64, 0:T:16],
                         KLB[0:64, hg * 64:(hg + 1) * 64], start=True, stop=True),
                t.matmul(ps1[:, 0:64], QBUF[64:128, 0:T:16],
                         KLB[64:128, hg * 64:(hg + 1) * 64], start=True, stop=True))[-1], inc="ts")
            vc = p.emit("vector", [("ts", tc)], lambda v: (
                v.reduce_max(out=SM[:, SM_MISC:SM_MISC + 1], in_=ps0[:, 0:64], axis=AX.X),
                v.reduce_max(out=SM[:, SM_MISC + 1:SM_MISC + 2], in_=ps1[:, 0:64], axis=AX.X),
                v.drain(),
                v.tensor_tensor(out=SM[:, SM_MISC:SM_MISC + 1], in0=SM[:, SM_MISC:SM_MISC + 1],
                                in1=SM[:, SM_MISC + 1:SM_MISC + 2], op=ALU.max))[-1], inc="vs")
            tc2 = p.emit("tensor", [("vs", vc)], lambda t: t.transpose(
                ps1[0:1, 0:128], SM[:, SM_MISC:SM_MISC + 1], IDF[:, :]), inc="ts")
            vc2 = p.emit("vector", [("ts", tc2)], lambda v: v.reduce_max(
                out=SM[0:1, SM_MISC + 1:SM_MISC + 2], in_=ps1[0:1, 0:128], axis=AX.X), inc="vs")
            tc3 = p.emit("tensor", [("vs", vc2)], lambda t: t.matmul(
                ps1[:, 128:129], ONEF[0:1, 0:128], SM[0:1, SM_MISC + 1:SM_MISC + 2],
                start=True, stop=True), inc="ts")
            s_c1 = p.emit("scalar", [("ts", tc3)], lambda s: s.activation(
                out=NEGC[:, 0:1], in_=ps1[:, 128:129], func=AF.Copy, scale=-1.0, bias=-CSLACK),
                inc="ss")

            if hg == 0:
                pchk("pa2b")
            e1_ev = {}
            for qg in range(4):
                pse = ps2 if qg % 2 == 0 else ps3
                pso = psx if qg % 2 == 0 else psk
                w = [("ss", s_c1), ("vs", v_vev[2]), ("vs", v_vev[3])] if qg == 0 else []
                if qg >= 2:
                    w.append(("ss", e1_ev[qg - 2]))

                def s1mm(t, qg=qg, pse=pse, pso=pso, hg=hg):
                    last = None
                    for i in range(4):
                        qt = qg * 4 + i
                        t.matmul(pse[:, i * 64:(i + 1) * 64],
                                 QBUF[0:64, qt * 128:(qt + 1) * 128],
                                 KLB[0:64, hg * 64:(hg + 1) * 64], start=True, stop=True)
                        last = t.matmul(pso[:, i * 64:(i + 1) * 64],
                                        QBUF[64:128, qt * 128:(qt + 1) * 128],
                                        KLB[64:128, hg * 64:(hg + 1) * 64], start=True, stop=True)
                    return last
                ts1 = p.emit("tensor", w, s1mm, inc="ts")
                e1_ev[qg] = p.emit("scalar", [("ts", ts1)], lambda s, qg=qg, pse=pse, pso=pso: (
                    s.activation(out=XTMPB[:, qg * 512:(qg + 1) * 512]
                                 .rearrange("p (i n) -> p i n", i=4)[:, :, 0:64],
                                 in_=pse[:, 0:256].rearrange("p (i n) -> p i n", i=4),
                                 func=AF.Exp, bias=NEGC[:, 0:1]),
                    s.activation(out=XTMPB[:, qg * 512:(qg + 1) * 512]
                                 .rearrange("p (i n) -> p i n", i=4)[:, :, 64:128],
                                 in_=pso[:, 0:256].rearrange("p (i n) -> p i n", i=4),
                                 func=AF.Exp, bias=NEGC[:, 0:1]))[-1], inc="ss")
            tb_ev = {}
            for half in range(2):
                psb = psb0 if half == 0 else psb1
                w = [("ss", e1_ev[half * 2 + 1])]
                if prev_tb is not None:
                    w.append(("vs", prev_tb[half]))

                def e1tr(t, half=half, psb=psb):
                    last = None
                    for i in range(8):
                        qt = half * 8 + i
                        last = t.transpose(psb[:, i * 128:(i + 1) * 128],
                                           XTMPB[:, qt * 128:(qt + 1) * 128], IDB[:, :])
                    return last
                tt1 = p.emit("tensor", w, e1tr, inc="ts")
                tb_ev[half] = p.emit("vector", [("ts", tt1)],
                                     lambda v, half=half, psb=psb, hg=hg: v.tensor_copy(
                    out=BIG[:, hg * 2048 + half * 1024: hg * 2048 + (half + 1) * 1024],
                    in_=psb[:, :]), inc="vs")
            prev_tb = tb_ev

            if hg == 0:
                pchk("pa2c")
            tc = p.emit("tensor", [("ss", s_c1)], lambda t, hg=hg: (
                t.matmul(ps0[0:64, 128:256], QLB[0:64, hg * 64:(hg + 1) * 64],
                         KBUF[0:64, 0:T:16], start=True, stop=True),
                t.matmul(ps0[64:128, 128:256], QLB[64:128, hg * 64:(hg + 1) * 64],
                         KBUF[64:128, 0:T:16], start=True, stop=True))[-1], inc="ts")
            vc = p.emit("vector", [("ts", tc)], lambda v: v.reduce_max(
                out=SM[:, SM_MISC:SM_MISC + 1], in_=ps0[:, 128:256], axis=AX.X), inc="vs")
            tc2 = p.emit("tensor", [("vs", vc)], lambda t: t.transpose(
                ps1[0:1, 0:128], SM[:, SM_MISC:SM_MISC + 1], IDF[:, :]), inc="ts")
            vc2 = p.emit("vector", [("ts", tc2)], lambda v: v.reduce_max(
                out=SM[0:1, SM_MISC + 1:SM_MISC + 2], in_=ps1[0:1, 0:128], axis=AX.X), inc="vs")
            tc3 = p.emit("tensor", [("vs", vc2)], lambda t: t.matmul(
                ps1[:, 128:129], ONEF[0:1, 0:128], SM[0:1, SM_MISC + 1:SM_MISC + 2],
                start=True, stop=True), inc="ts")
            s_c3 = p.emit("scalar", [("ts", tc3)], lambda s: s.activation(
                out=NEGC[:, 1:2], in_=ps1[:, 128:129], func=AF.Copy, scale=-1.0, bias=-CSLACK),
                inc="ss")

            e3_ev = {}
            for ng in range(4):
                pse = ps2 if ng % 2 == 0 else ps3
                pso = psx if ng % 2 == 0 else psk
                w = [("ss", s_c3), ("ss", e1_ev[3])] if ng == 0 else []
                if ng >= 2:
                    w.append(("ss", e3_ev[ng - 2]))

                def s3mm(t, ng=ng, pse=pse, pso=pso, hg=hg):
                    last = None
                    for i in range(4):
                        nb = ng * 4 + i
                        t.matmul(pse[:, i * 64:(i + 1) * 64],
                                 KBUF[0:64, nb * 128:(nb + 1) * 128],
                                 QLB[0:64, hg * 64:(hg + 1) * 64], start=True, stop=True)
                        last = t.matmul(pso[:, i * 64:(i + 1) * 64],
                                        KBUF[64:128, nb * 128:(nb + 1) * 128],
                                        QLB[64:128, hg * 64:(hg + 1) * 64], start=True, stop=True)
                    return last
                ts3 = p.emit("tensor", w, s3mm, inc="ts")
                e3_ev[ng] = p.emit("scalar", [("ts", ts3)], lambda s, ng=ng, pse=pse, pso=pso: (
                    s.activation(out=AE3[:, ng * 512:(ng + 1) * 512]
                                 .rearrange("p (i n) -> p i n", i=4)[:, :, 0:64],
                                 in_=pse[:, 0:256].rearrange("p (i n) -> p i n", i=4),
                                 func=AF.Exp, bias=NEGC[:, 1:2]),
                    s.activation(out=AE3[:, ng * 512:(ng + 1) * 512]
                                 .rearrange("p (i n) -> p i n", i=4)[:, :, 64:128],
                                 in_=pso[:, 0:256].rearrange("p (i n) -> p i n", i=4),
                                 func=AF.Exp, bias=NEGC[:, 1:2]))[-1], inc="ss")

            if hg == 0:
                pchk("pa2d")
            def k3vmm(t, hg=hg):
                last = None
                for hh in range(2):
                    for nb in range(NTT):
                        last = t.matmul(psk[0:64, hh * 65:(hh + 1) * 65],
                                        AE3[:, nb * 128 + hh * 64: nb * 128 + hh * 64 + 64],
                                        VBUF[:, hh * 1040 + nb * 65: hh * 1040 + (nb + 1) * 65],
                                        start=(nb == 0), stop=(nb == NTT - 1))
                return last
            tk3 = p.emit("tensor", [("ss", e3_ev[3]), ("vs", v_vev[2]), ("vs", v_vev[3]),
                                    ("vs", v_vbuf_ones)], k3vmm, inc="ts")

            def k3norm(v, hg=hg):
                v.reciprocal(out=SM[0:64, SM_MISC + 4:SM_MISC + 5], in_=psk[0:64, 64:65])
                v.reciprocal(out=SM[0:64, SM_MISC + 5:SM_MISC + 6], in_=psk[0:64, 129:130])
                v.drain()
                v.tensor_scalar(out=K3VB[0:64, (2 * hg) * 65:(2 * hg + 1) * 65],
                                in0=psk[0:64, 0:65],
                                scalar1=SM[0:64, SM_MISC + 4:SM_MISC + 5], scalar2=None,
                                op0=ALU.mult, op1=ALU.bypass)
                return v.tensor_scalar(out=K3VB[0:64, (2 * hg + 1) * 65:(2 * hg + 2) * 65],
                                       in0=psk[0:64, 65:130],
                                       scalar1=SM[0:64, SM_MISC + 5:SM_MISC + 6], scalar2=None,
                                       op0=ALU.mult, op1=ALU.bypass)
            v_k3n_all.append(p.emit("vector", [("ts", tk3)], k3norm, inc="vs"))
            hg_fence.append((p.cnt["vs"], p.cnt["ss"]))
        v_k3vb = p.cnt["vs"]
        pa2.close()
        p.flush("pa2")
        pchk("pa2")
        if "pa2" in debug:
            dbg_list += [("pa2_qbuf", QBUF), ("pa2_kbuf", KBUF), ("pa2_vbuf", VBUF),
                         ("pa2_qlb", QLB), ("pa2_klb", KLB), ("pa2_k3vb", K3VB),
                         ("pa2_big", BIG), ("pa2_negc", NEGC)]

        stA2.close()
        pa3 = ExitStack()
        SLAB = p.sb("SLAB", [64, 8192], F32, pa3)
        sK2, sKT, sA, sAT, sT1, sT2, sVN, sVNT = [SLAB[0:64, i * 1024:(i + 1) * 1024]
                                                  for i in range(8)]
        VNTB = p.sb("VNTB", [128, 1024], BF16, pa3)
        ZTMPB = p.sb("ZTMPB", [128, 520], BF16, pa3)
        psI1 = p.ps("psI1", [64, 1024], F32, pa3)
        psI2 = p.ps("psI2", [64, 1024], F32, pa3)
        psI3 = p.ps("psI3", [64, 1024], F32, pa3)
        psI4 = p.ps("psI4", [64, 1024], F32, pa3)
        E2B = XTMPF[0:64, 0:1024]

        def mcol(h):
            return SM_MISC + (h % 2) * 8 + h // 2

        def k2mm(t):
            last = None
            for h in range(NH):
                r0 = (h % 2) * 64
                hg = h // 2
                pb = psI1 if h % 2 == 0 else psI2
                last = t.matmul(pb[:, hg * 64:(hg + 1) * 64],
                                QLB[r0:r0 + 64, hg * 64:(hg + 1) * 64],
                                KLB[r0:r0 + 64, hg * 64:(hg + 1) * 64], start=True, stop=True)
            return last
        tk2 = p.emit("tensor", [("vs", v_k3vb)], k2mm, inc="ts")
        v_mx = p.emit("vector", [("ts", tk2)], lambda v: (
            v.reduce_max(out=SM[0:64, SM_MISC:SM_MISC + 8],
                         in_=psI1[:, 0:512].rearrange("p (g n) -> p g n", g=8),
                         axis=AX.X, negate=True),
            v.reduce_max(out=SM[0:64, SM_MISC + 8:SM_MISC + 16],
                         in_=psI2[:, 0:512].rearrange("p (g n) -> p g n", g=8),
                         axis=AX.X, negate=True))[-1], inc="vs")
        s_e2 = None
        for h in range(NH):
            pb2 = psI1 if h % 2 == 0 else psI2
            s_e2 = p.emit("scalar", [("vs", v_mx)], lambda s, h=h, pb2=pb2: s.activation(
                out=E2B[:, h * 64:(h + 1) * 64],
                in_=pb2[:, (h // 2) * 64:(h // 2 + 1) * 64], func=AF.Exp,
                bias=SM[0:64, mcol(h):mcol(h) + 1]), inc="ss")
        pchk("pa3a2")
        v_k2n = p.emit("vector", [("ss", s_e2)], lambda v: (
            v.reduce_sum(out=SM[0:64, SM_MISC + 16:SM_MISC + 32],
                         in_=E2B[:, :].rearrange("p (g n) -> p g n", g=16), axis=AX.X),
            v.drain(),
            v.reciprocal(out=SM[0:64, SM_MISC + 32:SM_MISC + 48],
                         in_=SM[0:64, SM_MISC + 16:SM_MISC + 32]))[-1], inc="vs")
        v_k2f = None
        for h in range(NH):
            v_k2f = p.emit("vector", [("vs", v_k2n)], lambda v, h=h: v.tensor_scalar(
                out=sK2[:, h * 64:(h + 1) * 64], in0=E2B[:, h * 64:(h + 1) * 64],
                scalar1=SM[0:64, SM_MISC + 32 + h:SM_MISC + 33 + h], scalar2=None,
                op0=ALU.mult, op1=ALU.bypass), inc="vs")

        pchk("pa3a3")
        def k2tr(t):
            last = None
            for h in range(NH):
                last = t.transpose(psI2[:, h * 64:(h + 1) * 64], sK2[:, h * 64:(h + 1) * 64],
                                   IDF[0:64, 0:64])
            return last
        tkt = p.emit("tensor", [("vs", v_k2f)], k2tr, inc="ts")
        v_kt = p.emit("vector", [("ts", tkt)], lambda v: v.tensor_copy(
            out=sKT[:, :], in_=psI2[:, :]), inc="vs")

        pchk("pa3a")
        def csmm(t):
            last = None
            for h in range(NH):
                last = t.matmul(psI3[0:1, h * 64:(h + 1) * 64], ONEF[0:64, 0:1],
                                sK2[:, h * 64:(h + 1) * 64], start=True, stop=True)
            return last
        tcs = p.emit("tensor", [("vs", v_kt)], csmm, inc="ts")
        v_cs = p.emit("vector", [("ts", tcs)], lambda v: (
            v.reduce_max(out=SM[0:1, SM_MISC + 48:SM_MISC + 64],
                         in_=psI3[0:1, :].rearrange("p (g n) -> p g n", g=16), axis=AX.X),
            v.drain(),
            v.reciprocal(out=SM[0:1, SM_MISC:SM_MISC + 16],
                         in_=SM[0:1, SM_MISC + 48:SM_MISC + 64]))[-1], inc="vs")

        def rdmm(t):
            last = None
            for h in range(NH):
                last = t.matmul(psI4[:, h:h + 1], ONEF[0:1, 0:64],
                                SM[0:1, SM_MISC + h:SM_MISC + h + 1], start=True, stop=True)
            return last
        trd = p.emit("tensor", [("vs", v_cs)], rdmm, inc="ts")
        v_rd = p.emit("vector", [("ts", trd)], lambda v: v.tensor_copy(
            out=SM[0:64, SM_MISC + 48:SM_MISC + 64], in_=psI4[:, 0:16]), inc="vs")
        v_v0 = None
        for h in range(NH):
            v_v0 = p.emit("vector", [("vs", v_rd)], lambda v, h=h: (
                v.tensor_scalar(out=sVN[:, h * 64:(h + 1) * 64], in0=sKT[:, h * 64:(h + 1) * 64],
                                scalar1=SM[0:64, SM_MISC + 48 + h:SM_MISC + 49 + h], scalar2=None,
                                op0=ALU.mult, op1=ALU.bypass),
                v.tensor_scalar(out=sVNT[:, h * 64:(h + 1) * 64], in0=sK2[:, h * 64:(h + 1) * 64],
                                scalar1=SM[0:64, SM_MISC + 48 + h:SM_MISC + 49 + h], scalar2=None,
                                op0=ALU.mult, op1=ALU.bypass))[-1], inc="vs")

        pchk("pa3b")
        def allmm(t, pso, lhs, rhs):
            last = None
            for h in range(NH):
                last = t.matmul(pso[:, h * 64:(h + 1) * 64], lhs[:, h * 64:(h + 1) * 64],
                                rhs[:, h * 64:(h + 1) * 64], start=True, stop=True)
            return last
        v_prev = v_v0
        for it in range(6):
            t1 = p.emit("tensor", [("vs", v_prev)], lambda t: (
                allmm(t, psI1, sKT, sVN), allmm(t, psI2, sVN, sKT))[-1], inc="ts")
            v1 = p.emit("vector", [("ts", t1)], lambda v: (
                v.tensor_copy(out=sA[:, :], in_=psI1[:, :]),
                v.tensor_copy(out=sAT[:, :], in_=psI2[:, :]),
                v.scalar_tensor_tensor(out=sT1[:, :], in0=CEYE[0:64, :], scalar=7.0,
                                       in1=psI1[:, :], op0=ALU.mult, op1=ALU.subtract),
                v.scalar_tensor_tensor(out=sT2[:, :], in0=CEYE[0:64, :], scalar=7.0,
                                       in1=psI2[:, :], op0=ALU.mult, op1=ALU.subtract))[-1], inc="vs")
            t2 = p.emit("tensor", [("vs", v1)], lambda t: (
                allmm(t, psI3, sAT, sT1), allmm(t, psI4, sT1, sAT))[-1], inc="ts")
            v2 = p.emit("vector", [("ts", t2)], lambda v: (
                v.scalar_tensor_tensor(out=sT1[:, :], in0=CEYE[0:64, :], scalar=15.0,
                                       in1=psI3[:, :], op0=ALU.mult, op1=ALU.subtract),
                v.scalar_tensor_tensor(out=sT2[:, :], in0=CEYE[0:64, :], scalar=15.0,
                                       in1=psI4[:, :], op0=ALU.mult, op1=ALU.subtract))[-1], inc="vs")
            t3 = p.emit("tensor", [("vs", v2)], lambda t: (
                allmm(t, psI1, sAT, sT1), allmm(t, psI2, sT1, sAT))[-1], inc="ts")
            v3 = p.emit("vector", [("ts", t3)], lambda v: (
                v.scalar_tensor_tensor(out=sT1[:, :], in0=CEYE[0:64, :], scalar=13.0,
                                       in1=psI1[:, :], op0=ALU.mult, op1=ALU.subtract),
                v.scalar_tensor_tensor(out=sT2[:, :], in0=CEYE[0:64, :], scalar=13.0,
                                       in1=psI2[:, :], op0=ALU.mult, op1=ALU.subtract))[-1], inc="vs")
            t4 = p.emit("tensor", [("vs", v3)], lambda t: (
                allmm(t, psI3, sVNT, sT1), allmm(t, psI4, sT1, sVNT))[-1], inc="ts")
            v_prev = p.emit("vector", [("ts", t4)], lambda v: (
                v.tensor_scalar(out=sVN[:, :], in0=psI3[:, :], scalar1=0.25, scalar2=None,
                                op0=ALU.mult, op1=ALU.bypass),
                v.tensor_scalar(out=sVNT[:, :], in0=psI4[:, :], scalar1=0.25, scalar2=None,
                                op0=ALU.mult, op1=ALU.bypass))[-1], inc="vs")

        pchk("pa3c")
        v_vntb = p.emit("vector", [("vs", v_prev)], lambda v: v.tensor_copy(
            out=VNTB[0:64, :], in_=sVNT[:, :]), inc="vs")

        def z2mm(t):
            last = None
            for h in range(NH):
                pso = psI1 if h % 2 == 0 else psI2
                last = t.matmul(pso[:, (h // 2) * 64:(h // 2 + 1) * 64],
                                VNTB[0:64, h * 64:(h + 1) * 64],
                                K3VB[0:64, h * 65:h * 65 + 64], start=True, stop=True)
            return last
        tz2 = p.emit("tensor", [("vs", v_vntb)], z2mm, inc="ts")
        v_z2e = p.emit("vector", [("ts", tz2)], lambda v: (
            v.memset(Z2A[:, :], 0.0),
            v.memset(ZTMPB[:, :], 0.0),
            v.drain(),
            v.tensor_copy(out=ZTMPB[0:64, :].rearrange("p (g y) -> p g y", g=NCH)[:, :, 0:64],
                          in_=psI2[:, 0:512].rearrange("p (g n) -> p g n", g=NCH)),
            v.tensor_copy(out=Z2A[0:64, :].rearrange("p (g y) -> p g y", g=NCH)[:, :, 0:64],
                          in_=psI1[:, 0:512].rearrange("p (g n) -> p g n", g=NCH)))[-1], inc="vs")
        v_z2d = p.dma("sync", [("vs", v_z2e)], Z2A[64:128, :], ZTMPB[0:64, :], "dq")
        v_ones = p.emit("vector", [("dq", v_z2d)], lambda v: v.memset(
            Z2A[:, 64:520:65], 1.0), inc="vs")
        pa3.close()
        p.flush("pa3")
        pchk("pa3")
        if "pa3" in debug:
            dbg_list += [("pa3_z2a", Z2A)]

        pa4 = ExitStack()
        SOFT = p.sb("SOFT", [128, 2048], F32, pa4)
        psSV = p.ps("psSV", [128, 2048], F32, pa4)
        psN = p.ps("psN", [128, 512], F32, pa4)
        psN2 = p.ps("psN2", [128, 512], F32, pa4)
        psN3 = p.ps("psN3", [128, 512], F32, pa4)
        psN4 = p.ps("psN4", [128, 512], F32, pa4)
        s_soft_prev = None
        v_rec_prev = None
        v_norm = {}
        for h in range(NH):
            r0 = (h % 2) * 64
            hg = h // 2

            def svmm(t, r0=r0, hg=hg):
                last = None
                for qg in range(4):
                    last = t.matmul(psSV[0:65, qg * 512:(qg + 1) * 512],
                                    Z2A[r0:r0 + 64, hg * 65:(hg + 1) * 65],
                                    BIG[r0:r0 + 64, hg * 2048 + qg * 512: hg * 2048 + (qg + 1) * 512],
                                    start=True, stop=True)
                return last
            w = [("vs", v_ones)] if h == 0 else [("ss", s_soft_prev)]
            tsv = p.emit("tensor", w, svmm, inc="ts")
            s_soft_prev = p.emit("scalar", [("ts", tsv)], lambda s: s.activation(
                out=SOFT[0:65, :], in_=psSV[0:65, :], func=AF.Copy), inc="ss")
            for qt in range(NTT):
                psn = [psN, psN2, psN3, psN4][qt % 4]
                w = [("ss", s_soft_prev)]
                idx0 = h * NTT + qt
                if idx0 >= 4:
                    w.append(("vs", v_norm[idx0 - 4]))
                tn = p.emit("tensor", w, lambda t, qt=qt, psn=psn: t.transpose(
                    psn[:, 0:65], SOFT[0:65, qt * 128:(qt + 1) * 128],
                    IDF[0:65, 0:65]), inc="ts")

                def nrm(v, qt=qt, psn=psn, h=h):
                    mc = SM_MISC + (qt % 4)
                    v.reciprocal(out=SM[:, mc:mc + 1], in_=psn[:, 64:65])
                    v.drain()
                    return v.tensor_scalar(out=XNFM[:, qt * 1024 + h * 64: qt * 1024 + (h + 1) * 64],
                                           in0=psn[:, 0:64],
                                           scalar1=SM[:, mc:mc + 1], scalar2=None,
                                           op0=ALU.mult, op1=ALU.bypass)
                v_norm[h * NTT + qt] = p.emit("vector", [("ts", tn)], nrm, inc="vs")
        v_svtm = p.cnt["vs"]
        pa4.close()
        p.flush("pa4")
        pchk("pa4")
        if "pa4" in debug:
            dbg_list += [("pa4_svtm", XNFM)]

        pa5 = ExitStack()
        psb0b = p.ps("psb0b", [128, 1024], BF16, pa5)
        psb1b = p.ps("psb1b", [128, 1024], BF16, pa5)
        psP = [p.ps("psP%d" % i, [128, 512], F32, pa5) for i in range(4)]
        v_wp = p.dma("gpsimd", [], WBUF[:, :].rearrange("p (c n) -> p c n", c=NCH),
                     wproj.rearrange("(c p) n -> p c n", p=128), "dw")
        t_tr2, v_ev2 = {}, {}
        for tt in range(NTT):
            psb = psb0b if tt % 2 == 0 else psb1b

            def trfn(t, tt=tt, psb=psb):
                last = None
                for c in range(NCH):
                    last = t.transpose(psb[:, c * 128:(c + 1) * 128],
                                       XNFM[:, tt * 1024 + c * 128: tt * 1024 + (c + 1) * 128],
                                       IDB[:, :])
                return last
            w = [("vs", v_svtm)] if tt == 0 else []
            if tt >= 2:
                w.append(("vs", v_ev2[tt - 2]))
            t_tr2[tt] = p.emit("tensor", w, trfn, inc="ts")
            v_ev2[tt] = p.emit("vector", [("ts", t_tr2[tt])], lambda v, tt=tt, psb=psb: v.tensor_copy(
                out=BIG[:, :].rearrange("p (c t) -> p c t", c=NCH)[:, :, tt * 128:(tt + 1) * 128],
                in_=psb[:, :].rearrange("p (c n) -> p c n", c=NCH)), inc="vs")
        v_pr = {}
        for tt in range(NTT):
            for og in range(2):
                k = tt * 2 + og
                pp = psP[k % 4]
                w = [("vs", v_ev2[tt])]
                if k == 0:
                    w.append(("dw", v_wp))
                if k >= 4:
                    w.append(("vs", v_pr[k - 4]))

                def pmm(t, tt=tt, og=og, pp=pp):
                    last = None
                    for c in range(NCH):
                        last = t.matmul(pp[:, :], BIG[:, c * T + tt * 128: c * T + (tt + 1) * 128],
                                        WBUF[:, c * 1024 + og * 512: c * 1024 + (og + 1) * 512],
                                        start=(c == 0), stop=(c == NCH - 1))
                    return last
                tp = p.emit("tensor", w, pmm, inc="ts")
                v_pr[k] = p.emit("vector", [("ts", tp)], lambda v, tt=tt, og=og, pp=pp, k=k: (
                                 v.drain(),
                                 v.scalar_tensor_tensor(
                    out=X[:, tt * D + og * 512: tt * D + (og + 1) * 512], in0=pp[:, :],
                    scalar=1.0, in1=X[:, tt * D + og * 512: tt * D + (og + 1) * 512],
                    op0=ALU.mult, op1=ALU.add,
                    accum_out=SM[:, SM_A + k:SM_A + k + 1]))[-1], inc="vs")
        v_x1 = p.cnt["vs"]
        pa5.close()
        stA.close()
        p.emit("vector", [("vs", v_x1)], lambda v: v.tensor_tensor(
            out=SM[:, SM_SUM:SM_SUM + 16], in0=SM[:, SM_A:SM_A + 32:2],
            in1=SM[:, SM_A + 1:SM_A + 32:2], op=ALU.add), inc="vs")
        p.flush("pa5")
        pchk("pa5")
        if "pa5" in debug:
            dbg_list += [("pa5_x", X)]

        def phase_ffnq(tag, accum_for_next):
            st = ExitStack()
            WQT = p.sb(tag + "WQT", [128, 8192], BF16, st)
            XQB = p.sb(tag + "XQB", [128, 2048], BF16, st)
            WSB = p.sb(tag + "WSB", [128, 1024], F32, st)
            BB = p.sb(tag + "BB", [128, 1024], F32, st)
            ROWS = p.sb(tag + "ROWS", [128, 16], F32, st)
            ps0 = p.ps(tag + "ps0", [128, 512], F32, st)
            ps1 = p.ps(tag + "ps1", [128, 512], F32, st)
            ps2 = p.ps(tag + "ps2", [128, 512], F32, st)
            ps3 = p.ps(tag + "ps3", [128, 512], F32, st)
            psq = p.ps(tag + "psq", [128, 1024], BF16, st)
            psq2 = p.ps(tag + "psq2", [128, 1024], BF16, st)
            stq = ExitStack()
            FP32S = p.sb(tag + "FP32S", [128, 2048], F32, stq)

            v_fw = {0: p.dma("sync", [], FP32S[:, 0:1024], fw[0:128, :], "dh0")}
            v_amax, t_wq = {}, {}
            for c in range(NCH):
                slot = c % 2
                if c + 1 < NCH:
                    wd = [("vs", v_amax[c - 1])] if c >= 1 else []
                    v_fw[c + 1] = p.dma("sync", wd, FP32S[:, (1 - slot) * 1024:(2 - slot) * 1024],
                                        fw[(c + 1) * 128:(c + 2) * 128, :], "dh0" if (c + 1) % 2 == 0 else "dh1")
                src = FP32S[:, slot * 1024:(slot + 1) * 1024]
                w = [("dh0" if c % 2 == 0 else "dh1", v_fw[c])]
                if c >= 2:
                    w.append(("ts", t_wq[c - 2]))
                v_amax[c] = p.emit("vector", w, lambda v, c=c, src=src: (
                    v.reduce_max(out=ROWS[:, c:c + 1], in_=src, axis=AX.X,
                                 apply_absolute_value=True),
                    v.drain(),
                    v.reciprocal(out=SM[:, SM_MISC + (c % 2):SM_MISC + (c % 2) + 1],
                                 in_=ROWS[:, c:c + 1]),
                    v.drain(),
                    v.tensor_scalar(out=ROWS[:, 8 + c:9 + c],
                                    in0=SM[:, SM_MISC + (c % 2):SM_MISC + (c % 2) + 1],
                                    scalar1=127.0, scalar2=None, op0=ALU.mult, op1=ALU.bypass),
                    v.drain(),
                    v.tensor_scalar(out=XTMPF[:, 0:1024], in0=src,
                                    scalar1=ROWS[:, 8 + c:9 + c], scalar2=MAGIC,
                                    op0=ALU.mult, op1=ALU.add),
                    v.drain(),
                    v.tensor_scalar(out=XQB[:, 0:1024], in0=XTMPF[:, 0:1024],
                                    scalar1=-MAGIC, scalar2=None, op0=ALU.add, op1=ALU.bypass)
                    )[-1], inc="vs")
                tq = p.emit("tensor", [("vs", v_amax[c]), ("dc", v_consts)],
                            lambda t, slot=slot: [
                    t.transpose((psq if slot == 0 else psq2)[:, j * 128:(j + 1) * 128],
                                XQB[:, j * 128:(j + 1) * 128], IDB[:, :])
                    for j in range(NCH)][-1], inc="ts")
                t_wq[c] = tq
                p.emit("vector", [("ts", tq)], lambda v, c=c, slot=slot: v.tensor_copy(
                    out=WQT[:, :].rearrange("p (j n) -> p j n", j=NCH)[:, :, c * 128:(c + 1) * 128],
                    in_=(psq if slot == 0 else psq2)[:, :].rearrange("p (j n) -> p j n", j=NCH)),
                    inc="vs")
            v_wqt = p.cnt["vs"]
            stq.close()
            WSR = p.sb(tag + "WSR", [1, 2048], F32, st)
            SOFT3 = p.sb(tag + "SOFT3", [128, 1024], F32, st)
            def wsrow(t):
                last = None
                for c in range(NCH):
                    pb = ps2 if c < 4 else ps3
                    last = t.transpose(pb[0:1, (c % 4) * 128:((c % 4) + 1) * 128],
                                       ROWS[:, c:c + 1], IDF[:, :])
                return last
            tws = p.emit("tensor", [("vs", v_amax[NCH - 1])], wsrow, inc="ts")
            v_wsr = p.emit("vector", [("ts", tws)], lambda v: (
                v.tensor_copy(out=WSR[0:1, 0:512], in_=ps2[0:1, 0:512]),
                v.tensor_copy(out=WSR[0:1, 512:1024], in_=ps3[0:1, 0:512]))[-1], inc="vs")
            v_fb = p.dma("sync", [("dh0", v_fw[6]), ("dh1", v_fw[7]), ("vs", v_amax[NCH - 1])], WSR[0:1, 1024:2048], fb, "dc")

            def bcasts(t):
                t.matmul(ps0[:, :], ONEF[0:1, 0:128], WSR[0:1, 0:512], start=True, stop=True)
                t.matmul(ps1[:, :], ONEF[0:1, 0:128], WSR[0:1, 512:1024], start=True, stop=True)
                t.matmul(ps2[:, :], ONEF[0:1, 0:128], WSR[0:1, 1024:1536], start=True, stop=True)
                return t.matmul(ps3[:, :], ONEF[0:1, 0:128], WSR[0:1, 1536:2048],
                                start=True, stop=True)
            tbc = p.emit("tensor", [("vs", v_wsr), ("dc", v_fb)], bcasts, inc="ts")
            v_bc = p.emit("vector", [("ts", tbc)], lambda v: (
                v.tensor_scalar(out=WSB[:, 0:512], in0=ps0[:, :], scalar1=1.0 / 127.0,
                                scalar2=None, op0=ALU.mult, op1=ALU.bypass),
                v.tensor_scalar(out=WSB[:, 512:1024], in0=ps1[:, :], scalar1=1.0 / 127.0,
                                scalar2=None, op0=ALU.mult, op1=ALU.bypass),
                v.tensor_copy(out=BB[:, 0:512], in_=ps2[:, :]),
                v.tensor_copy(out=BB[:, 512:1024], in_=ps3[:, :]))[-1], inc="vs")

            v_inv2 = ln_stats(have_sums=True)
            v_xq_ev, v_res = {}, {}
            for tt in range(NTT):
                slot = tt % 2
                xq = XQB[:, slot * 1024:(slot + 1) * 1024]
                xf = XTMPF[:, slot * 1024:(slot + 1) * 1024]
                w = [("vs", v_inv2), ("vs", v_bc), ("vs", v_wqt)]
                if tt >= 2:
                    w.append(("ts", v_xq_ev[tt - 2]))
                rot = tt % 2
                va = p.emit("vector", w, lambda v, tt=tt, xf=xf, xq=xq, rot=rot: (
                    v.tensor_scalar(out=xf, in0=X[:, tt * D:(tt + 1) * D],
                                    scalar1=SM[:, SM_INV + tt:SM_INV + tt + 1],
                                    scalar2=SM[:, SM_NMI + tt:SM_NMI + tt + 1],
                                    op0=ALU.mult, op1=ALU.add),
                    v.drain(),
                    v.reduce_max(out=SM[:, SM_AM + rot:SM_AM + rot + 1], in_=xf, axis=AX.X,
                                 apply_absolute_value=True),
                    v.drain(),
                    v.reciprocal(out=SM[:, SM_RA + rot:SM_RA + rot + 1],
                                 in_=SM[:, SM_AM + rot:SM_AM + rot + 1]),
                    v.drain(),
                    v.tensor_scalar(out=SM[:, SM_XS + tt:SM_XS + tt + 1],
                                    in0=SM[:, SM_AM + rot:SM_AM + rot + 1],
                                    scalar1=1.0 / 127.0, scalar2=None, op0=ALU.mult, op1=ALU.bypass),
                    v.tensor_scalar(out=SM[:, SM_R127 + rot:SM_R127 + rot + 1],
                                    in0=SM[:, SM_RA + rot:SM_RA + rot + 1],
                                    scalar1=127.0, scalar2=None, op0=ALU.mult, op1=ALU.bypass),
                    v.drain(),
                    v.tensor_scalar(out=xf, in0=xf, scalar1=SM[:, SM_R127 + rot:SM_R127 + rot + 1],
                                    scalar2=MAGIC, op0=ALU.mult, op1=ALU.add),
                    v.drain(),
                    v.tensor_scalar(out=xq, in0=xf, scalar1=-MAGIC, scalar2=None,
                                    op0=ALU.add, op1=ALU.bypass))[-1], inc="vs")
                tq = p.emit("tensor", [("vs", va)], lambda t, xq=xq, slot=slot: [
                    t.transpose((psq if slot == 0 else psq2)[:, j * 128:(j + 1) * 128],
                                xq[:, j * 128:(j + 1) * 128], IDB[:, :]) for j in range(NCH)][-1],
                    inc="ts")
                v_xq_ev[tt] = tq
                v_xq = p.emit("vector", [("ts", tq)], lambda v, tt=tt, slot=slot: v.tensor_copy(
                    out=XNFM[:, :].rearrange("p (c t) -> p c t", c=NCH)[:, :, tt * 128:(tt + 1) * 128],
                    in_=(psq if slot == 0 else psq2)[:, :].rearrange("p (j n) -> p j n", j=NCH)),
                    inc="vs")
                for og in range(2):
                    m2 = tt * 2 + og
                    pp = [ps0, ps1, ps2, ps3][m2 % 4]
                    sbf = SOFT2 if (m2 % 4) < 2 else SOFT3
                    w2 = [("vs", v_xq)]
                    if m2 >= 4:
                        w2.append(("vs", v_res[m2 - 4]))

                    def qmm(t, tt=tt, pp=pp, og=og):
                        last = None
                        for j in range(NCH):
                            last = t.matmul(pp[:, :],
                                            XNFM[:, j * T + tt * 128: j * T + (tt + 1) * 128],
                                            WQT[:, j * 1024 + og * 512: j * 1024 + (og + 1) * 512],
                                            start=(j == 0), stop=(j == NCH - 1))
                        return last
                    tmm = p.emit("tensor", w2, qmm, inc="ts")
                    s1 = p.emit("scalar", [("ts", tmm)], lambda s, pp=pp, tt=tt, og=og, sbf=sbf:
                                s.activation(
                        out=sbf[:, og * 512:(og + 1) * 512], in_=pp[:, :], func=AF.Identity,
                        scale=SM[:, SM_XS + tt:SM_XS + tt + 1], bias=CZCOL[:, 0:1]), inc="ss")

                    def resf(v, tt=tt, og=og, sbf=sbf):
                        sl = slice(og * 512, (og + 1) * 512)
                        xsl = slice(tt * D + og * 512, tt * D + (og + 1) * 512)
                        v.tensor_tensor(out=sbf[:, sl], in0=sbf[:, sl], in1=WSB[:, sl],
                                        op=ALU.mult)
                        v.drain()
                        v.tensor_tensor(out=sbf[:, sl], in0=sbf[:, sl], in1=BB[:, sl],
                                        op=ALU.add)
                        v.drain()
                        if accum_for_next:
                            return v.scalar_tensor_tensor(
                                out=X[:, xsl], in0=sbf[:, sl], scalar=1.0, in1=X[:, xsl],
                                op0=ALU.mult, op1=ALU.add,
                                accum_out=SM[:, SM_A + tt * 2 + og:SM_A + tt * 2 + og + 1])
                        v.drain()
                        return v.tensor_tensor(out=X[:, xsl], in0=sbf[:, sl], in1=X[:, xsl],
                                               op=ALU.add)
                    v_res[tt * 2 + og] = p.emit("vector", [("ss", s1)], resf, inc="vs")
            if accum_for_next:
                p.emit("vector", [("vs", p.cnt["vs"])], lambda v: v.tensor_tensor(
                    out=SM[:, SM_SUM:SM_SUM + 16], in0=SM[:, SM_A:SM_A + 32:2],
                    in1=SM[:, SM_A + 1:SM_A + 32:2], op=ALU.add), inc="vs")
            st.close()
            p.flush(tag)

        if nphases >= 2:
            phase_ffnq("pb", accum_for_next=True)
            if "pb" in debug:
                dbg_list += [("pb_x", X)]

        def phase_attnb(pccut=None):
            stC = ExitStack()
            T2K = p.sb("T2K", [128, T], F32, stC)
            NEGC = p.sb("cNEGC", [128, 4], F32, stC)
            GB = p.sb("cGB", [128, 64], F32, stC)
            B1C = p.sb("cB1C", [128, 32], F32, stC)
            BVR = p.sb("cBVR", [128, 1024], BF16, stC)

            p.dma("sync", [], T2K[:, :], c_t2, "dc")
            v_t2 = p.cnt["dc"]

            v_sp = p.dma("sync", [("vs", p.cnt["vs"]), ("ss", p.cnt["ss"])],
                         spill.rearrange("(t p) d -> p t d", p=128),
                         X[:, :].rearrange("p (t d) -> p t d", t=NTT), "dx")
            cut = [False]
            def cuts(name, *tens):
                if pccut == name:
                    for i, tn in enumerate(tens):
                        dbg_list.append((name + "_%d" % i, tn if isinstance(tn, bass.AP) else tn[:, :]))
                    cut[0] = True
                return cut[0]
            v_inv3 = ln_stats(have_sums=True)
            v_z = {}
            for tt in range(NTT):
                w = [("vs", v_inv3)]
                if tt == 0:
                    w.append(("dx", v_sp))
                v_z[tt] = p.emit("vector", w, lambda v, tt=tt: v.tensor_scalar(
                    out=X[:, tt * D:(tt + 1) * D], in0=X[:, tt * D:(tt + 1) * D],
                    scalar1=SM[:, SM_INV + tt:SM_INV + tt + 1],
                    scalar2=SM[:, SM_NMI + tt:SM_NMI + tt + 1],
                    op0=ALU.mult, op1=ALU.add), inc="vs")
            p.flush("pc0")
            if cuts("pc0", X):
                return
            for l in range(2):
                wpar = [("vs", p.cnt["vs"]), ("ss", p.cnt["ss"])] if l > 0 else []
                for i, (src, col) in enumerate([(cln1g, 0), (cln1b, 8), (cln2g, 16), (cln2b, 24),
                                                (cbq, 32), (cbk, 40)]):
                    p.dma("sync", wpar if i == 0 else [], GB[:, col:col + 8], src[l], "dc")
                p.dma("sync", [], B1C[:, :], cb1[l], "dc")
                v_par = p.cnt["dc"]
                v_bvr0 = p.emit("vector", [("vs", p.cnt["vs"])], lambda v: v.memset(
                    BVR[:, :], 0.0), inc="vs")
                v_bvr = p.dma("gpsimd", [("ss", p.cnt["ss"]), ("vs", v_bvr0)],
                              BVR[0:1, :], bbv[l:l + 1, :], "dw")
                v_bqs = p.emit("vector", [("dc", v_par)], lambda v: v.tensor_scalar(
                    out=GB[:, 48:56], in0=GB[:, 32:40], scalar1=0.125, scalar2=None,
                    op0=ALU.mult, op1=ALU.bypass), inc="vs")

                stP = ExitStack()
                psF1 = p.ps("psF1", [128, 1024], F32, stP)
                psF2 = p.ps("psF2", [128, 1024], F32, stP)
                v_inva = ln_stats(have_sums=False)
                t_f = {}
                v_fe = {}
                for tt in range(NTT):
                    slot = tt % 2
                    xf = XTMPF[:, slot * 1024:(slot + 1) * 1024]
                    w = [("vs", v_inva)]
                    if tt >= 2:
                        w.append(("ts", t_f[tt - 2]))
                    va = p.emit("vector", w, lambda v, tt=tt, xf=xf: v.tensor_scalar(
                        out=xf, in0=X[:, tt * D:(tt + 1) * D],
                        scalar1=SM[:, SM_INV + tt:SM_INV + tt + 1],
                        scalar2=SM[:, SM_NMI + tt:SM_NMI + tt + 1],
                        op0=ALU.mult, op1=ALU.add), inc="vs")

                    def tf(t, xf=xf, slot=slot):
                        psf = psF1 if slot == 0 else psF2
                        last = None
                        for c in range(NCH):
                            last = t.transpose(psf[:, (c % 4) * 256 + (c // 4) * 128:
                                                   (c % 4) * 256 + (c // 4) * 128 + 128]
                                               if False else psf[:, c * 128:(c + 1) * 128],
                                               xf[:, c * 128:(c + 1) * 128], IDF[:, :])
                        return last
                    w2 = [("vs", va)]
                    if tt >= 2:
                        w2.append(("vs", v_fe[tt - 2]))
                    t_f[tt] = p.emit("tensor", w2, tf, inc="ts")

                    def fe(v, tt=tt, slot=slot):
                        psf = psF1 if slot == 0 else psF2
                        last = None
                        for c in range(NCH):
                            last = v.tensor_scalar(
                                out=XNFM[:, c * T + tt * 128: c * T + (tt + 1) * 128],
                                in0=psf[:, c * 128:(c + 1) * 128],
                                scalar1=GB[:, 0 + c:1 + c], scalar2=GB[:, 8 + c:9 + c],
                                op0=ALU.mult, op1=ALU.add)
                        return last
                    v_fe[tt] = p.emit("vector", [("ts", t_f[tt])], fe, inc="vs")
                stP.close()
                p.flush("pc1_l%d" % l)
                if cuts("pc1", XNFM):
                    return

                stL = ExitStack()
                QBUF = p.sb("cQBUF", [128, T], BF16, stL)
                KBUF = p.sb("cKBUF", [128, T], BF16, stL)
                VBUF = p.sb("cVBUF", [128, 2 * 1040], BF16, stL)
                UT = p.sb("cUT", [128, 4096], BF16, stL)
                SOFT = p.sb("cSOFT", [128, 2048], F32, stL)
                stP = ExitStack()
                ps0 = p.ps("cps0", [128, 512], F32, stP)
                ps1 = p.ps("cps1", [128, 512], F32, stP)
                ps2 = p.ps("cps2", [128, 512], F32, stP)
                ps3 = p.ps("cps3", [128, 512], F32, stP)
                psPV = p.ps("cpsPV", [65, 2048], F32, stP)

                def dma_qkv_b(hg, slot, waits):
                    res = []
                    for vi, wsrc in enumerate([bwq, bwk, bwv]):
                        sem = ("dwa", "dwb", "dwc")[vi]
                        res.append((sem, p.dma("gpsimd", waits if vi == 0 else [],
                                    WBUF[:, slot * 4096 + vi * 1024: slot * 4096 + (vi + 1) * 1024]
                                    .rearrange("p (c n) -> p c n", c=NCH),
                                    wsrc[l].rearrange("(c p) n -> p c n", p=128)
                                    [:, :, hg * 128:(hg + 1) * 128], sem)))
                    return res
                v_w = {0: dma_qkv_b(0, 0, [("dw", v_bvr)])}
                fence = []
                v_norm = {}
                prev_soft = None
                for hg in range(NCH):
                    slot = hg % 2
                    WQ = WBUF[:, slot * 4096: slot * 4096 + 1024]
                    WK = WBUF[:, slot * 4096 + 1024: slot * 4096 + 2048]
                    WV = WBUF[:, slot * 4096 + 2048: slot * 4096 + 3072]
                    qk_ev = {}
                    for tg in range(4):
                        pq = ps0 if tg % 2 == 0 else ps1
                        pk = ps2 if tg % 2 == 0 else ps3
                        w = []
                        if tg == 0:
                            w += list(v_w[hg]) + [("vs", v_bqs)]
                            if fence:
                                w += [("vs", fence[-1][0]), ("ss", fence[-1][1])]
                        if tg >= 2:
                            w.append(("vs", qk_ev[tg - 2]))

                        def mmqk(t, tg=tg, pq=pq, pk=pk, WQ=WQ, WK=WK):
                            for c in range(NCH):
                                t.matmul(pq[:, :], WQ[:, c * 128:(c + 1) * 128],
                                         XNFM[:, c * T + tg * 512: c * T + (tg + 1) * 512],
                                         start=(c == 0), stop=(c == NCH - 1))
                            last = None
                            for c in range(NCH):
                                last = t.matmul(pk[:, :], WK[:, c * 128:(c + 1) * 128],
                                                XNFM[:, c * T + tg * 512: c * T + (tg + 1) * 512],
                                                start=(c == 0), stop=(c == NCH - 1))
                            return last
                        tq = p.emit("tensor", w, mmqk, inc="ts")
                        sq = p.emit("scalar", [("ts", tq)], lambda s, tg=tg, pq=pq, pk=pk, hg=hg: (
                            s.activation(out=XTMPF[:, (tg % 2) * 512:(tg % 2) * 512 + 512],
                                         in_=pq[:, :], func=AF.Identity, scale=0.125,
                                         bias=GB[:, 48 + hg:49 + hg]),
                            s.activation(out=XTMPF[:, 1024 + (tg % 2) * 512:1024 + (tg % 2) * 512 + 512],
                                         in_=pk[:, :], func=AF.Identity, scale=1.0,
                                         bias=GB[:, 40 + hg:41 + hg]))[-1], inc="ss")
                        qk_ev[tg] = p.emit("vector", [("ss", sq)], lambda v, tg=tg: (
                            v.tensor_tensor(out=QBUF[:, tg * 512:(tg + 1) * 512],
                                            in0=XTMPF[:, (tg % 2) * 512:(tg % 2) * 512 + 512],
                                            in1=T2K[:, tg * 512:(tg + 1) * 512], op=ALU.mult),
                            v.tensor_tensor(out=KBUF[:, tg * 512:(tg + 1) * 512],
                                            in0=XTMPF[:, 1024 + (tg % 2) * 512:1024 + (tg % 2) * 512 + 512],
                                            in1=T2K[:, tg * 512:(tg + 1) * 512], op=ALU.mult))[-1],
                            inc="vs")
                    v_vev = {}
                    for g in range(4):
                        pv = ps0 if g % 2 == 0 else ps1
                        w = [("ss", qk_ev[3] if False else qk_ev[3])] if g == 0 else []
                        if g == 0:
                            w = [("vs", qk_ev[3])]
                        if g >= 2:
                            w.append(("vs", v_vev[g - 2]))

                        def mmv(t, g=g, pv=pv, WV=WV, hg=hg):
                            last = None
                            for i in range(4):
                                tt = g * 4 + i
                                for c in range(NCH):
                                    t.matmul(pv[:, i * 128:(i + 1) * 128],
                                             XNFM[:, c * T + tt * 128: c * T + (tt + 1) * 128],
                                             WV[:, c * 128:(c + 1) * 128],
                                             start=(c == 0), stop=False)
                                last = t.matmul(pv[:, i * 128:(i + 1) * 128], ONEB[0:1, 0:128],
                                                BVR[0:1, hg * 128:(hg + 1) * 128],
                                                start=False, stop=True)
                            return last
                        tv = p.emit("tensor", w, mmv, inc="ts")

                        def vev(v, g=g, pv=pv):
                            v.tensor_copy(
                                out=VBUF[:, 0:1040].rearrange("p (t y) -> p t y", t=NTT)
                                [:, g * 4:(g + 1) * 4, 0:64],
                                in_=pv[:, :].rearrange("p (t n) -> p t n", t=4)[:, :, 0:64])
                            return v.tensor_copy(
                                out=VBUF[:, 1040:2080].rearrange("p (t y) -> p t y", t=NTT)
                                [:, g * 4:(g + 1) * 4, 0:64],
                                in_=pv[:, :].rearrange("p (t n) -> p t n", t=4)[:, :, 64:128])
                        v_vev[g] = p.emit("vector", [("ts", tv)], vev, inc="vs")
                    if hg == 0:
                        p.emit("vector", [], lambda v: (
                            v.memset(VBUF[:, 64:1040:65], 1.0),
                            v.memset(VBUF[:, 1104:2080:65], 1.0))[-1], inc="vs")
                    if hg + 1 < NCH:
                        v_w[hg + 1] = dma_qkv_b(hg + 1, (hg + 1) % 2,
                                                [("vs", qk_ev[3]), ("vs", v_vev[3])])

                    for hh in range(2):
                        h = hg * 2 + hh
                        r0 = hh * 64
                        w = [("vs", qk_ev[3]), ("vs", v_vev[3]), ("vs", v_vev[2])] if hh == 0 else \
                            [("ss", prev_soft)]
                        tc = p.emit("tensor", w, lambda t, r0=r0: t.matmul(
                            ps2[:, 0:128], KBUF[r0:r0 + 64, 0:T:16], QBUF[r0:r0 + 64, 0:T:16],
                            start=True, stop=True), inc="ts")
                        vc = p.emit("vector", [("ts", tc)], lambda v: v.reduce_max(
                            out=SM[:, SM_MISC:SM_MISC + 1], in_=ps2[:, 0:128], axis=AX.X), inc="vs")
                        tc2 = p.emit("tensor", [("vs", vc)], lambda t: t.transpose(
                            ps3[0:1, 0:128], SM[:, SM_MISC:SM_MISC + 1], IDF[:, :]), inc="ts")
                        vc2 = p.emit("vector", [("ts", tc2)], lambda v: v.reduce_max(
                            out=SM[0:1, SM_MISC + 1:SM_MISC + 2], in_=ps3[0:1, 0:128], axis=AX.X),
                            inc="vs")
                        tc3 = p.emit("tensor", [("vs", vc2)], lambda t: t.matmul(
                            ps3[:, 128:129], ONEF[0:1, 0:128], SM[0:1, SM_MISC + 1:SM_MISC + 2],
                            start=True, stop=True), inc="ts")
                        s_c = p.emit("scalar", [("ts", tc3)], lambda s, hh=hh: s.activation(
                            out=NEGC[:, hh:hh + 1], in_=ps3[:, 128:129], func=AF.Copy,
                            scale=-1.0, bias=-CSLACK), inc="ss")

                        e_ev = {}
                        for kb in range(NTT):
                            uts = UT[:, (kb % 2) * 2048:(kb % 2 + 1) * 2048]
                            for qg in range(4):
                                m = kb * 4 + qg
                                pse2 = [ps2, ps3, ps0, ps1][m % 4]
                                w2 = [("ss", s_c)] if m == 0 else []
                                if m < 4 and h > 0:
                                    w2.append(("vs", v_norm[h * NTT - 1]))
                                if m >= 4:
                                    w2.append(("ss", e_ev[m - 4]))
                                tsm = p.emit("tensor", w2, lambda t, kb=kb, qg=qg, pse2=pse2, r0=r0:
                                             t.matmul(pse2[:, :],
                                                      KBUF[r0:r0 + 64, kb * 128:(kb + 1) * 128],
                                                      QBUF[r0:r0 + 64, qg * 512:(qg + 1) * 512],
                                                      start=True, stop=True), inc="ts")
                                e_ev[m] = p.emit("scalar", [("ts", tsm)],
                                                 lambda s, pse2=pse2, uts=uts, qg=qg, hh=hh:
                                                 s.activation(
                                    out=uts[:, qg * 512:(qg + 1) * 512], in_=pse2[:, :], func=AF.Exp,
                                    bias=NEGC[:, hh:hh + 1]), inc="ss")

                            def pvmm(t, kb=kb, uts=uts, hh=hh):
                                last = None
                                for qg in range(4):
                                    last = t.matmul(psPV[0:65, qg * 512:(qg + 1) * 512],
                                                    VBUF[:, hh * 1040 + kb * 65:
                                                         hh * 1040 + (kb + 1) * 65],
                                                    uts[:, qg * 512:(qg + 1) * 512],
                                                    start=(kb == 0), stop=(kb == NTT - 1))
                                return last
                            p.emit("tensor", [("ss", e_ev[kb * 4 + 3])], pvmm, inc="ts")
                        tpv = p.cnt["ts"]
                        s_soft = p.emit("scalar", [("ts", tpv)], lambda s: s.activation(
                            out=SOFT[0:65, :], in_=psPV[0:65, :], func=AF.Copy), inc="ss")
                        prev_soft = s_soft
                        for qt in range(NTT):
                            psn = ps0 if qt % 2 == 0 else ps1
                            w3 = [("ss", s_soft)]
                            if qt == 0 and hh == 0:
                                w3 += [("vs", v_vev[3]), ("vs", v_vev[2])]
                            idx0 = h * NTT + qt
                            if idx0 - 2 >= 0 and (idx0 - 2) in v_norm:
                                w3.append(("vs", v_norm[idx0 - 2]))
                            tn = p.emit("tensor", w3, lambda t, qt=qt, psn=psn: t.transpose(
                                psn[:, 0:65], SOFT[0:65, qt * 128:(qt + 1) * 128],
                                IDF[0:65, 0:65]), inc="ts")

                            def nrm(v, qt=qt, psn=psn, h=h):
                                mc = SM_MISC + 2 + (qt % 4)
                                v.reciprocal(out=SM[:, mc:mc + 1], in_=psn[:, 64:65])
                                v.drain()
                                return v.tensor_scalar(
                                    out=BIG[:, qt * 1024 + h * 64: qt * 1024 + (h + 1) * 64],
                                    in0=psn[:, 0:64],
                                    scalar1=SM[:, mc:mc + 1], scalar2=None,
                                    op0=ALU.mult, op1=ALU.bypass)
                            v_norm[h * NTT + qt] = p.emit("vector", [("ts", tn)], nrm, inc="vs")
                    fence.append((p.cnt["vs"], p.cnt["ss"]))
                v_att = p.cnt["vs"]
                v_wo = p.dma("gpsimd", [("vs", fence[-1][0])],
                             WBUF[:, :].rearrange("p (c n) -> p c n", c=NCH),
                             bwo[l].rearrange("(c p) n -> p c n", p=128), "dw")
                stP.close()
                stL.close()
                p.flush("pc23_l%d" % l)
                if cuts("pc3", BIG):
                    return

                stP = ExitStack()
                psb0c = p.ps("cpsb0", [128, 1024], BF16, stP)
                psb1c = p.ps("cpsb1", [128, 1024], BF16, stP)
                psW = [p.ps("cpsW%d" % i, [128, 512], F32, stP) for i in range(4)]
                t_tr3, v_ev3, v_wr = {}, {}, {}
                for tt in range(NTT):
                    psb = psb0c if tt % 2 == 0 else psb1c

                    def trf(t, tt=tt, psb=psb):
                        last = None
                        for c in range(NCH):
                            last = t.transpose(psb[:, c * 128:(c + 1) * 128],
                                               BIG[:, tt * 1024 + c * 128: tt * 1024 + (c + 1) * 128],
                                               IDB[:, :])
                        return last
                    w = [("vs", v_att)] if tt == 0 else []
                    if tt >= 2:
                        w.append(("vs", v_ev3[tt - 2]))
                    t_tr3[tt] = p.emit("tensor", w, trf, inc="ts")
                    v_ev3[tt] = p.emit("vector", [("ts", t_tr3[tt])], lambda v, tt=tt, psb=psb:
                                       v.tensor_copy(
                        out=XNFM[:, :].rearrange("p (c t) -> p c t", c=NCH)[:, :, tt * 128:(tt + 1) * 128],
                        in_=psb[:, :].rearrange("p (c n) -> p c n", c=NCH)), inc="vs")
                for tt in range(NTT):
                    for og in range(2):
                        k = tt * 2 + og
                        pp = psW[k % 4]
                        w = [("vs", v_ev3[tt])]
                        if k == 0:
                            w.append(("dw", v_wo))
                        if k >= 4:
                            w.append(("vs", v_wr[k - 4]))

                        def wmm(t, tt=tt, og=og, pp=pp):
                            last = None
                            for c in range(NCH):
                                last = t.matmul(pp[:, :],
                                                XNFM[:, c * T + tt * 128: c * T + (tt + 1) * 128],
                                                WBUF[:, c * 1024 + og * 512: c * 1024 + (og + 1) * 512],
                                                start=(c == 0), stop=(c == NCH - 1))
                            return last
                        tw = p.emit("tensor", w, wmm, inc="ts")
                        v_wr[k] = p.emit("vector", [("ts", tw)], lambda v, tt=tt, og=og, pp=pp, k=k: (
                            v.drain(),
                            v.scalar_tensor_tensor(
                                out=X[:, tt * D + og * 512: tt * D + (og + 1) * 512], in0=pp[:, :],
                                scalar=1.0, in1=X[:, tt * D + og * 512: tt * D + (og + 1) * 512],
                                op0=ALU.mult, op1=ALU.add,
                                accum_out=SM[:, SM_A + k:SM_A + k + 1]))[-1], inc="vs")
                p.emit("vector", [("vs", p.cnt["vs"])], lambda v: v.tensor_tensor(
                    out=SM[:, SM_SUM:SM_SUM + 16], in0=SM[:, SM_A:SM_A + 32:2],
                    in1=SM[:, SM_A + 1:SM_A + 32:2], op=ALU.add), inc="vs")
                stP.close()
                p.flush("pc4_l%d" % l)
                if cuts("pc4", X):
                    return

                v_invb = ln_stats(have_sums=True)
                stP = ExitStack()
                psF1 = p.ps("dpsF1", [128, 1024], F32, stP)
                psF2 = p.ps("dpsF2", [128, 1024], F32, stP)
                psH0 = p.ps("dpsH0", [128, 512], F32, stP)
                psH1 = p.ps("dpsH1", [128, 512], F32, stP)
                HID2 = p.sb("cHID2", [128, 16384], BF16, stP)

                def hid_t(hc):
                    buf = BIG if hc < 16 else HID2
                    return buf[:, (hc % 16) * 1024:(hc % 16 + 1) * 1024]

                for half in range(2):
                    t_f2, v_fe2 = {}, {}
                    for i in range(8):
                        tt = half * 8 + i
                        slot = i % 2
                        xf = XTMPF[:, slot * 1024:(slot + 1) * 1024]
                        w = [("vs", v_invb)]
                        if i >= 2:
                            w.append(("ts", t_f2[i - 2]))
                        va = p.emit("vector", w, lambda v, tt=tt, xf=xf: v.tensor_scalar(
                            out=xf, in0=X[:, tt * D:(tt + 1) * D],
                            scalar1=SM[:, SM_INV + tt:SM_INV + tt + 1],
                            scalar2=SM[:, SM_NMI + tt:SM_NMI + tt + 1],
                            op0=ALU.mult, op1=ALU.add), inc="vs")

                        def tf2(t, xf=xf, slot=slot):
                            psf = psF1 if slot == 0 else psF2
                            last = None
                            for c in range(NCH):
                                last = t.transpose(psf[:, c * 128:(c + 1) * 128],
                                                   xf[:, c * 128:(c + 1) * 128], IDF[:, :])
                            return last
                        w2 = [("vs", va)]
                        if i >= 2:
                            w2.append(("vs", v_fe2[i - 2]))
                        t_f2[i] = p.emit("tensor", w2, tf2, inc="ts")

                        def fe2(v, i=i, slot=slot, half=half):
                            psf = psF1 if slot == 0 else psF2
                            last = None
                            for c in range(NCH):
                                last = v.tensor_scalar(
                                    out=XNFM[:, c * T + half * 1024 + i * 128:
                                             c * T + half * 1024 + (i + 1) * 128],
                                    in0=psf[:, c * 128:(c + 1) * 128],
                                    scalar1=GB[:, 16 + c:17 + c], scalar2=GB[:, 24 + c:25 + c],
                                    op0=ALU.mult, op1=ALU.add)
                            return last
                        v_fe2[i] = p.emit("vector", [("ts", t_f2[i])], fe2, inc="vs")
                    v_xl2 = p.cnt["vs"]

                    v_w1 = {}
                    v_w1[0] = ("dwa", p.dma("gpsimd", [("vs", v_xl2)],
                                    WBUF[:, 0:4096].rearrange("p (c n) -> p c n", c=NCH),
                                    bw1[l].rearrange("(c p) n -> p c n", p=128)[:, :, 0:512], "dwa"))
                    hid_ev = {}
                    for sl8 in range(8):
                        slot = sl8 % 2
                        if sl8 + 1 < 8:
                            wd = [("ss", hid_ev[sl8 * 8 - 1])] if sl8 >= 1 else []
                            sem = "dwa" if (sl8 + 1) % 2 == 0 else "dwb"
                            v_w1[sl8 + 1] = (sem, p.dma("gpsimd", wd,
                                                  WBUF[:, (1 - slot) * 4096:(2 - slot) * 4096]
                                                  .rearrange("p (c n) -> p c n", c=NCH),
                                                  bw1[l].rearrange("(c p) n -> p c n", p=128)
                                                  [:, :, (sl8 + 1) * 512:(sl8 + 2) * 512], sem))
                        for j in range(4):
                            hc = sl8 * 4 + j
                            for tg in range(2):
                                m = hc * 2 + tg
                                ph_ = psH0 if m % 2 == 0 else psH1
                                w = []
                                if j == 0 and tg == 0:
                                    w.append(v_w1[sl8])
                                if m >= 2:
                                    w.append(("ss", hid_ev[m - 2]))

                                def hmm(t, hc=hc, tg=tg, ph_=ph_, slot=slot, half=half):
                                    last = None
                                    for c in range(NCH):
                                        last = t.matmul(
                                            ph_[:, :],
                                            WBUF[:, slot * 4096 + c * 512 + (hc % 4) * 128:
                                                 slot * 4096 + c * 512 + (hc % 4) * 128 + 128],
                                            XNFM[:, c * T + half * 1024 + tg * 512:
                                                 c * T + half * 1024 + (tg + 1) * 512],
                                            start=(c == 0), stop=(c == NCH - 1))
                                    return last
                                th = p.emit("tensor", w, hmm, inc="ts")
                                hid_ev[hc * 2 + tg] = p.emit(
                                    "scalar", [("ts", th)],
                                    lambda s, hc=hc, tg=tg, ph_=ph_: s.activation(
                                        out=hid_t(hc)[:, tg * 512:(tg + 1) * 512], in_=ph_[:, :],
                                        func=AF.Gelu_apprx_sigmoid,
                                        bias=B1C[:, hc:hc + 1]), inc="ss")
                    v_hid_done = p.cnt["ss"]
                    if half == 0 and cuts("pc5h", BIG, HID2, XNFM):
                        stP.close()
                        p.flush("pc5h")
                        return

                    v_w2 = {}
                    v_w2[0] = ("dwa", p.dma("gpsimd", [("ss", v_hid_done)],
                                    WBUF[:, 0:4096].rearrange("p (c n) -> p c n", c=32),
                                    bw2[l].rearrange("(c p) n -> p c n", p=128)[:, :, 0:128], "dwa"))
                    v_w2r = {}
                    for og in range(8):
                        slot = og % 2
                        if og + 1 < 8:
                            wd = [("ts", v_w2r[(og - 1) * 2 + 1][0])] if og >= 1 else []
                            sem = "dwa" if (og + 1) % 2 == 0 else "dwb"
                            v_w2[og + 1] = (sem, p.dma("gpsimd", wd,
                                                 WBUF[:, (1 - slot) * 4096:(2 - slot) * 4096]
                                                 .rearrange("p (c n) -> p c n", c=32),
                                                 bw2[l].rearrange("(c p) n -> p c n", p=128)
                                                 [:, :, (og + 1) * 128:(og + 2) * 128], sem))
                        for tg in range(2):
                            ph_ = psH0 if tg % 2 == 0 else psH1
                            w = [("ss", v_hid_done)]
                            if tg == 0:
                                w.append(v_w2[og])
                            if (og * 2 + tg) >= 2:
                                w.append(("vs", v_w2r[(og * 2 + tg) - 2][1]))

                            def wmm2(t, og=og, tg=tg, ph_=ph_, slot=slot):
                                last = None
                                for hc in range(32):
                                    last = t.matmul(
                                        ph_[:, :],
                                        WBUF[:, slot * 4096 + hc * 128: slot * 4096 + (hc + 1) * 128],
                                        hid_t(hc)[:, tg * 512:(tg + 1) * 512],
                                        start=(hc == 0), stop=(hc == 31))
                                return last
                            tw2 = p.emit("tensor", w, wmm2, inc="ts")
                            sv = p.emit("vector", [("ts", tw2)], lambda v, ph_=ph_: v.tensor_copy(
                                out=SOFT2[:, 0:512] if ph_ is psH0 else SOFT2[:, 512:1024],
                                in_=ph_[:, :]), inc="vs")

                            def tr2(t, og=og, tg=tg, ph_=ph_):
                                src = SOFT2[:, 0:512] if ph_ is psH0 else SOFT2[:, 512:1024]
                                psf = psF1 if tg % 2 == 0 else psF2
                                last = None
                                for i in range(4):
                                    last = t.transpose(psf[:, i * 128:(i + 1) * 128],
                                                       src[:, i * 128:(i + 1) * 128], IDF[:, :])
                                return last
                            w3 = [("vs", sv)]
                            if (og * 2 + tg) >= 2:
                                w3.append(("vs", v_w2r[(og * 2 + tg) - 2][1]))
                            tt2 = p.emit("tensor", w3, tr2, inc="ts")

                            def radd(v, og=og, tg=tg, half=half):
                                psf = psF1 if tg % 2 == 0 else psF2
                                last = None
                                for i in range(4):
                                    tt = half * 8 + tg * 4 + i
                                    last = v.tensor_tensor(
                                        out=X[:, tt * D + og * 128: tt * D + (og + 1) * 128],
                                        in0=X[:, tt * D + og * 128: tt * D + (og + 1) * 128],
                                        in1=psf[:, i * 128:(i + 1) * 128], op=ALU.add)
                                return last
                            vr = p.emit("vector", [("ts", tt2)], radd, inc="vs")
                            v_w2r[og * 2 + tg] = (tw2, vr)
                            if half == 0 and og == 0 and tg == 0 and cuts("pc5w", SOFT2[:, 0:512]):
                                stP.close()
                                p.flush("pc5w")
                                return
                    if half == 0 and cuts("pc5a", X):
                        stP.close()
                        p.flush("pc5a")
                        return
                stP.close()
                p.flush("pc5_l%d" % l)
                if cuts("pc5", X):
                    return
            v_rd = {}
            v_x3 = {}
            for tt in range(NTT):
                slot = tt % 2
                sem = "dh0" if slot == 0 else "dh1"
                wd = [("vs", v_x3[tt - 2])] if tt >= 2 else []
                v_rd[tt] = p.dma("sync", wd, XTMPF[:, slot * 1024:(slot + 1) * 1024],
                                 spill.rearrange("(t p) d -> p t d", p=128)[:, tt], sem)
                v_x3[tt] = p.emit("vector", [(sem, v_rd[tt])], lambda v, tt=tt, slot=slot: (
                    v.drain(),
                    v.scalar_tensor_tensor(
                        out=X[:, tt * D:(tt + 1) * D], in0=XTMPF[:, slot * 1024:(slot + 1) * 1024],
                        scalar=1.0, in1=X[:, tt * D:(tt + 1) * D], op0=ALU.mult, op1=ALU.add,
                        accum_out=SM[:, SM_A + tt:SM_A + tt + 1]))[-1], inc="vs")
            p.emit("vector", [("vs", p.cnt["vs"])], lambda v: v.tensor_copy(
                out=SM[:, SM_SUM:SM_SUM + 16], in_=SM[:, SM_A:SM_A + 16]), inc="vs")
            stC.close()
            p.flush("pc_end")

        if nphases >= 3:
            phase_attnb(pccut)
            if "pc" in debug:
                dbg_list += [("pc_x", X)]

        if nphases >= 4:
            phase_ffnq("pd", accum_for_next=False)


    except _Cut:
        pass
    wfin = [(k, v) for k, v in p.cnt.items() if v > 0 and k != "dq"]
    p.emit("sync", wfin, lambda s: s.dma_start(
        out=y_out.rearrange("(t p) d -> p t d", p=128),
        in_=X[:, :].rearrange("p (t d) -> p t d", t=NTT)), inc="dq", incby=16)
    wdbg = [("vs", p.cnt["vs"]), ("ss", p.cnt["ss"]), ("ts", p.cnt["ts"])]
    for name, ten in dbg_list:
        ap = ten if isinstance(ten, bass.AP) else ten[:, :]
        o = nc.dram_tensor("dbg_" + name, list(ap.shape), ap.dtype, kind="ExternalOutput").ap()
        p.emit("sync", wdbg, lambda s, o=o, ap=ap: s.dma_start(out=o, in_=ap),
               inc="dq", incby=16)
        wdbg = []
    vq = p.cnt["dq"]
    if _os2.environ.get("NONOP"):
        p.emit("sync", [("dq", vq)], lambda s: None)
    else:
        p.emit("sync", [("dq", vq)], lambda s: s.nop())
    p.flush("fin")
    return p


def make_inputs(inputs):
    consts = host_consts()
    base = {
        "wqkv": inputs["L0_attn_a_w_qkv"], "wproj": inputs["L0_attn_a_w_proj"],
        "fw": inputs["L0_ffn_weight"], "fb": np.asarray(inputs["L0_ffn_bias"]).reshape(1, -1),
        "bln1g": inputs["L0_attn_b_ln1_g"], "bln1b": inputs["L0_attn_b_ln1_b"],
        "bln2g": inputs["L0_attn_b_ln2_g"], "bln2b": inputs["L0_attn_b_ln2_b"],
        "bwq": inputs["L0_attn_b_wq"], "bwk": inputs["L0_attn_b_wk"],
        "bwv": inputs["L0_attn_b_wv"], "bwo": inputs["L0_attn_b_wo"],
        "bbq": inputs["L0_attn_b_bq"], "bbk": inputs["L0_attn_b_bk"],
        "bbv": inputs["L0_attn_b_bv"],
        "bw1": inputs["L0_attn_b_w1"], "bb1": inputs["L0_attn_b_b1"],
        "bw2": inputs["L0_attn_b_w2"],
    }
    def cols(a, n):
        a = np.asarray(a, dtype=np.float32)
        return np.ascontiguousarray(a.reshape(2, n, 128).transpose(0, 2, 1))
    base["cln1g"] = cols(inputs["L0_attn_b_ln1_g"], 8)
    base["cln1b"] = cols(inputs["L0_attn_b_ln1_b"], 8)
    base["cln2g"] = cols(inputs["L0_attn_b_ln2_g"], 8)
    base["cln2b"] = cols(inputs["L0_attn_b_ln2_b"], 8)
    base["cbq"] = cols(inputs["L0_attn_b_bq"], 8)
    base["cbk"] = cols(inputs["L0_attn_b_bk"], 8)
    base["cb1"] = cols(inputs["L0_attn_b_b1"], 32)
    base = {k: np.ascontiguousarray(np.asarray(v, dtype=np.float32)) for k, v in base.items()}
    base.update(consts)
    x = np.asarray(inputs["L0_attn_a_x"], dtype=np.float32)
    return [dict(base, x=np.ascontiguousarray(x[b])) for b in range(x.shape[0])]


def kernel(**inputs):
    prog = build()
    in_maps = make_inputs(inputs)
    res = run_bass_kernel_spmd(prog.nc, in_maps, core_ids=list(range(8)))
    return np.stack([r["y"] for r in res.results], axis=0)


# version 07 (final)
# speedup vs baseline: 1.0259x; 1.0023x over previous
import math
import numpy as np
import ml_dtypes
from contextlib import ExitStack

import concourse.bass as bass
import concourse.mybir as mybir
from concourse.bass_utils import run_bass_kernel_spmd

F32 = mybir.dt.float32
BF16 = mybir.dt.bfloat16
AF = mybir.ActivationFunctionType
ALU = mybir.AluOpType
AX = mybir.AxisListType

T = 2048
D = 1024
NTT = 16
NCH = 8
NH = 16
MAGIC = 12582912.0
EPS = 1e-5
CSLACK = 8.0


def host_consts():
    bf = ml_dtypes.bfloat16
    c = {}
    c["C_IDB"] = np.eye(128, dtype=bf)
    c["C_IDF"] = np.eye(128, dtype=np.float32)
    g = np.zeros((128, NTT * 64), dtype=np.float32)
    for tt in range(NTT):
        for n in range(128):
            g[n, tt * 64 + 4 * tt + n // 32] = 1.0 / 32.0
    c["C_G"] = g.astype(bf)
    freqs = 2.0 ** np.linspace(0.0, math.log2(5.0), 16).astype(np.float32)
    pos = 1.0 + np.arange(T, dtype=np.float32)
    rad = pos[None, :] * freqs[:, None] * math.pi
    sinr, cosr = np.sin(rad).astype(np.float32), np.cos(rad).astype(np.float32)
    t2 = np.ones((128, T), dtype=np.float32)
    for d in range(32):
        alt = -1.0 if d % 2 == 0 else 1.0
        t2[d] = sinr[d % 16] + alt * cosr[d % 16]
        t2[64 + d] = t2[d]
    c["C_T2"] = t2
    e = np.zeros((128, 1024), dtype=np.float32)
    for pp in range(128):
        e[pp, pp % 64::64] = 1.0
    c["C_EYE"] = e
    c["C_ONEF"] = np.ones((128, 128), dtype=np.float32)
    c["C_ONEB"] = np.ones((128, 128), dtype=bf)
    return c


class Prog:
    def __init__(self):
        self.nc = bass.Bass("TRN2", target_bir_lowering=False, debug=False)
        self.es = ExitStack()
        self.sems = {}
        self.cnt = {}
        self.steps = []
        self._names = {}

    def mksem(self, *names):
        for n in names:
            self.sems[n] = self.es.enter_context(self.nc.semaphore(n))
            self.cnt[n] = 0

    def _uniq(self, name):
        n = self._names.get(name, 0)
        self._names[name] = n + 1
        return name if n == 0 else "%s_%d" % (name, n)

    def sb(self, name, shape, dt, stack=None):
        return (stack or self.es).enter_context(
            self.nc.sbuf_tensor(self._uniq(name), shape, dt))

    def ps(self, name, shape, dt, stack=None):
        return (stack or self.es).enter_context(
            self.nc.psum_tensor(self._uniq(name), shape, dt))

    def bump(self, name, by=1):
        self.cnt[name] += by
        return self.cnt[name]

    def emit(self, eng_name, waits, fn, inc=None, incby=1):
        post = self.bump(inc, incby) if inc else None
        semtab = self.sems

        def run(eng, waits=tuple(waits), fn=fn, inc=inc, incby=incby):
            for n, v in waits:
                eng.wait_ge(semtab[n], v)
            last = fn(eng)
            if inc is not None:
                assert last is not None, "step with inc returned no instruction"
                last.then_inc(semtab[inc], incby)
        self.steps.append((eng_name, run))
        return post

    def dma(self, eng_name, waits, out, in_, sem="dw"):
        pre = self.cnt[sem]
        post = self.bump(sem, 16)
        semtab = self.sems
        waits = list(waits)
        if pre > 0:
            waits.append((sem, pre))

        def run(eng, waits=tuple(waits)):
            for n, v in waits:
                eng.wait_ge(semtab[n], v)
            eng.dma_start(out=out, in_=in_).then_inc(semtab[sem], 16)
        self.steps.append((eng_name, run))
        return post

    def flush(self, name=None):
        by_eng = {}
        for e, fn in self.steps:
            by_eng.setdefault(e, []).append(fn)
        self.steps = []
        if not by_eng:
            return
        import os as _os
        with self.nc.Block(None if _os.environ.get("NOBLOCKNAME") else name) as block:
            for e, fns in by_eng.items():
                def body(eng, fns=fns):
                    for fn in fns:
                        fn(eng)
                getattr(block, e)(body)


class _Cut(Exception):
    pass


def build(nphases=99, debug=(), pccut=None, pacut=None):
    p = Prog()
    nc = p.nc
    dbg_list = []

    def pchk(tag):
        if pacut == tag:
            raise _Cut()

    def din(name, shape, dt=F32):
        return nc.dram_tensor(name, shape, dt, kind="ExternalInput").ap()

    x_in = din("x", [T, D])
    wqkv = din("wqkv", [D, 3 * D])
    wproj = din("wproj", [D, D])
    fw = din("fw", [D, D])
    fb = din("fb", [1, D])
    bln1g = din("bln1g", [2, D]); bln1b = din("bln1b", [2, D])
    bln2g = din("bln2g", [2, D]); bln2b = din("bln2b", [2, D])
    bwq = din("bwq", [2, D, D]); bwk = din("bwk", [2, D, D])
    bwv = din("bwv", [2, D, D]); bwo = din("bwo", [2, D, D])
    bbq = din("bbq", [2, D]); bbk = din("bbk", [2, D]); bbv = din("bbv", [2, D])
    bw1 = din("bw1", [2, D, 4 * D]); bb1 = din("bb1", [2, 4 * D])
    bw2 = din("bw2", [2, 4 * D, D])
    cln1g = din("cln1g", [2, 128, 8]); cln1b = din("cln1b", [2, 128, 8])
    cln2g = din("cln2g", [2, 128, 8]); cln2b = din("cln2b", [2, 128, 8])
    cbq = din("cbq", [2, 128, 8]); cbk = din("cbk", [2, 128, 8])
    cb1 = din("cb1", [2, 128, 32])
    c_idb = din("C_IDB", [128, 128], BF16)
    c_idf = din("C_IDF", [128, 128], F32)
    c_g = din("C_G", [128, NTT * 64], BF16)
    c_t2 = din("C_T2", [128, T], F32)
    c_eye = din("C_EYE", [128, 1024], F32)
    c_onef = din("C_ONEF", [128, 128], F32)
    c_oneb = din("C_ONEB", [128, 128], BF16)

    y_out = nc.dram_tensor("y", [T, D], F32, kind="ExternalOutput").ap()
    import os as _os
    if _os.environ.get("NOSPILL"):
        spill = nc.dram_tensor("spill", [T, D], F32, kind="ExternalInput").ap()
    else:
        spill = nc.dram_tensor("spill", [T, D], F32).ap()

    X = p.sb("X", [128, NTT * D], F32)
    XNFM = p.sb("XNFM", [128, NCH * T], BF16)
    BIG = p.sb("BIG", [128, 16384], BF16)
    WBUF = p.sb("WBUF", [128, 8192], BF16)
    IDB = p.sb("IDB", [128, 128], BF16)
    IDF = p.sb("IDF", [128, 128], F32)
    CG = p.sb("CG", [128, NTT * 64], BF16)
    CEYE = p.sb("CEYE", [128, 1024], F32)
    ONEF = p.sb("ONEF", [128, 128], F32)
    ONEB = p.sb("ONEB", [128, 128], BF16)
    SM = p.sb("SMALL", [128, 256], F32)
    CZCOL = p.sb("CZCOL", [128, 2], F32)
    XTMPF = p.sb("XTMPF", [128, 2048], F32)
    SOFT2 = p.sb("SOFT2", [128, 1024], F32)

    SM_SUM, SM_SQ, SM_MM, SM_VAR, SM_INV, SM_NMI = 0, 16, 32, 48, 64, 80
    SM_A, SM_AM, SM_RA, SM_R127, SM_XS, SM_MISC = 96, 128, 144, 160, 176, 192

    p.mksem("dx", "dw", "dwa", "dwb", "dwc", "dh0", "dh1", "dc", "dq", "ts", "vs", "ss", "gs")

    v_x = p.dma("sync", [], X[:, :].rearrange("p (t d) -> p t d", t=NTT),
                x_in.rearrange("(t p) d -> p t d", p=128), "dx")
    import os as _os2
    for dst, src in [(IDB, c_idb), (IDF, c_idf), (CG, c_g),
                     (CEYE, c_eye), (ONEF, c_onef), (ONEB, c_oneb)]:
        if not _os2.environ.get("NOCONST"):
            p.dma("sync", [], dst[:, :], src, "dc")
    v_consts = p.cnt["dc"]
    if not _os2.environ.get("NOMEM"):
        p.emit("vector", [], lambda v: (
            v.memset(CZCOL[:, 0:1], 0.0),
            v.memset(CZCOL[:, 1:2], EPS * 1024.0 * 1024.0))[-1], inc="vs")
    v_czcol = p.cnt["vs"]
    p.flush("p0")
    try:
        pchk("p0")

        def ln_stats(have_sums):
            for tt in range(NTT):
                if not have_sums:
                    w = [("dx", v_x)] if tt == 0 else []
                    if p.cnt["ss"] >= 1:
                        w.append(("ss", p.cnt["ss"]))
                    p.emit("scalar", w, lambda s, tt=tt: s.activation(
                        out=XTMPF[:, 0:1024], in_=X[:, tt * D:(tt + 1) * D], func=AF.Copy,
                        accum_out=SM[:, SM_SUM + tt:SM_SUM + tt + 1]), inc="ss")
                w = [("dx", v_x)] if (tt == 0 and have_sums) else []
                if p.cnt["ss"] >= 1:
                    w.append(("ss", p.cnt["ss"]))
                p.emit("scalar", w, lambda s, tt=tt: s.activation(
                    out=XTMPF[:, 1024:2048], in_=X[:, tt * D:(tt + 1) * D], func=AF.Square,
                    accum_out=SM[:, SM_SQ + tt:SM_SQ + tt + 1]), inc="ss")
            vss = p.cnt["ss"]

            def m1(v):
                v.tensor_tensor(out=SM[:, SM_MM:SM_MM + 16], in0=SM[:, SM_SUM:SM_SUM + 16],
                                in1=SM[:, SM_SUM:SM_SUM + 16], op=ALU.mult)
                v.drain()
                return v.scalar_tensor_tensor(out=SM[:, SM_VAR:SM_VAR + 16],
                                              in0=SM[:, SM_SQ:SM_SQ + 16], scalar=1024.0,
                                              in1=SM[:, SM_MM:SM_MM + 16],
                                              op0=ALU.mult, op1=ALU.subtract)
            vv = p.emit("vector", [("ss", vss)], m1, inc="vs")
            sv = p.emit("scalar", [("vs", vv)], lambda s: s.activation(
                out=SM[:, SM_MM:SM_MM + 16], in_=SM[:, SM_VAR:SM_VAR + 16], func=AF.Sqrt,
                bias=CZCOL[:, 1:2]), inc="ss")

            def m2(v):
                v.reciprocal(out=SM[:, SM_VAR:SM_VAR + 16], in_=SM[:, SM_MM:SM_MM + 16])
                v.drain()
                v.tensor_scalar(out=SM[:, SM_INV:SM_INV + 16], in0=SM[:, SM_VAR:SM_VAR + 16],
                                scalar1=1024.0, scalar2=None, op0=ALU.mult, op1=ALU.bypass)
                v.drain()
                return v.scalar_tensor_tensor(out=SM[:, SM_NMI:SM_NMI + 16],
                                              in0=SM[:, SM_SUM:SM_SUM + 16], scalar=-1.0 / 1024.0,
                                              in1=SM[:, SM_INV:SM_INV + 16],
                                              op0=ALU.mult, op1=ALU.mult)
            return p.emit("vector", [("ss", sv)], m2, inc="vs")

        stA = ExitStack()
        stA2 = ExitStack()
        QLB = p.sb("QLB", [128, 512], BF16, stA)
        KLB = p.sb("KLB", [128, 512], BF16, stA)
        XNLF = p.sb("XNLF", [128, 512], BF16, stA)
        K3VB = p.sb("K3VB", [128, 1040], BF16, stA)
        Z2A = p.sb("Z2A", [128, 520], BF16, stA)
        QBUF = p.sb("QBUF", [128, T], BF16, stA2)
        KBUF = p.sb("KBUF", [128, T], BF16, stA2)
        VBUF = p.sb("VBUF", [128, 2 * 1040], BF16, stA2)
        XTMPB = p.sb("XTMPB", [128, 2048], BF16, stA2)
        AE3 = p.sb("AE3", [128, 2048], BF16, stA2)
        XNL = p.sb("XNL", [128, 1024], BF16, stA2)
        NEGC = p.sb("NEGC", [128, 8], F32, stA2)

        pa1 = ExitStack()
        psT = p.ps("psT", [128, 1024], BF16, pa1)
        psT2 = p.ps("psT2", [128, 1024], BF16, pa1)
        psLM = p.ps("psLM", [64, 1024], F32, pa1)
        v_inv = ln_stats(have_sums=False)
        t_tp, v_ev = {}, {}
        for tt in range(NTT):
            slot = tt % 2
            w = [("vs", v_inv)]
            if tt >= 2:
                w.append(("ts", t_tp[tt - 2]))
            va = p.emit("vector", w, lambda v, tt=tt, slot=slot: v.tensor_scalar(
                out=XTMPB[:, slot * 1024:(slot + 1) * 1024], in0=X[:, tt * D:(tt + 1) * D],
                scalar1=SM[:, SM_INV + tt:SM_INV + tt + 1],
                scalar2=SM[:, SM_NMI + tt:SM_NMI + tt + 1],
                op0=ALU.mult, op1=ALU.add), inc="vs")

            def tfn(t, tt=tt, slot=slot):
                xn = XTMPB[:, slot * 1024:(slot + 1) * 1024]
                pst = psT if slot == 0 else psT2
                for c in range(NCH):
                    t.transpose(pst[:, c * 128:(c + 1) * 128], xn[:, c * 128:(c + 1) * 128], IDB[:, :])
                t.matmul(psLM[:, 0:512], CG[:, tt * 64:(tt + 1) * 64], xn[:, 0:512],
                         start=(tt == 0), stop=(tt == NTT - 1))
                return t.matmul(psLM[:, 512:1024], CG[:, tt * 64:(tt + 1) * 64], xn[:, 512:1024],
                                start=(tt == 0), stop=(tt == NTT - 1))
            w = [("vs", va), ("dc", v_consts)]
            if tt >= 2:
                w.append(("vs", v_ev[tt - 2]))
            t_tp[tt] = p.emit("tensor", w, tfn, inc="ts")
            v_ev[tt] = p.emit("vector", [("ts", t_tp[tt])], lambda v, slot=slot, tt=tt: v.tensor_copy(
                out=XNFM[:, :].rearrange("p (c t) -> p c t", c=NCH)[:, :, tt * 128:(tt + 1) * 128],
                in_=(psT if slot == 0 else psT2)[:, :].rearrange("p (c n) -> p c n", c=NCH)),
                inc="vs")
        s_xnl = p.emit("scalar", [("ts", t_tp[NTT - 1])], lambda s: s.activation(
            out=XNL[0:64, :], in_=psLM[:, :], func=AF.Copy), inc="ss")

        def t_xnl(t):
            last = None
            for c in range(NCH):
                last = t.transpose(psT[:, c * 64:(c + 1) * 64], XNL[0:64, c * 128:(c + 1) * 128],
                                   IDB[0:64, 0:64])
            return last
        tx = p.emit("tensor", [("ss", s_xnl), ("vs", v_ev[NTT - 1]), ("vs", v_ev[NTT - 2])],
                    t_xnl, inc="ts")
        v_xnlf = p.emit("vector", [("ts", tx)], lambda v: v.tensor_copy(
            out=XNLF[:, :], in_=psT[:, 0:512]), inc="vs")
        pa1.close()
        p.flush("pa1")
        pchk("pa1")
        if "pa1" in debug:
            dbg_list += [("pa1_xnfm", XNFM), ("pa1_xnlf", XNLF)]

        pa2 = ExitStack()
        ps0 = p.ps("ps0", [128, 512], F32, pa2)
        ps1 = p.ps("ps1", [128, 512], F32, pa2)
        ps2 = p.ps("ps2", [128, 512], F32, pa2)
        ps3 = p.ps("ps3", [128, 512], F32, pa2)
        psx = p.ps("psx", [128, 512], F32, pa2)
        psk = p.ps("psk", [128, 512], F32, pa2)
        psb0 = p.ps("psb0", [128, 1024], BF16, pa2)
        psb1 = p.ps("psb1", [128, 1024], BF16, pa2)

        def dma_qkv(hg, slot, waits):
            res = []
            for vi in range(3):
                sem = ("dwa", "dwb", "dwc")[vi]
                res.append((sem, p.dma("gpsimd", waits if vi == 0 else [],
                            WBUF[:, slot * 4096 + vi * 1024: slot * 4096 + (vi + 1) * 1024]
                            .rearrange("p (c n) -> p c n", c=NCH),
                            wqkv.rearrange("(c p) n -> p c n", p=128)
                            [:, :, vi * D + hg * 128: vi * D + (hg + 1) * 128], sem)))
            return res

        v_w = {0: dma_qkv(0, 0, [])}
        prev_tb = None
        v_vbuf_ones = None
        v_k3n_all = []
        hg_fence = []
        for hg in range(NCH):
            slot = hg % 2
            WQ = WBUF[:, slot * 4096: slot * 4096 + 1024]
            WK = WBUF[:, slot * 4096 + 1024: slot * 4096 + 2048]
            WV = WBUF[:, slot * 4096 + 2048: slot * 4096 + 3072]

            qk_ev = {}
            for tg in range(4):
                pq = ps0 if tg % 2 == 0 else ps1
                pk = ps2 if tg % 2 == 0 else ps3
                w = []
                if tg == 0:
                    w += list(v_w[hg]) + [("vs", v_xnlf)]
                    if hg_fence:
                        w += [("vs", hg_fence[-1][0]), ("ss", hg_fence[-1][1])]
                if tg >= 2:
                    w.append(("ss", qk_ev[tg - 2]))

                def mmqk(t, tg=tg, pq=pq, pk=pk, WQ=WQ, WK=WK):
                    for c in range(NCH):
                        t.matmul(pq[:, :], WQ[:, c * 128:(c + 1) * 128],
                                 XNFM[:, c * T + tg * 512: c * T + (tg + 1) * 512],
                                 start=(c == 0), stop=(c == NCH - 1))
                    last = None
                    for c in range(NCH):
                        last = t.matmul(pk[:, :], WK[:, c * 128:(c + 1) * 128],
                                        XNFM[:, c * T + tg * 512: c * T + (tg + 1) * 512],
                                        start=(c == 0), stop=(c == NCH - 1))
                    return last
                tq = p.emit("tensor", w, mmqk, inc="ts")
                qk_ev[tg] = p.emit("scalar", [("ts", tq)], lambda s, tg=tg, pq=pq, pk=pk: (
                    s.activation(out=QBUF[:, tg * 512:(tg + 1) * 512], in_=pq[:, :],
                                 func=AF.Copy, scale=0.125),
                    s.activation(out=KBUF[:, tg * 512:(tg + 1) * 512], in_=pk[:, :],
                                 func=AF.Copy))[-1], inc="ss")

            def mmql(t, WQ=WQ, WK=WK):
                for c in range(NCH):
                    t.matmul(psx[:, 0:64], WQ[:, c * 128:(c + 1) * 128], XNLF[:, c * 64:(c + 1) * 64],
                             start=(c == 0), stop=(c == NCH - 1))
                last = None
                for c in range(NCH):
                    last = t.matmul(psx[:, 64:128], WK[:, c * 128:(c + 1) * 128],
                                    XNLF[:, c * 64:(c + 1) * 64],
                                    start=(c == 0), stop=(c == NCH - 1))
                return last
            wql = list(v_w[hg]) + [("vs", v_xnlf)]
            if hg_fence:
                wql.append(("vs", hg_fence[-1][0]))
            tql = p.emit("tensor", wql, mmql, inc="ts")
            v_ql = p.emit("vector", [("ts", tql)], lambda v, hg=hg: (
                v.tensor_scalar(out=QLB[:, hg * 64:(hg + 1) * 64], in0=psx[:, 0:64],
                                scalar1=0.125, scalar2=None, op0=ALU.mult, op1=ALU.bypass),
                v.tensor_copy(out=KLB[:, hg * 64:(hg + 1) * 64], in_=psx[:, 64:128]))[-1], inc="vs")

            v_vev = {}
            for g in range(4):
                pv = psk if g % 2 == 0 else psx
                w = []
                if g == 0:
                    w += list(v_w[hg])
                    if hg_fence:
                        w.append(("vs", hg_fence[-1][0]))
                if g == 1:
                    w.append(("vs", v_ql))
                if g >= 2:
                    w.append(("vs", v_vev[g - 2]))

                def mmv(t, g=g, pv=pv, WV=WV):
                    last = None
                    for i in range(4):
                        tt = g * 4 + i
                        for c in range(NCH):
                            last = t.matmul(pv[:, i * 128:(i + 1) * 128],
                                            XNFM[:, c * T + tt * 128: c * T + (tt + 1) * 128],
                                            WV[:, c * 128:(c + 1) * 128],
                                            start=(c == 0), stop=(c == NCH - 1))
                    return last
                tv = p.emit("tensor", w, mmv, inc="ts")

                def vev(v, g=g, pv=pv):
                    v.tensor_copy(
                        out=VBUF[:, 0:1040].rearrange("p (t y) -> p t y", t=NTT)
                        [:, g * 4:(g + 1) * 4, 0:64],
                        in_=pv[:, :].rearrange("p (t n) -> p t n", t=4)[:, :, 0:64])
                    return v.tensor_copy(
                        out=VBUF[:, 1040:2080].rearrange("p (t y) -> p t y", t=NTT)
                        [:, g * 4:(g + 1) * 4, 0:64],
                        in_=pv[:, :].rearrange("p (t n) -> p t n", t=4)[:, :, 64:128])
                v_vev[g] = p.emit("vector", [("ts", tv)], vev, inc="vs")
            if hg == 0:
                v_vbuf_ones = p.emit("vector", [], lambda v: (
                    v.memset(VBUF[:, 64:1040:65], 1.0),
                    v.memset(VBUF[:, 1104:2080:65], 1.0))[-1], inc="vs")

            if hg + 1 < NCH:
                v_w[hg + 1] = dma_qkv(hg + 1, (hg + 1) % 2,
                                      [("ss", qk_ev[3]), ("vs", v_vev[3]), ("vs", v_ql)])
            if hg == 0:
                pchk("pa2a")

            base_w = [("ss", qk_ev[3]), ("vs", v_ql), ("vs", v_vev[2]), ("vs", v_vev[3])]
            tc = p.emit("tensor", base_w, lambda t, hg=hg: (
                t.matmul(ps0[:, 0:64], QBUF[0:64, 0:T:16],
                         KLB[0:64, hg * 64:(hg + 1) * 64], start=True, stop=True),
                t.matmul(ps1[:, 0:64], QBUF[64:128, 0:T:16],
                         KLB[64:128, hg * 64:(hg + 1) * 64], start=True, stop=True))[-1], inc="ts")
            vc = p.emit("vector", [("ts", tc)], lambda v: (
                v.reduce_max(out=SM[:, SM_MISC:SM_MISC + 1], in_=ps0[:, 0:64], axis=AX.X),
                v.reduce_max(out=SM[:, SM_MISC + 1:SM_MISC + 2], in_=ps1[:, 0:64], axis=AX.X),
                v.drain(),
                v.tensor_tensor(out=SM[:, SM_MISC:SM_MISC + 1], in0=SM[:, SM_MISC:SM_MISC + 1],
                                in1=SM[:, SM_MISC + 1:SM_MISC + 2], op=ALU.max))[-1], inc="vs")
            tc2 = p.emit("tensor", [("vs", vc)], lambda t: t.transpose(
                ps1[0:1, 0:128], SM[:, SM_MISC:SM_MISC + 1], IDF[:, :]), inc="ts")
            vc2 = p.emit("vector", [("ts", tc2)], lambda v: v.reduce_max(
                out=SM[0:1, SM_MISC + 1:SM_MISC + 2], in_=ps1[0:1, 0:128], axis=AX.X), inc="vs")
            tc3 = p.emit("tensor", [("vs", vc2)], lambda t: t.matmul(
                ps1[:, 128:129], ONEF[0:1, 0:128], SM[0:1, SM_MISC + 1:SM_MISC + 2],
                start=True, stop=True), inc="ts")
            s_c1 = p.emit("scalar", [("ts", tc3)], lambda s: s.activation(
                out=NEGC[:, 0:1], in_=ps1[:, 128:129], func=AF.Copy, scale=-1.0, bias=-CSLACK),
                inc="ss")

            if hg == 0:
                pchk("pa2b")
            e1_ev = {}
            for qg in range(4):
                pse = ps2 if qg % 2 == 0 else ps3
                pso = psx if qg % 2 == 0 else psk
                w = [("ss", s_c1), ("vs", v_vev[2]), ("vs", v_vev[3])] if qg == 0 else []
                if qg >= 2:
                    w.append(("ss", e1_ev[qg - 2]))

                def s1mm(t, qg=qg, pse=pse, pso=pso, hg=hg):
                    last = None
                    for i in range(4):
                        qt = qg * 4 + i
                        t.matmul(pse[:, i * 64:(i + 1) * 64],
                                 QBUF[0:64, qt * 128:(qt + 1) * 128],
                                 KLB[0:64, hg * 64:(hg + 1) * 64], start=True, stop=True)
                        last = t.matmul(pso[:, i * 64:(i + 1) * 64],
                                        QBUF[64:128, qt * 128:(qt + 1) * 128],
                                        KLB[64:128, hg * 64:(hg + 1) * 64], start=True, stop=True)
                    return last
                ts1 = p.emit("tensor", w, s1mm, inc="ts")
                e1_ev[qg] = p.emit("scalar", [("ts", ts1)], lambda s, qg=qg, pse=pse, pso=pso: (
                    s.activation(out=XTMPB[:, qg * 512:(qg + 1) * 512]
                                 .rearrange("p (i n) -> p i n", i=4)[:, :, 0:64],
                                 in_=pse[:, 0:256].rearrange("p (i n) -> p i n", i=4),
                                 func=AF.Exp, bias=NEGC[:, 0:1]),
                    s.activation(out=XTMPB[:, qg * 512:(qg + 1) * 512]
                                 .rearrange("p (i n) -> p i n", i=4)[:, :, 64:128],
                                 in_=pso[:, 0:256].rearrange("p (i n) -> p i n", i=4),
                                 func=AF.Exp, bias=NEGC[:, 0:1]))[-1], inc="ss")
            tb_ev = {}
            for half in range(2):
                psb = psb0 if half == 0 else psb1
                w = [("ss", e1_ev[half * 2 + 1])]
                if prev_tb is not None:
                    w.append(("vs", prev_tb[half]))

                def e1tr(t, half=half, psb=psb):
                    last = None
                    for i in range(8):
                        qt = half * 8 + i
                        last = t.transpose(psb[:, i * 128:(i + 1) * 128],
                                           XTMPB[:, qt * 128:(qt + 1) * 128], IDB[:, :])
                    return last
                tt1 = p.emit("tensor", w, e1tr, inc="ts")
                tb_ev[half] = p.emit("vector", [("ts", tt1)],
                                     lambda v, half=half, psb=psb, hg=hg: v.tensor_copy(
                    out=BIG[:, hg * 2048 + half * 1024: hg * 2048 + (half + 1) * 1024],
                    in_=psb[:, :]), inc="vs")
            prev_tb = tb_ev

            if hg == 0:
                pchk("pa2c")
            tc = p.emit("tensor", [("ss", s_c1)], lambda t, hg=hg: (
                t.matmul(ps0[0:64, 128:256], QLB[0:64, hg * 64:(hg + 1) * 64],
                         KBUF[0:64, 0:T:16], start=True, stop=True),
                t.matmul(ps0[64:128, 128:256], QLB[64:128, hg * 64:(hg + 1) * 64],
                         KBUF[64:128, 0:T:16], start=True, stop=True))[-1], inc="ts")
            vc = p.emit("vector", [("ts", tc)], lambda v: v.reduce_max(
                out=SM[:, SM_MISC:SM_MISC + 1], in_=ps0[:, 128:256], axis=AX.X), inc="vs")
            tc2 = p.emit("tensor", [("vs", vc)], lambda t: t.transpose(
                ps1[0:1, 0:128], SM[:, SM_MISC:SM_MISC + 1], IDF[:, :]), inc="ts")
            vc2 = p.emit("vector", [("ts", tc2)], lambda v: v.reduce_max(
                out=SM[0:1, SM_MISC + 1:SM_MISC + 2], in_=ps1[0:1, 0:128], axis=AX.X), inc="vs")
            tc3 = p.emit("tensor", [("vs", vc2)], lambda t: t.matmul(
                ps1[:, 128:129], ONEF[0:1, 0:128], SM[0:1, SM_MISC + 1:SM_MISC + 2],
                start=True, stop=True), inc="ts")
            s_c3 = p.emit("scalar", [("ts", tc3)], lambda s: s.activation(
                out=NEGC[:, 1:2], in_=ps1[:, 128:129], func=AF.Copy, scale=-1.0, bias=-CSLACK),
                inc="ss")

            e3_ev = {}
            for ng in range(4):
                pse = ps2 if ng % 2 == 0 else ps3
                pso = psx if ng % 2 == 0 else psk
                w = [("ss", s_c3), ("ss", e1_ev[3])] if ng == 0 else []
                if ng >= 2:
                    w.append(("ss", e3_ev[ng - 2]))

                def s3mm(t, ng=ng, pse=pse, pso=pso, hg=hg):
                    last = None
                    for i in range(4):
                        nb = ng * 4 + i
                        t.matmul(pse[:, i * 64:(i + 1) * 64],
                                 KBUF[0:64, nb * 128:(nb + 1) * 128],
                                 QLB[0:64, hg * 64:(hg + 1) * 64], start=True, stop=True)
                        last = t.matmul(pso[:, i * 64:(i + 1) * 64],
                                        KBUF[64:128, nb * 128:(nb + 1) * 128],
                                        QLB[64:128, hg * 64:(hg + 1) * 64], start=True, stop=True)
                    return last
                ts3 = p.emit("tensor", w, s3mm, inc="ts")
                e3_ev[ng] = p.emit("scalar", [("ts", ts3)], lambda s, ng=ng, pse=pse, pso=pso: (
                    s.activation(out=AE3[:, ng * 512:(ng + 1) * 512]
                                 .rearrange("p (i n) -> p i n", i=4)[:, :, 0:64],
                                 in_=pse[:, 0:256].rearrange("p (i n) -> p i n", i=4),
                                 func=AF.Exp, bias=NEGC[:, 1:2]),
                    s.activation(out=AE3[:, ng * 512:(ng + 1) * 512]
                                 .rearrange("p (i n) -> p i n", i=4)[:, :, 64:128],
                                 in_=pso[:, 0:256].rearrange("p (i n) -> p i n", i=4),
                                 func=AF.Exp, bias=NEGC[:, 1:2]))[-1], inc="ss")

            if hg == 0:
                pchk("pa2d")
            def k3vmm(t, hg=hg):
                last = None
                for hh in range(2):
                    for nb in range(NTT):
                        last = t.matmul(psk[0:64, hh * 65:(hh + 1) * 65],
                                        AE3[:, nb * 128 + hh * 64: nb * 128 + hh * 64 + 64],
                                        VBUF[:, hh * 1040 + nb * 65: hh * 1040 + (nb + 1) * 65],
                                        start=(nb == 0), stop=(nb == NTT - 1))
                return last
            tk3 = p.emit("tensor", [("ss", e3_ev[3]), ("vs", v_vev[2]), ("vs", v_vev[3]),
                                    ("vs", v_vbuf_ones)], k3vmm, inc="ts")

            def k3norm(v, hg=hg):
                v.reciprocal(out=SM[0:64, SM_MISC + 4:SM_MISC + 5], in_=psk[0:64, 64:65])
                v.reciprocal(out=SM[0:64, SM_MISC + 5:SM_MISC + 6], in_=psk[0:64, 129:130])
                v.drain()
                v.tensor_scalar(out=K3VB[0:64, (2 * hg) * 65:(2 * hg + 1) * 65],
                                in0=psk[0:64, 0:65],
                                scalar1=SM[0:64, SM_MISC + 4:SM_MISC + 5], scalar2=None,
                                op0=ALU.mult, op1=ALU.bypass)
                return v.tensor_scalar(out=K3VB[0:64, (2 * hg + 1) * 65:(2 * hg + 2) * 65],
                                       in0=psk[0:64, 65:130],
                                       scalar1=SM[0:64, SM_MISC + 5:SM_MISC + 6], scalar2=None,
                                       op0=ALU.mult, op1=ALU.bypass)
            v_k3n_all.append(p.emit("vector", [("ts", tk3)], k3norm, inc="vs"))
            hg_fence.append((p.cnt["vs"], p.cnt["ss"]))
        v_k3vb = p.cnt["vs"]
        pa2.close()
        p.flush("pa2")
        pchk("pa2")
        if "pa2" in debug:
            dbg_list += [("pa2_qbuf", QBUF), ("pa2_kbuf", KBUF), ("pa2_vbuf", VBUF),
                         ("pa2_qlb", QLB), ("pa2_klb", KLB), ("pa2_k3vb", K3VB),
                         ("pa2_big", BIG), ("pa2_negc", NEGC)]

        stA2.close()
        pa3 = ExitStack()
        SLAB = p.sb("SLAB", [64, 8192], F32, pa3)
        sK2, sKT, sA, sAT, sT1, sT2, sVN, sVNT = [SLAB[0:64, i * 1024:(i + 1) * 1024]
                                                  for i in range(8)]
        VNTB = p.sb("VNTB", [128, 1024], BF16, pa3)
        ZTMPB = p.sb("ZTMPB", [128, 520], BF16, pa3)
        psI1 = p.ps("psI1", [64, 1024], F32, pa3)
        psI2 = p.ps("psI2", [64, 1024], F32, pa3)
        psI3 = p.ps("psI3", [64, 1024], F32, pa3)
        psI4 = p.ps("psI4", [64, 1024], F32, pa3)
        E2B = XTMPF[0:64, 0:1024]

        def mcol(h):
            return SM_MISC + (h % 2) * 8 + h // 2

        def k2mm(t):
            last = None
            for h in range(NH):
                r0 = (h % 2) * 64
                hg = h // 2
                pb = psI1 if h % 2 == 0 else psI2
                last = t.matmul(pb[:, hg * 64:(hg + 1) * 64],
                                QLB[r0:r0 + 64, hg * 64:(hg + 1) * 64],
                                KLB[r0:r0 + 64, hg * 64:(hg + 1) * 64], start=True, stop=True)
            return last
        tk2 = p.emit("tensor", [("vs", v_k3vb)], k2mm, inc="ts")
        v_mx = p.emit("vector", [("ts", tk2)], lambda v: (
            v.reduce_max(out=SM[0:64, SM_MISC:SM_MISC + 8],
                         in_=psI1[:, 0:512].rearrange("p (g n) -> p g n", g=8),
                         axis=AX.X, negate=True),
            v.reduce_max(out=SM[0:64, SM_MISC + 8:SM_MISC + 16],
                         in_=psI2[:, 0:512].rearrange("p (g n) -> p g n", g=8),
                         axis=AX.X, negate=True))[-1], inc="vs")
        s_e2 = None
        for h in range(NH):
            pb2 = psI1 if h % 2 == 0 else psI2
            s_e2 = p.emit("scalar", [("vs", v_mx)], lambda s, h=h, pb2=pb2: s.activation(
                out=E2B[:, h * 64:(h + 1) * 64],
                in_=pb2[:, (h // 2) * 64:(h // 2 + 1) * 64], func=AF.Exp,
                bias=SM[0:64, mcol(h):mcol(h) + 1]), inc="ss")
        pchk("pa3a2")
        v_k2n = p.emit("vector", [("ss", s_e2)], lambda v: (
            v.reduce_sum(out=SM[0:64, SM_MISC + 16:SM_MISC + 32],
                         in_=E2B[:, :].rearrange("p (g n) -> p g n", g=16), axis=AX.X),
            v.drain(),
            v.reciprocal(out=SM[0:64, SM_MISC + 32:SM_MISC + 48],
                         in_=SM[0:64, SM_MISC + 16:SM_MISC + 32]))[-1], inc="vs")
        v_k2f = None
        for h in range(NH):
            v_k2f = p.emit("vector", [("vs", v_k2n)], lambda v, h=h: v.tensor_scalar(
                out=sK2[:, h * 64:(h + 1) * 64], in0=E2B[:, h * 64:(h + 1) * 64],
                scalar1=SM[0:64, SM_MISC + 32 + h:SM_MISC + 33 + h], scalar2=None,
                op0=ALU.mult, op1=ALU.bypass), inc="vs")

        pchk("pa3a3")
        def k2tr(t):
            last = None
            for h in range(NH):
                last = t.transpose(psI2[:, h * 64:(h + 1) * 64], sK2[:, h * 64:(h + 1) * 64],
                                   IDF[0:64, 0:64])
            return last
        tkt = p.emit("tensor", [("vs", v_k2f)], k2tr, inc="ts")
        v_kt = p.emit("vector", [("ts", tkt)], lambda v: v.tensor_copy(
            out=sKT[:, :], in_=psI2[:, :]), inc="vs")

        pchk("pa3a")
        def csmm(t):
            last = None
            for h in range(NH):
                last = t.matmul(psI3[0:1, h * 64:(h + 1) * 64], ONEF[0:64, 0:1],
                                sK2[:, h * 64:(h + 1) * 64], start=True, stop=True)
            return last
        tcs = p.emit("tensor", [("vs", v_kt)], csmm, inc="ts")
        v_cs = p.emit("vector", [("ts", tcs)], lambda v: (
            v.reduce_max(out=SM[0:1, SM_MISC + 48:SM_MISC + 64],
                         in_=psI3[0:1, :].rearrange("p (g n) -> p g n", g=16), axis=AX.X),
            v.drain(),
            v.reciprocal(out=SM[0:1, SM_MISC:SM_MISC + 16],
                         in_=SM[0:1, SM_MISC + 48:SM_MISC + 64]))[-1], inc="vs")

        def rdmm(t):
            last = None
            for h in range(NH):
                last = t.matmul(psI4[:, h:h + 1], ONEF[0:1, 0:64],
                                SM[0:1, SM_MISC + h:SM_MISC + h + 1], start=True, stop=True)
            return last
        trd = p.emit("tensor", [("vs", v_cs)], rdmm, inc="ts")
        v_rd = p.emit("vector", [("ts", trd)], lambda v: v.tensor_copy(
            out=SM[0:64, SM_MISC + 48:SM_MISC + 64], in_=psI4[:, 0:16]), inc="vs")
        v_v0 = None
        for h in range(NH):
            v_v0 = p.emit("vector", [("vs", v_rd)], lambda v, h=h: (
                v.tensor_scalar(out=sVN[:, h * 64:(h + 1) * 64], in0=sKT[:, h * 64:(h + 1) * 64],
                                scalar1=SM[0:64, SM_MISC + 48 + h:SM_MISC + 49 + h], scalar2=None,
                                op0=ALU.mult, op1=ALU.bypass),
                v.tensor_scalar(out=sVNT[:, h * 64:(h + 1) * 64], in0=sK2[:, h * 64:(h + 1) * 64],
                                scalar1=SM[0:64, SM_MISC + 48 + h:SM_MISC + 49 + h], scalar2=None,
                                op0=ALU.mult, op1=ALU.bypass))[-1], inc="vs")

        pchk("pa3b")
        def allmm(t, pso, lhs, rhs):
            last = None
            for h in range(NH):
                last = t.matmul(pso[:, h * 64:(h + 1) * 64], lhs[:, h * 64:(h + 1) * 64],
                                rhs[:, h * 64:(h + 1) * 64], start=True, stop=True)
            return last
        v_prev = v_v0
        for it in range(6):
            t1 = p.emit("tensor", [("vs", v_prev)], lambda t: (
                allmm(t, psI1, sKT, sVN), allmm(t, psI2, sVN, sKT))[-1], inc="ts")
            v1 = p.emit("vector", [("ts", t1)], lambda v: (
                v.tensor_copy(out=sA[:, :], in_=psI1[:, :]),
                v.tensor_copy(out=sAT[:, :], in_=psI2[:, :]),
                v.scalar_tensor_tensor(out=sT1[:, :], in0=CEYE[0:64, :], scalar=7.0,
                                       in1=psI1[:, :], op0=ALU.mult, op1=ALU.subtract),
                v.scalar_tensor_tensor(out=sT2[:, :], in0=CEYE[0:64, :], scalar=7.0,
                                       in1=psI2[:, :], op0=ALU.mult, op1=ALU.subtract))[-1], inc="vs")
            t2 = p.emit("tensor", [("vs", v1)], lambda t: (
                allmm(t, psI3, sAT, sT1), allmm(t, psI4, sT1, sAT))[-1], inc="ts")
            v2 = p.emit("vector", [("ts", t2)], lambda v: (
                v.scalar_tensor_tensor(out=sT1[:, :], in0=CEYE[0:64, :], scalar=15.0,
                                       in1=psI3[:, :], op0=ALU.mult, op1=ALU.subtract),
                v.scalar_tensor_tensor(out=sT2[:, :], in0=CEYE[0:64, :], scalar=15.0,
                                       in1=psI4[:, :], op0=ALU.mult, op1=ALU.subtract))[-1], inc="vs")
            t3 = p.emit("tensor", [("vs", v2)], lambda t: (
                allmm(t, psI1, sAT, sT1), allmm(t, psI2, sT1, sAT))[-1], inc="ts")
            v3 = p.emit("vector", [("ts", t3)], lambda v: (
                v.scalar_tensor_tensor(out=sT1[:, :], in0=CEYE[0:64, :], scalar=13.0,
                                       in1=psI1[:, :], op0=ALU.mult, op1=ALU.subtract),
                v.scalar_tensor_tensor(out=sT2[:, :], in0=CEYE[0:64, :], scalar=13.0,
                                       in1=psI2[:, :], op0=ALU.mult, op1=ALU.subtract))[-1], inc="vs")
            t4 = p.emit("tensor", [("vs", v3)], lambda t: (
                allmm(t, psI3, sVNT, sT1), allmm(t, psI4, sT1, sVNT))[-1], inc="ts")
            v_prev = p.emit("vector", [("ts", t4)], lambda v: (
                v.tensor_scalar(out=sVN[:, :], in0=psI3[:, :], scalar1=0.25, scalar2=None,
                                op0=ALU.mult, op1=ALU.bypass),
                v.tensor_scalar(out=sVNT[:, :], in0=psI4[:, :], scalar1=0.25, scalar2=None,
                                op0=ALU.mult, op1=ALU.bypass))[-1], inc="vs")

        pchk("pa3c")
        v_vntb = p.emit("vector", [("vs", v_prev)], lambda v: v.tensor_copy(
            out=VNTB[0:64, :], in_=sVNT[:, :]), inc="vs")

        def z2mm(t):
            last = None
            for h in range(NH):
                pso = psI1 if h % 2 == 0 else psI2
                last = t.matmul(pso[:, (h // 2) * 64:(h // 2 + 1) * 64],
                                VNTB[0:64, h * 64:(h + 1) * 64],
                                K3VB[0:64, h * 65:h * 65 + 64], start=True, stop=True)
            return last
        tz2 = p.emit("tensor", [("vs", v_vntb)], z2mm, inc="ts")
        v_z2e = p.emit("vector", [("ts", tz2)], lambda v: (
            v.memset(Z2A[:, :], 0.0),
            v.memset(ZTMPB[:, :], 0.0),
            v.drain(),
            v.tensor_copy(out=ZTMPB[0:64, :].rearrange("p (g y) -> p g y", g=NCH)[:, :, 0:64],
                          in_=psI2[:, 0:512].rearrange("p (g n) -> p g n", g=NCH)),
            v.tensor_copy(out=Z2A[0:64, :].rearrange("p (g y) -> p g y", g=NCH)[:, :, 0:64],
                          in_=psI1[:, 0:512].rearrange("p (g n) -> p g n", g=NCH)))[-1], inc="vs")
        v_z2d = p.dma("sync", [("vs", v_z2e)], Z2A[64:128, :], ZTMPB[0:64, :], "dq")
        v_ones = p.emit("vector", [("dq", v_z2d)], lambda v: v.memset(
            Z2A[:, 64:520:65], 1.0), inc="vs")
        pa3.close()
        p.flush("pa3")
        pchk("pa3")
        if "pa3" in debug:
            dbg_list += [("pa3_z2a", Z2A)]

        pa4 = ExitStack()
        SOFT = p.sb("SOFT", [128, 2048], F32, pa4)
        psSV = p.ps("psSV", [128, 2048], F32, pa4)
        psN = p.ps("psN", [128, 512], F32, pa4)
        psN2 = p.ps("psN2", [128, 512], F32, pa4)
        psN3 = p.ps("psN3", [128, 512], F32, pa4)
        psN4 = p.ps("psN4", [128, 512], F32, pa4)
        s_soft_prev = None
        v_rec_prev = None
        v_norm = {}
        for h in range(NH):
            r0 = (h % 2) * 64
            hg = h // 2

            def svmm(t, r0=r0, hg=hg):
                last = None
                for qg in range(4):
                    last = t.matmul(psSV[0:65, qg * 512:(qg + 1) * 512],
                                    Z2A[r0:r0 + 64, hg * 65:(hg + 1) * 65],
                                    BIG[r0:r0 + 64, hg * 2048 + qg * 512: hg * 2048 + (qg + 1) * 512],
                                    start=True, stop=True)
                return last
            w = [("vs", v_ones)] if h == 0 else [("ss", s_soft_prev)]
            tsv = p.emit("tensor", w, svmm, inc="ts")
            s_soft_prev = p.emit("scalar", [("ts", tsv)], lambda s: s.activation(
                out=SOFT[0:65, :], in_=psSV[0:65, :], func=AF.Copy), inc="ss")
            for qt in range(NTT):
                psn = [psN, psN2, psN3, psN4][qt % 4]
                w = [("ss", s_soft_prev)]
                idx0 = h * NTT + qt
                if idx0 >= 4:
                    w.append(("vs", v_norm[idx0 - 4]))
                tn = p.emit("tensor", w, lambda t, qt=qt, psn=psn: t.transpose(
                    psn[:, 0:65], SOFT[0:65, qt * 128:(qt + 1) * 128],
                    IDF[0:65, 0:65]), inc="ts")

                def nrm(v, qt=qt, psn=psn, h=h):
                    mc = SM_MISC + (qt % 4)
                    v.reciprocal(out=SM[:, mc:mc + 1], in_=psn[:, 64:65])
                    v.drain()
                    return v.tensor_scalar(out=XNFM[:, qt * 1024 + h * 64: qt * 1024 + (h + 1) * 64],
                                           in0=psn[:, 0:64],
                                           scalar1=SM[:, mc:mc + 1], scalar2=None,
                                           op0=ALU.mult, op1=ALU.bypass)
                v_norm[h * NTT + qt] = p.emit("vector", [("ts", tn)], nrm, inc="vs")
        v_svtm = p.cnt["vs"]
        pa4.close()
        p.flush("pa4")
        pchk("pa4")
        if "pa4" in debug:
            dbg_list += [("pa4_svtm", XNFM)]

        pa5 = ExitStack()
        psb0b = p.ps("psb0b", [128, 1024], BF16, pa5)
        psb1b = p.ps("psb1b", [128, 1024], BF16, pa5)
        psP = [p.ps("psP%d" % i, [128, 512], F32, pa5) for i in range(4)]
        v_wp = p.dma("gpsimd", [], WBUF[:, :].rearrange("p (c n) -> p c n", c=NCH),
                     wproj.rearrange("(c p) n -> p c n", p=128), "dw")
        t_tr2, v_ev2 = {}, {}
        for tt in range(NTT):
            psb = psb0b if tt % 2 == 0 else psb1b

            def trfn(t, tt=tt, psb=psb):
                last = None
                for c in range(NCH):
                    last = t.transpose(psb[:, c * 128:(c + 1) * 128],
                                       XNFM[:, tt * 1024 + c * 128: tt * 1024 + (c + 1) * 128],
                                       IDB[:, :])
                return last
            w = [("vs", v_svtm)] if tt == 0 else []
            if tt >= 2:
                w.append(("vs", v_ev2[tt - 2]))
            t_tr2[tt] = p.emit("tensor", w, trfn, inc="ts")
            v_ev2[tt] = p.emit("vector", [("ts", t_tr2[tt])], lambda v, tt=tt, psb=psb: v.tensor_copy(
                out=BIG[:, :].rearrange("p (c t) -> p c t", c=NCH)[:, :, tt * 128:(tt + 1) * 128],
                in_=psb[:, :].rearrange("p (c n) -> p c n", c=NCH)), inc="vs")
        v_pr = {}
        for tt in range(NTT):
            for og in range(2):
                k = tt * 2 + og
                pp = psP[k % 4]
                w = [("vs", v_ev2[tt])]
                if k == 0:
                    w.append(("dw", v_wp))
                if k >= 4:
                    w.append(("vs", v_pr[k - 4]))

                def pmm(t, tt=tt, og=og, pp=pp):
                    last = None
                    for c in range(NCH):
                        last = t.matmul(pp[:, :], BIG[:, c * T + tt * 128: c * T + (tt + 1) * 128],
                                        WBUF[:, c * 1024 + og * 512: c * 1024 + (og + 1) * 512],
                                        start=(c == 0), stop=(c == NCH - 1))
                    return last
                tp = p.emit("tensor", w, pmm, inc="ts")
                v_pr[k] = p.emit("vector", [("ts", tp)], lambda v, tt=tt, og=og, pp=pp, k=k: (
                                 v.drain(),
                                 v.scalar_tensor_tensor(
                    out=X[:, tt * D + og * 512: tt * D + (og + 1) * 512], in0=pp[:, :],
                    scalar=1.0, in1=X[:, tt * D + og * 512: tt * D + (og + 1) * 512],
                    op0=ALU.mult, op1=ALU.add,
                    accum_out=SM[:, SM_A + k:SM_A + k + 1]))[-1], inc="vs")
        v_x1 = p.cnt["vs"]
        pa5.close()
        stA.close()
        p.emit("vector", [("vs", v_x1)], lambda v: v.tensor_tensor(
            out=SM[:, SM_SUM:SM_SUM + 16], in0=SM[:, SM_A:SM_A + 32:2],
            in1=SM[:, SM_A + 1:SM_A + 32:2], op=ALU.add), inc="vs")
        p.flush("pa5")
        pchk("pa5")
        if "pa5" in debug:
            dbg_list += [("pa5_x", X)]

        def phase_ffnq(tag, accum_for_next):
            st = ExitStack()
            WQT = p.sb(tag + "WQT", [128, 8192], BF16, st)
            XQB = p.sb(tag + "XQB", [128, 2048], BF16, st)
            WSB = p.sb(tag + "WSB", [128, 1024], F32, st)
            BB = p.sb(tag + "BB", [128, 1024], F32, st)
            ROWS = p.sb(tag + "ROWS", [128, 16], F32, st)
            ps0 = p.ps(tag + "ps0", [128, 512], F32, st)
            ps1 = p.ps(tag + "ps1", [128, 512], F32, st)
            ps2 = p.ps(tag + "ps2", [128, 512], F32, st)
            ps3 = p.ps(tag + "ps3", [128, 512], F32, st)
            psq = p.ps(tag + "psq", [128, 1024], BF16, st)
            psq2 = p.ps(tag + "psq2", [128, 1024], BF16, st)
            stq = ExitStack()
            FP32S = p.sb(tag + "FP32S", [128, 2048], F32, stq)

            v_fw = {0: p.dma("sync", [], FP32S[:, 0:1024], fw[0:128, :], "dh0")}
            v_amax, t_wq = {}, {}
            for c in range(NCH):
                slot = c % 2
                if c + 1 < NCH:
                    wd = [("vs", v_amax[c - 1])] if c >= 1 else []
                    v_fw[c + 1] = p.dma("sync", wd, FP32S[:, (1 - slot) * 1024:(2 - slot) * 1024],
                                        fw[(c + 1) * 128:(c + 2) * 128, :], "dh0" if (c + 1) % 2 == 0 else "dh1")
                src = FP32S[:, slot * 1024:(slot + 1) * 1024]
                w = [("dh0" if c % 2 == 0 else "dh1", v_fw[c])]
                if c >= 2:
                    w.append(("ts", t_wq[c - 2]))
                v_amax[c] = p.emit("vector", w, lambda v, c=c, src=src: (
                    v.reduce_max(out=ROWS[:, c:c + 1], in_=src, axis=AX.X,
                                 apply_absolute_value=True),
                    v.drain(),
                    v.reciprocal(out=SM[:, SM_MISC + (c % 2):SM_MISC + (c % 2) + 1],
                                 in_=ROWS[:, c:c + 1]),
                    v.drain(),
                    v.tensor_scalar(out=ROWS[:, 8 + c:9 + c],
                                    in0=SM[:, SM_MISC + (c % 2):SM_MISC + (c % 2) + 1],
                                    scalar1=127.0, scalar2=None, op0=ALU.mult, op1=ALU.bypass),
                    v.drain(),
                    v.tensor_scalar(out=XTMPF[:, 0:1024], in0=src,
                                    scalar1=ROWS[:, 8 + c:9 + c], scalar2=MAGIC,
                                    op0=ALU.mult, op1=ALU.add),
                    v.drain(),
                    v.tensor_scalar(out=XQB[:, 0:1024], in0=XTMPF[:, 0:1024],
                                    scalar1=-MAGIC, scalar2=None, op0=ALU.add, op1=ALU.bypass)
                    )[-1], inc="vs")
                tq = p.emit("tensor", [("vs", v_amax[c]), ("dc", v_consts)],
                            lambda t, slot=slot: [
                    t.transpose((psq if slot == 0 else psq2)[:, j * 128:(j + 1) * 128],
                                XQB[:, j * 128:(j + 1) * 128], IDB[:, :])
                    for j in range(NCH)][-1], inc="ts")
                t_wq[c] = tq
                p.emit("vector", [("ts", tq)], lambda v, c=c, slot=slot: v.tensor_copy(
                    out=WQT[:, :].rearrange("p (j n) -> p j n", j=NCH)[:, :, c * 128:(c + 1) * 128],
                    in_=(psq if slot == 0 else psq2)[:, :].rearrange("p (j n) -> p j n", j=NCH)),
                    inc="vs")
            v_wqt = p.cnt["vs"]
            stq.close()
            WSR = p.sb(tag + "WSR", [1, 2048], F32, st)
            SOFT3 = p.sb(tag + "SOFT3", [128, 1024], F32, st)
            def wsrow(t):
                last = None
                for c in range(NCH):
                    pb = ps2 if c < 4 else ps3
                    last = t.transpose(pb[0:1, (c % 4) * 128:((c % 4) + 1) * 128],
                                       ROWS[:, c:c + 1], IDF[:, :])
                return last
            tws = p.emit("tensor", [("vs", v_amax[NCH - 1])], wsrow, inc="ts")
            v_wsr = p.emit("vector", [("ts", tws)], lambda v: (
                v.tensor_copy(out=WSR[0:1, 0:512], in_=ps2[0:1, 0:512]),
                v.tensor_copy(out=WSR[0:1, 512:1024], in_=ps3[0:1, 0:512]))[-1], inc="vs")
            v_fb = p.dma("sync", [("dh0", v_fw[6]), ("dh1", v_fw[7]), ("vs", v_amax[NCH - 1])], WSR[0:1, 1024:2048], fb, "dc")

            def bcasts(t):
                t.matmul(ps0[:, :], ONEF[0:1, 0:128], WSR[0:1, 0:512], start=True, stop=True)
                t.matmul(ps1[:, :], ONEF[0:1, 0:128], WSR[0:1, 512:1024], start=True, stop=True)
                t.matmul(ps2[:, :], ONEF[0:1, 0:128], WSR[0:1, 1024:1536], start=True, stop=True)
                return t.matmul(ps3[:, :], ONEF[0:1, 0:128], WSR[0:1, 1536:2048],
                                start=True, stop=True)
            tbc = p.emit("tensor", [("vs", v_wsr), ("dc", v_fb)], bcasts, inc="ts")
            v_bc = p.emit("vector", [("ts", tbc)], lambda v: (
                v.tensor_scalar(out=WSB[:, 0:512], in0=ps0[:, :], scalar1=1.0 / 127.0,
                                scalar2=None, op0=ALU.mult, op1=ALU.bypass),
                v.tensor_scalar(out=WSB[:, 512:1024], in0=ps1[:, :], scalar1=1.0 / 127.0,
                                scalar2=None, op0=ALU.mult, op1=ALU.bypass),
                v.tensor_copy(out=BB[:, 0:512], in_=ps2[:, :]),
                v.tensor_copy(out=BB[:, 512:1024], in_=ps3[:, :]))[-1], inc="vs")

            v_inv2 = ln_stats(have_sums=True)
            v_xq_ev, v_res = {}, {}
            va_hist = {}
            for tt in range(NTT):
                slot = tt % 2
                xq = XQB[:, slot * 1024:(slot + 1) * 1024]
                xf = XTMPF[:, slot * 1024:(slot + 1) * 1024]
                wl = [("vs", v_inv2)]
                if tt >= 2:
                    wl.append(("vs", va_hist[tt - 2]))
                s_ln = p.emit("scalar", wl, lambda s, tt=tt, xf=xf: s.activation(
                    out=xf, in_=X[:, tt * D:(tt + 1) * D], func=AF.Identity,
                    scale=SM[:, SM_INV + tt:SM_INV + tt + 1],
                    bias=SM[:, SM_NMI + tt:SM_NMI + tt + 1]), inc="ss")
                w = [("ss", s_ln), ("vs", v_bc), ("vs", v_wqt)]
                if tt >= 2:
                    w.append(("ts", v_xq_ev[tt - 2]))
                rot = tt % 2
                va = p.emit("vector", w, lambda v, tt=tt, xf=xf, xq=xq, rot=rot: (
                    v.reduce_max(out=SM[:, SM_AM + rot:SM_AM + rot + 1], in_=xf, axis=AX.X,
                                 apply_absolute_value=True),
                    v.drain(),
                    v.reciprocal(out=SM[:, SM_RA + rot:SM_RA + rot + 1],
                                 in_=SM[:, SM_AM + rot:SM_AM + rot + 1]),
                    v.drain(),
                    v.tensor_scalar(out=SM[:, SM_XS + tt:SM_XS + tt + 1],
                                    in0=SM[:, SM_AM + rot:SM_AM + rot + 1],
                                    scalar1=1.0 / 127.0, scalar2=None, op0=ALU.mult, op1=ALU.bypass),
                    v.tensor_scalar(out=SM[:, SM_R127 + rot:SM_R127 + rot + 1],
                                    in0=SM[:, SM_RA + rot:SM_RA + rot + 1],
                                    scalar1=127.0, scalar2=None, op0=ALU.mult, op1=ALU.bypass),
                    v.drain(),
                    v.tensor_scalar(out=xf, in0=xf, scalar1=SM[:, SM_R127 + rot:SM_R127 + rot + 1],
                                    scalar2=MAGIC, op0=ALU.mult, op1=ALU.add),
                    v.drain(),
                    v.tensor_scalar(out=xq, in0=xf, scalar1=-MAGIC, scalar2=None,
                                    op0=ALU.add, op1=ALU.bypass))[-1], inc="vs")
                va_hist[tt] = va
                tq = p.emit("tensor", [("vs", va)], lambda t, xq=xq, slot=slot: [
                    t.transpose((psq if slot == 0 else psq2)[:, j * 128:(j + 1) * 128],
                                xq[:, j * 128:(j + 1) * 128], IDB[:, :]) for j in range(NCH)][-1],
                    inc="ts")
                v_xq_ev[tt] = tq
                v_xq = p.emit("scalar", [("ts", tq)], lambda s, tt=tt, slot=slot: s.activation(
                    out=XNFM[:, :].rearrange("p (c t) -> p c t", c=NCH)[:, :, tt * 128:(tt + 1) * 128],
                    in_=(psq if slot == 0 else psq2)[:, :].rearrange("p (j n) -> p j n", j=NCH),
                    func=AF.Copy), inc="ss")
                for og in range(2):
                    m2 = tt * 2 + og
                    pp = [ps0, ps1, ps2, ps3][m2 % 4]
                    sbf = SOFT2 if (m2 % 4) < 2 else SOFT3
                    w2 = [("ss", v_xq)]
                    if m2 >= 4:
                        w2.append(("vs", v_res[m2 - 4]))

                    def qmm(t, tt=tt, pp=pp, og=og):
                        last = None
                        for j in range(NCH):
                            last = t.matmul(pp[:, :],
                                            XNFM[:, j * T + tt * 128: j * T + (tt + 1) * 128],
                                            WQT[:, j * 1024 + og * 512: j * 1024 + (og + 1) * 512],
                                            start=(j == 0), stop=(j == NCH - 1))
                        return last
                    tmm = p.emit("tensor", w2, qmm, inc="ts")
                    s1 = p.emit("scalar", [("ts", tmm)], lambda s, pp=pp, tt=tt, og=og, sbf=sbf:
                                s.activation(
                        out=sbf[:, og * 512:(og + 1) * 512], in_=pp[:, :], func=AF.Identity,
                        scale=SM[:, SM_XS + tt:SM_XS + tt + 1], bias=CZCOL[:, 0:1]), inc="ss")

                    def resf(v, tt=tt, og=og, sbf=sbf):
                        sl = slice(og * 512, (og + 1) * 512)
                        xsl = slice(tt * D + og * 512, tt * D + (og + 1) * 512)
                        v.tensor_tensor(out=sbf[:, sl], in0=sbf[:, sl], in1=WSB[:, sl],
                                        op=ALU.mult)
                        v.drain()
                        v.tensor_tensor(out=sbf[:, sl], in0=sbf[:, sl], in1=BB[:, sl],
                                        op=ALU.add)
                        v.drain()
                        if accum_for_next:
                            return v.scalar_tensor_tensor(
                                out=X[:, xsl], in0=sbf[:, sl], scalar=1.0, in1=X[:, xsl],
                                op0=ALU.mult, op1=ALU.add,
                                accum_out=SM[:, SM_A + tt * 2 + og:SM_A + tt * 2 + og + 1])
                        v.drain()
                        return v.tensor_tensor(out=X[:, xsl], in0=sbf[:, sl], in1=X[:, xsl],
                                               op=ALU.add)
                    v_res[tt * 2 + og] = p.emit("vector", [("ss", s1)], resf, inc="vs")
            if accum_for_next:
                p.emit("vector", [("vs", p.cnt["vs"])], lambda v: v.tensor_tensor(
                    out=SM[:, SM_SUM:SM_SUM + 16], in0=SM[:, SM_A:SM_A + 32:2],
                    in1=SM[:, SM_A + 1:SM_A + 32:2], op=ALU.add), inc="vs")
            st.close()
            p.flush(tag)

        if nphases >= 2:
            phase_ffnq("pb", accum_for_next=True)
            if "pb" in debug:
                dbg_list += [("pb_x", X)]

        def phase_attnb(pccut=None):
            stC = ExitStack()
            T2K = p.sb("T2K", [128, T], F32, stC)
            NEGC = p.sb("cNEGC", [128, 4], F32, stC)
            GB = p.sb("cGB", [128, 64], F32, stC)
            B1C = p.sb("cB1C", [128, 32], F32, stC)
            BVR = p.sb("cBVR", [128, 1024], BF16, stC)

            p.dma("sync", [], T2K[:, :], c_t2, "dc")
            v_t2 = p.cnt["dc"]

            v_sp = p.dma("sync", [("vs", p.cnt["vs"]), ("ss", p.cnt["ss"])],
                         spill.rearrange("(t p) d -> p t d", p=128),
                         X[:, :].rearrange("p (t d) -> p t d", t=NTT), "dx")
            cut = [False]
            def cuts(name, *tens):
                if pccut == name:
                    for i, tn in enumerate(tens):
                        dbg_list.append((name + "_%d" % i, tn if isinstance(tn, bass.AP) else tn[:, :]))
                    cut[0] = True
                return cut[0]
            v_inv3 = ln_stats(have_sums=True)
            v_z = {}
            for tt in range(NTT):
                w = [("vs", v_inv3)]
                if tt == 0:
                    w.append(("dx", v_sp))
                v_z[tt] = p.emit("vector", w, lambda v, tt=tt: v.tensor_scalar(
                    out=X[:, tt * D:(tt + 1) * D], in0=X[:, tt * D:(tt + 1) * D],
                    scalar1=SM[:, SM_INV + tt:SM_INV + tt + 1],
                    scalar2=SM[:, SM_NMI + tt:SM_NMI + tt + 1],
                    op0=ALU.mult, op1=ALU.add), inc="vs")
            p.flush("pc0")
            if cuts("pc0", X):
                return
            for l in range(2):
                wpar = [("vs", p.cnt["vs"]), ("ss", p.cnt["ss"])] if l > 0 else []
                for i, (src, col) in enumerate([(cln1g, 0), (cln1b, 8), (cln2g, 16), (cln2b, 24),
                                                (cbq, 32), (cbk, 40)]):
                    p.dma("sync", wpar if i == 0 else [], GB[:, col:col + 8], src[l], "dc")
                p.dma("sync", [], B1C[:, :], cb1[l], "dc")
                v_par = p.cnt["dc"]
                v_bvr0 = p.emit("vector", [("vs", p.cnt["vs"])], lambda v: v.memset(
                    BVR[:, :], 0.0), inc="vs")
                v_bvr = p.dma("gpsimd", [("ss", p.cnt["ss"]), ("vs", v_bvr0)],
                              BVR[0:1, :], bbv[l:l + 1, :], "dw")
                v_bqs = p.emit("vector", [("dc", v_par)], lambda v: v.tensor_scalar(
                    out=GB[:, 48:56], in0=GB[:, 32:40], scalar1=0.125, scalar2=None,
                    op0=ALU.mult, op1=ALU.bypass), inc="vs")

                stP = ExitStack()
                psF1 = p.ps("psF1", [128, 1024], F32, stP)
                psF2 = p.ps("psF2", [128, 1024], F32, stP)
                v_inva = ln_stats(have_sums=False)
                t_f = {}
                v_fe = {}
                for tt in range(NTT):
                    slot = tt % 2
                    xf = XTMPF[:, slot * 1024:(slot + 1) * 1024]
                    w = [("vs", v_inva)]
                    if tt >= 2:
                        w.append(("ts", t_f[tt - 2]))
                    va = p.emit("vector", w, lambda v, tt=tt, xf=xf: v.tensor_scalar(
                        out=xf, in0=X[:, tt * D:(tt + 1) * D],
                        scalar1=SM[:, SM_INV + tt:SM_INV + tt + 1],
                        scalar2=SM[:, SM_NMI + tt:SM_NMI + tt + 1],
                        op0=ALU.mult, op1=ALU.add), inc="vs")

                    def tf(t, xf=xf, slot=slot):
                        psf = psF1 if slot == 0 else psF2
                        last = None
                        for c in range(NCH):
                            last = t.transpose(psf[:, (c % 4) * 256 + (c // 4) * 128:
                                                   (c % 4) * 256 + (c // 4) * 128 + 128]
                                               if False else psf[:, c * 128:(c + 1) * 128],
                                               xf[:, c * 128:(c + 1) * 128], IDF[:, :])
                        return last
                    w2 = [("vs", va)]
                    if tt >= 2:
                        w2.append(("vs", v_fe[tt - 2]))
                    t_f[tt] = p.emit("tensor", w2, tf, inc="ts")

                    def fe(v, tt=tt, slot=slot):
                        psf = psF1 if slot == 0 else psF2
                        last = None
                        for c in range(NCH):
                            last = v.tensor_scalar(
                                out=XNFM[:, c * T + tt * 128: c * T + (tt + 1) * 128],
                                in0=psf[:, c * 128:(c + 1) * 128],
                                scalar1=GB[:, 0 + c:1 + c], scalar2=GB[:, 8 + c:9 + c],
                                op0=ALU.mult, op1=ALU.add)
                        return last
                    v_fe[tt] = p.emit("vector", [("ts", t_f[tt])], fe, inc="vs")
                stP.close()
                p.flush("pc1_l%d" % l)
                if cuts("pc1", XNFM):
                    return

                stL = ExitStack()
                QBUF = p.sb("cQBUF", [128, T], BF16, stL)
                KBUF = p.sb("cKBUF", [128, T], BF16, stL)
                VBUF = p.sb("cVBUF", [128, 2 * 1040], BF16, stL)
                UT = p.sb("cUT", [128, 4096], BF16, stL)
                SOFT = p.sb("cSOFT", [128, 2048], F32, stL)
                stP = ExitStack()
                ps0 = p.ps("cps0", [128, 512], F32, stP)
                ps1 = p.ps("cps1", [128, 512], F32, stP)
                ps2 = p.ps("cps2", [128, 512], F32, stP)
                ps3 = p.ps("cps3", [128, 512], F32, stP)
                psPV = p.ps("cpsPV", [65, 2048], F32, stP)

                def dma_qkv_b(hg, slot, waits):
                    res = []
                    for vi, wsrc in enumerate([bwq, bwk, bwv]):
                        sem = ("dwa", "dwb", "dwc")[vi]
                        res.append((sem, p.dma("gpsimd", waits if vi == 0 else [],
                                    WBUF[:, slot * 4096 + vi * 1024: slot * 4096 + (vi + 1) * 1024]
                                    .rearrange("p (c n) -> p c n", c=NCH),
                                    wsrc[l].rearrange("(c p) n -> p c n", p=128)
                                    [:, :, hg * 128:(hg + 1) * 128], sem)))
                    return res
                v_w = {0: dma_qkv_b(0, 0, [("dw", v_bvr)])}
                fence = []
                v_norm = {}
                prev_soft = None
                for hg in range(NCH):
                    slot = hg % 2
                    WQ = WBUF[:, slot * 4096: slot * 4096 + 1024]
                    WK = WBUF[:, slot * 4096 + 1024: slot * 4096 + 2048]
                    WV = WBUF[:, slot * 4096 + 2048: slot * 4096 + 3072]
                    qk_ev = {}
                    for tg in range(4):
                        pq = ps0 if tg % 2 == 0 else ps1
                        pk = ps2 if tg % 2 == 0 else ps3
                        w = []
                        if tg == 0:
                            w += list(v_w[hg]) + [("vs", v_bqs)]
                            if fence:
                                w += [("vs", fence[-1][0]), ("ss", fence[-1][1])]
                        if tg >= 2:
                            w.append(("vs", qk_ev[tg - 2]))

                        def mmqk(t, tg=tg, pq=pq, pk=pk, WQ=WQ, WK=WK):
                            for c in range(NCH):
                                t.matmul(pq[:, :], WQ[:, c * 128:(c + 1) * 128],
                                         XNFM[:, c * T + tg * 512: c * T + (tg + 1) * 512],
                                         start=(c == 0), stop=(c == NCH - 1))
                            last = None
                            for c in range(NCH):
                                last = t.matmul(pk[:, :], WK[:, c * 128:(c + 1) * 128],
                                                XNFM[:, c * T + tg * 512: c * T + (tg + 1) * 512],
                                                start=(c == 0), stop=(c == NCH - 1))
                            return last
                        tq = p.emit("tensor", w, mmqk, inc="ts")
                        sq = p.emit("scalar", [("ts", tq)], lambda s, tg=tg, pq=pq, pk=pk, hg=hg: (
                            s.activation(out=XTMPF[:, (tg % 2) * 512:(tg % 2) * 512 + 512],
                                         in_=pq[:, :], func=AF.Identity, scale=0.125,
                                         bias=GB[:, 48 + hg:49 + hg]),
                            s.activation(out=XTMPF[:, 1024 + (tg % 2) * 512:1024 + (tg % 2) * 512 + 512],
                                         in_=pk[:, :], func=AF.Identity, scale=1.0,
                                         bias=GB[:, 40 + hg:41 + hg]))[-1], inc="ss")
                        qk_ev[tg] = p.emit("vector", [("ss", sq)], lambda v, tg=tg: (
                            v.tensor_tensor(out=QBUF[:, tg * 512:(tg + 1) * 512],
                                            in0=XTMPF[:, (tg % 2) * 512:(tg % 2) * 512 + 512],
                                            in1=T2K[:, tg * 512:(tg + 1) * 512], op=ALU.mult),
                            v.tensor_tensor(out=KBUF[:, tg * 512:(tg + 1) * 512],
                                            in0=XTMPF[:, 1024 + (tg % 2) * 512:1024 + (tg % 2) * 512 + 512],
                                            in1=T2K[:, tg * 512:(tg + 1) * 512], op=ALU.mult))[-1],
                            inc="vs")
                    v_vev = {}
                    for g in range(4):
                        pv = ps0 if g % 2 == 0 else ps1
                        w = [("ss", qk_ev[3] if False else qk_ev[3])] if g == 0 else []
                        if g == 0:
                            w = [("vs", qk_ev[3])]
                        if g >= 2:
                            w.append(("vs", v_vev[g - 2]))

                        def mmv(t, g=g, pv=pv, WV=WV, hg=hg):
                            last = None
                            for i in range(4):
                                tt = g * 4 + i
                                for c in range(NCH):
                                    t.matmul(pv[:, i * 128:(i + 1) * 128],
                                             XNFM[:, c * T + tt * 128: c * T + (tt + 1) * 128],
                                             WV[:, c * 128:(c + 1) * 128],
                                             start=(c == 0), stop=False)
                                last = t.matmul(pv[:, i * 128:(i + 1) * 128], ONEB[0:1, 0:128],
                                                BVR[0:1, hg * 128:(hg + 1) * 128],
                                                start=False, stop=True)
                            return last
                        tv = p.emit("tensor", w, mmv, inc="ts")

                        def vev(v, g=g, pv=pv):
                            v.tensor_copy(
                                out=VBUF[:, 0:1040].rearrange("p (t y) -> p t y", t=NTT)
                                [:, g * 4:(g + 1) * 4, 0:64],
                                in_=pv[:, :].rearrange("p (t n) -> p t n", t=4)[:, :, 0:64])
                            return v.tensor_copy(
                                out=VBUF[:, 1040:2080].rearrange("p (t y) -> p t y", t=NTT)
                                [:, g * 4:(g + 1) * 4, 0:64],
                                in_=pv[:, :].rearrange("p (t n) -> p t n", t=4)[:, :, 64:128])
                        v_vev[g] = p.emit("vector", [("ts", tv)], vev, inc="vs")
                    if hg == 0:
                        p.emit("vector", [], lambda v: (
                            v.memset(VBUF[:, 64:1040:65], 1.0),
                            v.memset(VBUF[:, 1104:2080:65], 1.0))[-1], inc="vs")
                    if hg + 1 < NCH:
                        v_w[hg + 1] = dma_qkv_b(hg + 1, (hg + 1) % 2,
                                                [("vs", qk_ev[3]), ("vs", v_vev[3])])

                    for hh in range(2):
                        h = hg * 2 + hh
                        r0 = hh * 64
                        w = [("vs", qk_ev[3]), ("vs", v_vev[3]), ("vs", v_vev[2])] if hh == 0 else \
                            [("ss", prev_soft)]
                        tc = p.emit("tensor", w, lambda t, r0=r0: t.matmul(
                            ps2[:, 0:128], KBUF[r0:r0 + 64, 0:T:16], QBUF[r0:r0 + 64, 0:T:16],
                            start=True, stop=True), inc="ts")
                        vc = p.emit("vector", [("ts", tc)], lambda v: v.reduce_max(
                            out=SM[:, SM_MISC:SM_MISC + 1], in_=ps2[:, 0:128], axis=AX.X), inc="vs")
                        tc2 = p.emit("tensor", [("vs", vc)], lambda t: t.transpose(
                            ps3[0:1, 0:128], SM[:, SM_MISC:SM_MISC + 1], IDF[:, :]), inc="ts")
                        vc2 = p.emit("vector", [("ts", tc2)], lambda v: v.reduce_max(
                            out=SM[0:1, SM_MISC + 1:SM_MISC + 2], in_=ps3[0:1, 0:128], axis=AX.X),
                            inc="vs")
                        tc3 = p.emit("tensor", [("vs", vc2)], lambda t: t.matmul(
                            ps3[:, 128:129], ONEF[0:1, 0:128], SM[0:1, SM_MISC + 1:SM_MISC + 2],
                            start=True, stop=True), inc="ts")
                        s_c = p.emit("scalar", [("ts", tc3)], lambda s, hh=hh: s.activation(
                            out=NEGC[:, hh:hh + 1], in_=ps3[:, 128:129], func=AF.Copy,
                            scale=-1.0, bias=-CSLACK), inc="ss")

                        e_ev = {}
                        for kb in range(NTT):
                            uts = UT[:, (kb % 2) * 2048:(kb % 2 + 1) * 2048]
                            for qg in range(4):
                                m = kb * 4 + qg
                                pse2 = [ps2, ps3, ps0, ps1][m % 4]
                                w2 = [("ss", s_c)] if m == 0 else []
                                if m < 4 and h > 0:
                                    w2.append(("vs", v_norm[h * NTT - 1]))
                                if m >= 4:
                                    w2.append(("ss", e_ev[m - 4]))
                                tsm = p.emit("tensor", w2, lambda t, kb=kb, qg=qg, pse2=pse2, r0=r0:
                                             t.matmul(pse2[:, :],
                                                      KBUF[r0:r0 + 64, kb * 128:(kb + 1) * 128],
                                                      QBUF[r0:r0 + 64, qg * 512:(qg + 1) * 512],
                                                      start=True, stop=True), inc="ts")
                                e_ev[m] = p.emit("scalar", [("ts", tsm)],
                                                 lambda s, pse2=pse2, uts=uts, qg=qg, hh=hh:
                                                 s.activation(
                                    out=uts[:, qg * 512:(qg + 1) * 512], in_=pse2[:, :], func=AF.Exp,
                                    bias=NEGC[:, hh:hh + 1]), inc="ss")

                            def pvmm(t, kb=kb, uts=uts, hh=hh):
                                last = None
                                for qg in range(4):
                                    last = t.matmul(psPV[0:65, qg * 512:(qg + 1) * 512],
                                                    VBUF[:, hh * 1040 + kb * 65:
                                                         hh * 1040 + (kb + 1) * 65],
                                                    uts[:, qg * 512:(qg + 1) * 512],
                                                    start=(kb == 0), stop=(kb == NTT - 1))
                                return last
                            p.emit("tensor", [("ss", e_ev[kb * 4 + 3])], pvmm, inc="ts")
                        tpv = p.cnt["ts"]
                        s_soft = p.emit("scalar", [("ts", tpv)], lambda s: s.activation(
                            out=SOFT[0:65, :], in_=psPV[0:65, :], func=AF.Copy), inc="ss")
                        prev_soft = s_soft
                        for qt in range(NTT):
                            psn = ps0 if qt % 2 == 0 else ps1
                            w3 = [("ss", s_soft)]
                            if qt == 0 and hh == 0:
                                w3 += [("vs", v_vev[3]), ("vs", v_vev[2])]
                            idx0 = h * NTT + qt
                            if idx0 - 2 >= 0 and (idx0 - 2) in v_norm:
                                w3.append(("vs", v_norm[idx0 - 2]))
                            tn = p.emit("tensor", w3, lambda t, qt=qt, psn=psn: t.transpose(
                                psn[:, 0:65], SOFT[0:65, qt * 128:(qt + 1) * 128],
                                IDF[0:65, 0:65]), inc="ts")

                            def nrm(v, qt=qt, psn=psn, h=h):
                                mc = SM_MISC + 2 + (qt % 4)
                                v.reciprocal(out=SM[:, mc:mc + 1], in_=psn[:, 64:65])
                                v.drain()
                                return v.tensor_scalar(
                                    out=BIG[:, qt * 1024 + h * 64: qt * 1024 + (h + 1) * 64],
                                    in0=psn[:, 0:64],
                                    scalar1=SM[:, mc:mc + 1], scalar2=None,
                                    op0=ALU.mult, op1=ALU.bypass)
                            v_norm[h * NTT + qt] = p.emit("vector", [("ts", tn)], nrm, inc="vs")
                    fence.append((p.cnt["vs"], p.cnt["ss"]))
                v_att = p.cnt["vs"]
                v_wo = p.dma("gpsimd", [("vs", fence[-1][0])],
                             WBUF[:, :].rearrange("p (c n) -> p c n", c=NCH),
                             bwo[l].rearrange("(c p) n -> p c n", p=128), "dw")
                stP.close()
                stL.close()
                p.flush("pc23_l%d" % l)
                if cuts("pc3", BIG):
                    return

                stP = ExitStack()
                psb0c = p.ps("cpsb0", [128, 1024], BF16, stP)
                psb1c = p.ps("cpsb1", [128, 1024], BF16, stP)
                psW = [p.ps("cpsW%d" % i, [128, 512], F32, stP) for i in range(4)]
                t_tr3, v_ev3, v_wr = {}, {}, {}
                for tt in range(NTT):
                    psb = psb0c if tt % 2 == 0 else psb1c

                    def trf(t, tt=tt, psb=psb):
                        last = None
                        for c in range(NCH):
                            last = t.transpose(psb[:, c * 128:(c + 1) * 128],
                                               BIG[:, tt * 1024 + c * 128: tt * 1024 + (c + 1) * 128],
                                               IDB[:, :])
                        return last
                    w = [("vs", v_att)] if tt == 0 else []
                    if tt >= 2:
                        w.append(("vs", v_ev3[tt - 2]))
                    t_tr3[tt] = p.emit("tensor", w, trf, inc="ts")
                    v_ev3[tt] = p.emit("vector", [("ts", t_tr3[tt])], lambda v, tt=tt, psb=psb:
                                       v.tensor_copy(
                        out=XNFM[:, :].rearrange("p (c t) -> p c t", c=NCH)[:, :, tt * 128:(tt + 1) * 128],
                        in_=psb[:, :].rearrange("p (c n) -> p c n", c=NCH)), inc="vs")
                for tt in range(NTT):
                    for og in range(2):
                        k = tt * 2 + og
                        pp = psW[k % 4]
                        w = [("vs", v_ev3[tt])]
                        if k == 0:
                            w.append(("dw", v_wo))
                        if k >= 4:
                            w.append(("vs", v_wr[k - 4]))

                        def wmm(t, tt=tt, og=og, pp=pp):
                            last = None
                            for c in range(NCH):
                                last = t.matmul(pp[:, :],
                                                XNFM[:, c * T + tt * 128: c * T + (tt + 1) * 128],
                                                WBUF[:, c * 1024 + og * 512: c * 1024 + (og + 1) * 512],
                                                start=(c == 0), stop=(c == NCH - 1))
                            return last
                        tw = p.emit("tensor", w, wmm, inc="ts")
                        v_wr[k] = p.emit("vector", [("ts", tw)], lambda v, tt=tt, og=og, pp=pp, k=k: (
                            v.drain(),
                            v.scalar_tensor_tensor(
                                out=X[:, tt * D + og * 512: tt * D + (og + 1) * 512], in0=pp[:, :],
                                scalar=1.0, in1=X[:, tt * D + og * 512: tt * D + (og + 1) * 512],
                                op0=ALU.mult, op1=ALU.add,
                                accum_out=SM[:, SM_A + k:SM_A + k + 1]))[-1], inc="vs")
                p.emit("vector", [("vs", p.cnt["vs"])], lambda v: v.tensor_tensor(
                    out=SM[:, SM_SUM:SM_SUM + 16], in0=SM[:, SM_A:SM_A + 32:2],
                    in1=SM[:, SM_A + 1:SM_A + 32:2], op=ALU.add), inc="vs")
                stP.close()
                p.flush("pc4_l%d" % l)
                if cuts("pc4", X):
                    return

                v_invb = ln_stats(have_sums=True)
                stP = ExitStack()
                psF1 = p.ps("dpsF1", [128, 1024], F32, stP)
                psF2 = p.ps("dpsF2", [128, 1024], F32, stP)
                psH0 = p.ps("dpsH0", [128, 512], F32, stP)
                psH1 = p.ps("dpsH1", [128, 512], F32, stP)
                HID2 = p.sb("cHID2", [128, 16384], BF16, stP)

                def hid_t(hc):
                    buf = BIG if hc < 16 else HID2
                    return buf[:, (hc % 16) * 1024:(hc % 16 + 1) * 1024]

                for half in range(2):
                    t_f2, v_fe2 = {}, {}
                    for i in range(8):
                        tt = half * 8 + i
                        slot = i % 2
                        xf = XTMPF[:, slot * 1024:(slot + 1) * 1024]
                        w = [("vs", v_invb)]
                        if i >= 2:
                            w.append(("ts", t_f2[i - 2]))
                        va = p.emit("vector", w, lambda v, tt=tt, xf=xf: v.tensor_scalar(
                            out=xf, in0=X[:, tt * D:(tt + 1) * D],
                            scalar1=SM[:, SM_INV + tt:SM_INV + tt + 1],
                            scalar2=SM[:, SM_NMI + tt:SM_NMI + tt + 1],
                            op0=ALU.mult, op1=ALU.add), inc="vs")

                        def tf2(t, xf=xf, slot=slot):
                            psf = psF1 if slot == 0 else psF2
                            last = None
                            for c in range(NCH):
                                last = t.transpose(psf[:, c * 128:(c + 1) * 128],
                                                   xf[:, c * 128:(c + 1) * 128], IDF[:, :])
                            return last
                        w2 = [("vs", va)]
                        if i >= 2:
                            w2.append(("vs", v_fe2[i - 2]))
                        t_f2[i] = p.emit("tensor", w2, tf2, inc="ts")

                        def fe2(v, i=i, slot=slot, half=half):
                            psf = psF1 if slot == 0 else psF2
                            last = None
                            for c in range(NCH):
                                last = v.tensor_scalar(
                                    out=XNFM[:, c * T + half * 1024 + i * 128:
                                             c * T + half * 1024 + (i + 1) * 128],
                                    in0=psf[:, c * 128:(c + 1) * 128],
                                    scalar1=GB[:, 16 + c:17 + c], scalar2=GB[:, 24 + c:25 + c],
                                    op0=ALU.mult, op1=ALU.add)
                            return last
                        v_fe2[i] = p.emit("vector", [("ts", t_f2[i])], fe2, inc="vs")
                    v_xl2 = p.cnt["vs"]

                    v_w1 = {}
                    v_w1[0] = ("dwa", p.dma("gpsimd", [("vs", v_xl2)],
                                    WBUF[:, 0:4096].rearrange("p (c n) -> p c n", c=NCH),
                                    bw1[l].rearrange("(c p) n -> p c n", p=128)[:, :, 0:512], "dwa"))
                    hid_ev = {}
                    for sl8 in range(8):
                        slot = sl8 % 2
                        if sl8 + 1 < 8:
                            wd = [("ss", hid_ev[sl8 * 8 - 1])] if sl8 >= 1 else []
                            sem = "dwa" if (sl8 + 1) % 2 == 0 else "dwb"
                            v_w1[sl8 + 1] = (sem, p.dma("gpsimd", wd,
                                                  WBUF[:, (1 - slot) * 4096:(2 - slot) * 4096]
                                                  .rearrange("p (c n) -> p c n", c=NCH),
                                                  bw1[l].rearrange("(c p) n -> p c n", p=128)
                                                  [:, :, (sl8 + 1) * 512:(sl8 + 2) * 512], sem))
                        for j in range(4):
                            hc = sl8 * 4 + j
                            for tg in range(2):
                                m = hc * 2 + tg
                                ph_ = psH0 if m % 2 == 0 else psH1
                                w = []
                                if j == 0 and tg == 0:
                                    w.append(v_w1[sl8])
                                if m >= 2:
                                    w.append(("ss", hid_ev[m - 2]))

                                def hmm(t, hc=hc, tg=tg, ph_=ph_, slot=slot, half=half):
                                    last = None
                                    for c in range(NCH):
                                        last = t.matmul(
                                            ph_[:, :],
                                            WBUF[:, slot * 4096 + c * 512 + (hc % 4) * 128:
                                                 slot * 4096 + c * 512 + (hc % 4) * 128 + 128],
                                            XNFM[:, c * T + half * 1024 + tg * 512:
                                                 c * T + half * 1024 + (tg + 1) * 512],
                                            start=(c == 0), stop=(c == NCH - 1))
                                    return last
                                th = p.emit("tensor", w, hmm, inc="ts")
                                hid_ev[hc * 2 + tg] = p.emit(
                                    "scalar", [("ts", th)],
                                    lambda s, hc=hc, tg=tg, ph_=ph_: s.activation(
                                        out=hid_t(hc)[:, tg * 512:(tg + 1) * 512], in_=ph_[:, :],
                                        func=AF.Gelu_apprx_sigmoid,
                                        bias=B1C[:, hc:hc + 1]), inc="ss")
                    v_hid_done = p.cnt["ss"]
                    if half == 0 and cuts("pc5h", BIG, HID2, XNFM):
                        stP.close()
                        p.flush("pc5h")
                        return

                    v_w2 = {}
                    v_w2[0] = ("dwa", p.dma("gpsimd", [("ss", v_hid_done)],
                                    WBUF[:, 0:4096].rearrange("p (c n) -> p c n", c=32),
                                    bw2[l].rearrange("(c p) n -> p c n", p=128)[:, :, 0:128], "dwa"))
                    v_w2r = {}
                    for og in range(8):
                        slot = og % 2
                        if og + 1 < 8:
                            wd = [("ts", v_w2r[(og - 1) * 2 + 1][0])] if og >= 1 else []
                            sem = "dwa" if (og + 1) % 2 == 0 else "dwb"
                            v_w2[og + 1] = (sem, p.dma("gpsimd", wd,
                                                 WBUF[:, (1 - slot) * 4096:(2 - slot) * 4096]
                                                 .rearrange("p (c n) -> p c n", c=32),
                                                 bw2[l].rearrange("(c p) n -> p c n", p=128)
                                                 [:, :, (og + 1) * 128:(og + 2) * 128], sem))
                        for tg in range(2):
                            ph_ = psH0 if tg % 2 == 0 else psH1
                            w = [("ss", v_hid_done)]
                            if tg == 0:
                                w.append(v_w2[og])
                            if (og * 2 + tg) >= 2:
                                w.append(("vs", v_w2r[(og * 2 + tg) - 2][1]))

                            def wmm2(t, og=og, tg=tg, ph_=ph_, slot=slot):
                                last = None
                                for hc in range(32):
                                    last = t.matmul(
                                        ph_[:, :],
                                        WBUF[:, slot * 4096 + hc * 128: slot * 4096 + (hc + 1) * 128],
                                        hid_t(hc)[:, tg * 512:(tg + 1) * 512],
                                        start=(hc == 0), stop=(hc == 31))
                                return last
                            tw2 = p.emit("tensor", w, wmm2, inc="ts")
                            sv = p.emit("vector", [("ts", tw2)], lambda v, ph_=ph_: v.tensor_copy(
                                out=SOFT2[:, 0:512] if ph_ is psH0 else SOFT2[:, 512:1024],
                                in_=ph_[:, :]), inc="vs")

                            def tr2(t, og=og, tg=tg, ph_=ph_):
                                src = SOFT2[:, 0:512] if ph_ is psH0 else SOFT2[:, 512:1024]
                                psf = psF1 if tg % 2 == 0 else psF2
                                last = None
                                for i in range(4):
                                    last = t.transpose(psf[:, i * 128:(i + 1) * 128],
                                                       src[:, i * 128:(i + 1) * 128], IDF[:, :])
                                return last
                            w3 = [("vs", sv)]
                            if (og * 2 + tg) >= 2:
                                w3.append(("vs", v_w2r[(og * 2 + tg) - 2][1]))
                            tt2 = p.emit("tensor", w3, tr2, inc="ts")

                            def radd(v, og=og, tg=tg, half=half):
                                psf = psF1 if tg % 2 == 0 else psF2
                                last = None
                                for i in range(4):
                                    tt = half * 8 + tg * 4 + i
                                    last = v.tensor_tensor(
                                        out=X[:, tt * D + og * 128: tt * D + (og + 1) * 128],
                                        in0=X[:, tt * D + og * 128: tt * D + (og + 1) * 128],
                                        in1=psf[:, i * 128:(i + 1) * 128], op=ALU.add)
                                return last
                            vr = p.emit("vector", [("ts", tt2)], radd, inc="vs")
                            v_w2r[og * 2 + tg] = (tw2, vr)
                            if half == 0 and og == 0 and tg == 0 and cuts("pc5w", SOFT2[:, 0:512]):
                                stP.close()
                                p.flush("pc5w")
                                return
                    if half == 0 and cuts("pc5a", X):
                        stP.close()
                        p.flush("pc5a")
                        return
                stP.close()
                p.flush("pc5_l%d" % l)
                if cuts("pc5", X):
                    return
            v_rd = {}
            v_x3 = {}
            for tt in range(NTT):
                slot = tt % 2
                sem = "dh0" if slot == 0 else "dh1"
                wd = [("vs", v_x3[tt - 2])] if tt >= 2 else []
                v_rd[tt] = p.dma("sync", wd, XTMPF[:, slot * 1024:(slot + 1) * 1024],
                                 spill.rearrange("(t p) d -> p t d", p=128)[:, tt], sem)
                v_x3[tt] = p.emit("vector", [(sem, v_rd[tt])], lambda v, tt=tt, slot=slot: (
                    v.drain(),
                    v.scalar_tensor_tensor(
                        out=X[:, tt * D:(tt + 1) * D], in0=XTMPF[:, slot * 1024:(slot + 1) * 1024],
                        scalar=1.0, in1=X[:, tt * D:(tt + 1) * D], op0=ALU.mult, op1=ALU.add,
                        accum_out=SM[:, SM_A + tt:SM_A + tt + 1]))[-1], inc="vs")
            p.emit("vector", [("vs", p.cnt["vs"])], lambda v: v.tensor_copy(
                out=SM[:, SM_SUM:SM_SUM + 16], in_=SM[:, SM_A:SM_A + 16]), inc="vs")
            stC.close()
            p.flush("pc_end")

        if nphases >= 3:
            phase_attnb(pccut)
            if "pc" in debug:
                dbg_list += [("pc_x", X)]

        if nphases >= 4:
            phase_ffnq("pd", accum_for_next=False)


    except _Cut:
        pass
    wfin = [(k, v) for k, v in p.cnt.items() if v > 0 and k != "dq"]
    p.emit("sync", wfin, lambda s: s.dma_start(
        out=y_out.rearrange("(t p) d -> p t d", p=128),
        in_=X[:, :].rearrange("p (t d) -> p t d", t=NTT)), inc="dq", incby=16)
    wdbg = [("vs", p.cnt["vs"]), ("ss", p.cnt["ss"]), ("ts", p.cnt["ts"])]
    for name, ten in dbg_list:
        ap = ten if isinstance(ten, bass.AP) else ten[:, :]
        o = nc.dram_tensor("dbg_" + name, list(ap.shape), ap.dtype, kind="ExternalOutput").ap()
        p.emit("sync", wdbg, lambda s, o=o, ap=ap: s.dma_start(out=o, in_=ap),
               inc="dq", incby=16)
        wdbg = []
    vq = p.cnt["dq"]
    if _os2.environ.get("NONOP"):
        p.emit("sync", [("dq", vq)], lambda s: None)
    else:
        p.emit("sync", [("dq", vq)], lambda s: s.nop())
    p.flush("fin")
    return p


def make_inputs(inputs):
    consts = host_consts()
    base = {
        "wqkv": inputs["L0_attn_a_w_qkv"], "wproj": inputs["L0_attn_a_w_proj"],
        "fw": inputs["L0_ffn_weight"], "fb": np.asarray(inputs["L0_ffn_bias"]).reshape(1, -1),
        "bln1g": inputs["L0_attn_b_ln1_g"], "bln1b": inputs["L0_attn_b_ln1_b"],
        "bln2g": inputs["L0_attn_b_ln2_g"], "bln2b": inputs["L0_attn_b_ln2_b"],
        "bwq": inputs["L0_attn_b_wq"], "bwk": inputs["L0_attn_b_wk"],
        "bwv": inputs["L0_attn_b_wv"], "bwo": inputs["L0_attn_b_wo"],
        "bbq": inputs["L0_attn_b_bq"], "bbk": inputs["L0_attn_b_bk"],
        "bbv": inputs["L0_attn_b_bv"],
        "bw1": inputs["L0_attn_b_w1"], "bb1": inputs["L0_attn_b_b1"],
        "bw2": inputs["L0_attn_b_w2"],
    }
    def cols(a, n):
        a = np.asarray(a, dtype=np.float32)
        return np.ascontiguousarray(a.reshape(2, n, 128).transpose(0, 2, 1))
    base["cln1g"] = cols(inputs["L0_attn_b_ln1_g"], 8)
    base["cln1b"] = cols(inputs["L0_attn_b_ln1_b"], 8)
    base["cln2g"] = cols(inputs["L0_attn_b_ln2_g"], 8)
    base["cln2b"] = cols(inputs["L0_attn_b_ln2_b"], 8)
    base["cbq"] = cols(inputs["L0_attn_b_bq"], 8)
    base["cbk"] = cols(inputs["L0_attn_b_bk"], 8)
    base["cb1"] = cols(inputs["L0_attn_b_b1"], 32)
    base = {k: np.ascontiguousarray(np.asarray(v, dtype=np.float32)) for k, v in base.items()}
    base.update(consts)
    x = np.asarray(inputs["L0_attn_a_x"], dtype=np.float32)
    return [dict(base, x=np.ascontiguousarray(x[b])) for b in range(x.shape[0])]


def kernel(**inputs):
    prog = build()
    in_maps = make_inputs(inputs)
    res = run_bass_kernel_spmd(prog.nc, in_maps, core_ids=list(range(8)))
    return np.stack([r["y"] for r in res.results], axis=0)
